# version 03
# baseline (speedup 1.0000x reference)
import numpy as np
import ml_dtypes
from contextlib import ExitStack

import concourse.bass as bass
import concourse.mybir as mybir
from concourse.bass_utils import run_bass_kernel_spmd

F32 = mybir.dt.float32
BF16 = mybir.dt.bfloat16
AF = mybir.ActivationFunctionType
ALU = mybir.AluOpType
AX = mybir.AxisListType
BFNP = ml_dtypes.bfloat16

H = 2048
KH = H // 128
I_EXP = 1408
KI = I_EXP // 128
N_CORES = 8


class Plan:
    ENGINES = ("tensor", "vector", "scalar", "gpsimd", "sync")
    SELF = {"vector": "dv", "scalar": "ac"}

    def __init__(self):
        self.ops = {e: [] for e in self.ENGINES}
        self.cnt = {}

    def sem(self, name):
        self.cnt.setdefault(name, 0)

    def emit(self, eng, fn, waits=(), inc=None, amt=1):
        if eng in self.SELF and fn is not None and (inc is None or inc == self.SELF[eng]):
            s = self.SELF[eng]
            waits = tuple(waits) + ((s, self.cnt[s]),)
            if inc is None:
                inc, amt = s, 1
        for s, v in waits:
            assert s in self.cnt, s
            assert v <= self.cnt[s], ("wait ahead of plan", s, v, self.cnt[s])
        self.ops[eng].append((tuple(waits), fn, inc, amt))
        if inc is not None:
            self.cnt[inc] += amt
            return self.cnt[inc]
        return None


def build_program(GT=1024, TOKC=2048):
    NG = TOKC // GT
    TT = GT // 128
    SGT = GT // 2
    STT = TT // 2
    NQ = GT // 4
    HGT = GT // 2

    nc = bass.Bass("TRN2", target_bir_lowering=False)
    ap = {}
    def din(name, shape, dt=F32):
        ap[name] = nc.dram_tensor(name, shape, dt, kind="ExternalInput").ap()
    din("x", [TOKC, H])
    din("gw", [88, 128, KH * 128])
    din("uw", [88, 128, KH * 128])
    din("dw", [32, 128, KI * 512])
    din("sg", [22, 128, KH * 128])
    din("su", [22, 128, KH * 128])
    din("sd", [8, 128, KI * 512])
    din("wi", [64, 128, KH * 128])
    din("cv", [32, 128, 32 * 128])
    din("wo", [8, 128, 16 * 512])
    din("rw", [128, 128])
    din("dvec", [128, 32])
    din("bsg", [128, 22])
    din("bsu", [128, 22])
    din("bsdb", [128, H])
    out_ap = nc.dram_tensor("out", [TOKC, H], F32, kind="ExternalOutput").ap()
    scr = {}
    scr["wi"] = nc.dram_tensor("scr_wi", [64, 128, KH * 128], BF16, kind="Internal").ap()
    scr["cv"] = nc.dram_tensor("scr_cv", [32, 128, 32 * 128], BF16, kind="Internal").ap()
    scr["wo"] = nc.dram_tensor("scr_wo", [8, 128, 16 * 512], BF16, kind="Internal").ap()

    p = Plan()
    for s in ("pe", "dv", "ac", "st", "ld_y", "ld_x", "ld_c", "scr_c", "tr0", "tr1"):
        p.sem(s)
    BUFN = (("g", 2), ("u", 2), ("d", 2), ("wi", 3), ("cvb", 2), ("wob", 3))
    for pref, n in BUFN:
        for i in range(n):
            p.sem(f"ld_{pref}{i}")

    es = ExitStack()
    def sbuf(name, cols, dt):
        return es.enter_context(nc.sbuf_tensor(name, [128, cols], dt))

    rbuf = sbuf("rbuf", TT * H, F32)
    uT = sbuf("uT", KH * GT, BF16)
    ubf = sbuf("ubf", H, BF16)
    tsil = sbuf("tsil", 2048, BF16)
    bsdb_sb = sbuf("bsdb_sb", H, F32)
    stats = sbuf("stats", 8 * TT, F32)
    lgsb = sbuf("lgsb", 8 * TT, F32)
    esb = sbuf("esb", 8 * TT, F32)
    tmp8 = sbuf("tmp8", 8 * TT, F32)
    tmp8b = sbuf("tmp8b", 8 * TT, F32)
    st2 = sbuf("st2", 8 * TT, F32)
    c_buf = sbuf("c_buf", 8 * TT, F32)
    rw_bf = sbuf("rw_bf", 128, BF16)
    dvec_sb = sbuf("dvec_sb", 32, F32)
    bsg_sb = sbuf("bsg_sb", 22, F32)
    bsu_sb = sbuf("bsu_sb", 22, F32)

    off_h = 0
    sz_h = KI * GT
    off_g = [sz_h + j * 2048 for j in range(2)]
    off_u = [sz_h + 2 * 2048 + j * 2048 for j in range(2)]
    off_d = [sz_h + 4 * 2048 + j * (KI * 512) for j in range(2)]
    ffn_end = sz_h + 4 * 2048 + 2 * KI * 512
    off_xT = 0
    off_sb_ = 32 * SGT
    off_wi = [max(ffn_end, 64 * SGT) + j * 2048 for j in range(3)]
    off_cv = [off_wi[0] + 3 * 2048 + j * 4096 for j in range(2)]
    attn_end = off_cv[0] + 2 * 4096
    if GT == 1024:
        off_wo = [0, 8192, off_wi[0]]
        arena_cols = attn_end
    else:
        off_wo = [attn_end + j * 8192 for j in range(3)]
        arena_cols = attn_end + 3 * 8192
    arena = sbuf("arena", arena_cols, BF16)

    psb = [es.enter_context(nc.psum_tensor(f"pb{i}", [128, 512], F32)) for i in range(8)]
    sems = {}
    for name in p.cnt:
        sems[name] = es.enter_context(nc.semaphore(name))

    bank_guard = {i: ("dv", 0) for i in range(8)}
    buf_lastuse = {}
    buf_gen = {}
    for pref, n in BUFN:
        for j in range(n):
            buf_lastuse[(pref, j)] = 0
            buf_gen[(pref, j)] = 0
    tsil_guard = {0: 0, 1: 0}
    CONSTS = 4 * 16
    SCRT = [0]

    p.emit("sync", lambda e: e.dma_start(out=dvec_sb[:, :], in_=ap["dvec"]), inc="ld_y", amt=16)
    p.emit("sync", lambda e: e.dma_start(out=bsg_sb[:, :], in_=ap["bsg"]), inc="ld_y", amt=16)
    p.emit("sync", lambda e: e.dma_start(out=bsu_sb[:, :], in_=ap["bsu"]), inc="ld_y", amt=16)
    p.emit("sync", lambda e: e.dma_start(out=bsdb_sb[:, :], in_=ap["bsdb"]), inc="ld_y", amt=16)
    p.emit("gpsimd", lambda e: e.dma_start(out=rw_bf[:, :], in_=ap["rw"]), inc="ld_c", amt=16)
    for name in ("wi", "cv", "wo"):
        SCRT[0] = p.emit("gpsimd", lambda e, name=name: e.dma_start(
            out=scr[name], in_=ap[name]), inc="scr_c", amt=16)

    WQ = {"g": "gpsimd", "u": "gpsimd", "d": "gpsimd",
          "wi": "scalar", "cvb": "scalar", "wob": "scalar"}
    WSRC_SCR = {"wi": "wi", "cvb": "cv", "wob": "wo"}
    OFFS = {"g": off_g, "u": off_u, "d": off_d, "wi": off_wi, "cvb": off_cv, "wob": off_wo}
    NCOLS = {"g": 2048, "u": 2048, "d": KI * 512, "wi": 2048, "cvb": 4096, "wob": 8192}

    def wdma(stream, j, src_name, blk, extra_pe_guard=0):
        gen = buf_gen[(stream, j)] = buf_gen[(stream, j)] + 1
        guard = max(buf_lastuse[(stream, j)], extra_pe_guard)
        o = OFFS[stream][j]
        ncols = NCOLS[stream]
        waits = []
        if guard:
            waits.append(("pe", guard))
        if stream in WSRC_SCR:
            waits.append(("scr_c", SCRT[0]))
            srcap = scr[WSRC_SCR[stream]]
        else:
            srcap = ap[src_name]
        p.emit(WQ[stream],
               lambda e, o=o, n=ncols, s=srcap, b=blk: e.dma_start(
                   out=arena[:, o:o + n], in_=s[b]),
               waits=waits, inc=f"ld_{stream}{j}", amt=16)
        return gen * 16

    def a_view(stream, j, lo, n):
        o = OFFS[stream][j]
        return arena[:, o + lo: o + lo + n]

    SUBIDX = [0]
    lnbuf = [ubf, tsil]
    lnbuf_dma_guard = [0, 0]
    tr_total = [0, 0]
    def ln_phase(g):
        acg = p.cnt["ac"]
        v_negm = [0] * TT
        a_sq = [0] * TT
        v_var = [0] * TT
        a_sd = [0] * TT
        NPIECE = min(4, TT)
        TPP = TT // NPIECE
        for t in range(TT):
            w0 = [("ld_x", (g * NPIECE + t // TPP + 1) * 16)] if SUBIDX[0] == 0 else []
            p.emit("vector", lambda e, t=t: e.reduce_sum(
                stats[:, t * 8: t * 8 + 1], rbuf[:, t * H:(t + 1) * H], axis=AX.X),
                waits=w0)
        for t in range(TT):
            v_negm[t] = p.emit("vector", lambda e, t=t: e.tensor_scalar(
                stats[:, t * 8 + 1: t * 8 + 2], stats[:, t * 8: t * 8 + 1],
                -1.0 / H, None, ALU.mult))
        for t in range(TT):
            buf = lnbuf[t % 2]
            a_sq[t] = p.emit("scalar", lambda e, t=t, buf=buf: e.activation(
                buf[:, :], rbuf[:, t * H:(t + 1) * H], AF.Square,
                bias=stats[:, t * 8 + 1: t * 8 + 2], scale=1.0,
                accum_out=stats[:, t * 8 + 2: t * 8 + 3]),
                waits=[("dv", v_negm[t]), (f"tr{t % 2}", lnbuf_dma_guard[t % 2])])
        for t in range(TT):
            v_var[t] = p.emit("vector", lambda e, t=t: e.tensor_scalar(
                stats[:, t * 8 + 3: t * 8 + 4], stats[:, t * 8 + 2: t * 8 + 3],
                1.0 / H, 1e-5, ALU.mult, ALU.add), waits=[("ac", a_sq[t])])
        for t in range(TT):
            a_sd[t] = p.emit("scalar", lambda e, t=t: e.activation(
                stats[:, t * 8 + 4: t * 8 + 5], stats[:, t * 8 + 3: t * 8 + 4],
                AF.Sqrt), waits=[("dv", v_var[t])])
        for t in range(TT):
            p.emit("vector", lambda e, t=t: e.reciprocal(
                stats[:, t * 8 + 5: t * 8 + 6], stats[:, t * 8 + 4: t * 8 + 5]),
                waits=[("ac", a_sd[t])])
        for t in range(TT):
            buf = lnbuf[t % 2]
            v_ubf = p.emit("vector", lambda e, t=t, buf=buf: e.tensor_scalar(
                buf[:, :], rbuf[:, t * H:(t + 1) * H], stats[:, t * 8 + 1: t * 8 + 2],
                stats[:, t * 8 + 5: t * 8 + 6], ALU.add, ALU.mult),
                waits=[(f"tr{t % 2}", lnbuf_dma_guard[t % 2]), ("ac", acg)])
            for c in range(KH):
                tr_total[t % 2] = p.emit("sync", lambda e, t=t, c=c, buf=buf: e.dma_start(
                    uT[:, c * GT + t * 128: c * GT + t * 128 + 128],
                    buf[:, c * 128:(c + 1) * 128], transpose=True),
                    waits=[("dv", v_ubf)], inc=f"tr{t % 2}", amt=16)
            lnbuf_dma_guard[t % 2] = tr_total[t % 2]
            if t % 2 == 1:
                tsil_guard[0] = max(tsil_guard[0], v_ubf)
                tsil_guard[1] = max(tsil_guard[1], v_ubf)
        return (("tr0", tr_total[0]), ("tr1", tr_total[1]))

    def router_phase(uT_ready):
        for t in range(TT):
            bk = t % 2
            s0, v0 = bank_guard[bk]
            ins_l = None
            for c in range(KH):
                ins_l = p.emit("tensor", lambda e, c=c, t=t, bk=bk: e.matmul(
                    psb[bk][:, 0:8], uT[:, c * GT + t * 128: c * GT + t * 128 + 128],
                    rw_bf[:, c * 8:(c + 1) * 8], start=(c == 0), stop=(c == KH - 1)),
                    waits=[uT_ready[0], uT_ready[1], (s0, v0), ("ld_c", 16)],
                    inc="pe" if c == KH - 1 else None)
            p_lg = ins_l
            gv = p.emit("vector", lambda e, t=t, bk=bk: e.tensor_copy(
                lgsb[:, t * 8:(t + 1) * 8], psb[bk][:, 0:8]),
                waits=[("pe", p_lg)])
            bank_guard[bk] = ("dv", gv)
            p.emit("vector", lambda e, t=t: e.reduce_max(
                st2[:, t * 8: t * 8 + 1], lgsb[:, t * 8:(t + 1) * 8], axis=AX.X))
            v_ngm = p.emit("vector", lambda e, t=t: e.tensor_scalar(
                st2[:, t * 8 + 1: t * 8 + 2], st2[:, t * 8: t * 8 + 1],
                -1.0, None, ALU.mult))
            a_exp = p.emit("scalar", lambda e, t=t: e.activation(
                esb[:, t * 8:(t + 1) * 8], lgsb[:, t * 8:(t + 1) * 8], AF.Exp,
                bias=st2[:, t * 8 + 1: t * 8 + 2], scale=1.0),
                waits=[("dv", v_ngm)])
            p.emit("vector", lambda e, t=t: e.tensor_scalar(
                tmp8[:, t * 8:(t + 1) * 8], esb[:, t * 8:(t + 1) * 8],
                1.0, -2.0, ALU.is_ge, ALU.mult), waits=[("ac", a_exp)])
            p.emit("vector", lambda e, t=t: e.tensor_tensor(
                tmp8[:, t * 8:(t + 1) * 8], tmp8[:, t * 8:(t + 1) * 8],
                esb[:, t * 8:(t + 1) * 8], ALU.add))
            v_m2 = p.emit("vector", lambda e, t=t: e.reduce_max(
                st2[:, t * 8 + 2: t * 8 + 3], tmp8[:, t * 8:(t + 1) * 8], axis=AX.X))
            p.emit("vector", lambda e, t=t: e.tensor_scalar(
                st2[:, t * 8 + 3: t * 8 + 4], st2[:, t * 8 + 2: t * 8 + 3],
                1.0, None, ALU.add))
            v_inv = p.emit("vector", lambda e, t=t: e.reciprocal(
                st2[:, t * 8 + 4: t * 8 + 5], st2[:, t * 8 + 3: t * 8 + 4]))
            p.emit("vector", lambda e, t=t: e.tensor_scalar(
                tmp8b[:, t * 8:(t + 1) * 8], esb[:, t * 8:(t + 1) * 8],
                st2[:, t * 8 + 2: t * 8 + 3], None, ALU.is_ge),
                waits=[("dv", v_m2)])
            p.emit("vector", lambda e, t=t: e.tensor_tensor(
                tmp8b[:, t * 8:(t + 1) * 8], tmp8b[:, t * 8:(t + 1) * 8],
                esb[:, t * 8:(t + 1) * 8], ALU.mult))
            p.emit("vector", lambda e, t=t: e.tensor_scalar(
                c_buf[:, t * 8:(t + 1) * 8], tmp8b[:, t * 8:(t + 1) * 8],
                st2[:, t * 8 + 4: t * 8 + 5], 0.5, ALU.mult, ALU.mult),
                waits=[("dv", v_inv)])

    def ffn_phase(g, uT_ready, pair_ctr=[0]):
        fin_tiles = [0] * TT
        for ex in range(10):
            shared = ex >= 8
            hf = ex - 8
            gsrc = "sg" if shared else "gw"
            usrc = "su" if shared else "uw"
            bbase = (hf if shared else ex) * KI
            for m in range(KI):
                gj = (ex * KI + m) % 2
                wg = wdma("g", gj, gsrc, bbase + m)
                wu = wdma("u", gj, usrc, bbase + m)
                pr = pair_ctr[0] % 4
                pair_ctr[0] += 1
                pu = pair_ctr[0] % 4
                pair_ctr[0] += 1
                bG = (2 * pr, 2 * pr + 1)
                bU = (2 * pu, 2 * pu + 1)
                gw0, gw1 = bank_guard[bG[0]], bank_guard[bG[1]]
                p_G = None
                for k in range(KH):
                    for h in range(2):
                        last = (k == KH - 1 and h == 1)
                        p_G = p.emit("tensor", lambda e, k=k, h=h, gj=gj, bG=bG: e.matmul(
                            psb[bG[h]][:, 0:HGT], a_view("g", gj, k * 128, 128),
                            uT[:, k * GT + h * HGT: k * GT + (h + 1) * HGT],
                            start=(k == 0), stop=(k == KH - 1)),
                            waits=[(f"ld_g{gj}", wg), uT_ready[0], uT_ready[1], gw0, gw1],
                            inc="pe" if last else None)
                gw0, gw1 = bank_guard[bU[0]], bank_guard[bU[1]]
                p_U = None
                for k in range(KH):
                    for h in range(2):
                        last = (k == KH - 1 and h == 1)
                        p_U = p.emit("tensor", lambda e, k=k, h=h, gj=gj, bU=bU: e.matmul(
                            psb[bU[h]][:, 0:HGT], a_view("u", gj, k * 128, 128),
                            uT[:, k * GT + h * HGT: k * GT + (h + 1) * HGT],
                            start=(k == 0), stop=(k == KH - 1)),
                            waits=[(f"ld_u{gj}", wu), gw0, gw1],
                            inc="pe" if last else None)
                buf_lastuse[("g", gj)] = p_G
                buf_lastuse[("u", gj)] = p_U
                for h in range(2):
                    slot = (m % 2) * 1024 + h * 512
                    tsw = tsil_guard[m % 2]
                    sw = [("pe", p_G), ("dv", tsw), uT_ready[0], uT_ready[1], ("ld_y", CONSTS)]
                    if shared:
                        col = hf * KI + m
                        a_sil = p.emit("scalar", lambda e, h=h, slot=slot, bG=bG, col=col: e.activation(
                            tsil[:, slot: slot + HGT], psb[bG[h]][:, 0:HGT], AF.Silu,
                            bias=bsg_sb[:, col: col + 1], scale=1.0), waits=sw)
                    else:
                        a_sil = p.emit("scalar", lambda e, h=h, slot=slot, bG=bG: e.activation(
                            tsil[:, slot: slot + HGT], psb[bG[h]][:, 0:HGT], AF.Silu),
                            waits=sw)
                    bank_guard[bG[h]] = ("ac", a_sil)
                    hv = m * GT + h * HGT
                    if shared:
                        col = hf * KI + m
                        dvv = p.emit("vector", lambda e, h=h, slot=slot, bU=bU, hv=hv, col=col: e.scalar_tensor_tensor(
                            arena[:, off_h + hv: off_h + hv + HGT], psb[bU[h]][:, 0:HGT],
                            bsu_sb[:, col: col + 1], tsil[:, slot: slot + HGT],
                            ALU.add, ALU.mult),
                            waits=[("pe", p_U), ("ac", a_sil), ("ld_y", CONSTS)])
                    else:
                        dvv = p.emit("vector", lambda e, h=h, slot=slot, bU=bU, hv=hv: e.tensor_tensor(
                            arena[:, off_h + hv: off_h + hv + HGT], psb[bU[h]][:, 0:HGT],
                            tsil[:, slot: slot + HGT], ALU.mult),
                            waits=[("pe", p_U), ("ac", a_sil)])
                    bank_guard[bU[h]] = ("dv", dvv)
                    tsil_guard[m % 2] = dvv
            hbuf_ready = p.cnt["dv"]
            for n_ in range(4):
                dj = (ex * 4 + n_) % 2
                wd = wdma("d", dj, "sd" if shared else "dw",
                          (hf if shared else ex) * 4 + n_)
                p_dn_last = 0
                for t in range(TT):
                    pr = pair_ctr[0] % 4
                    pair_ctr[0] += 1
                    bks = (2 * pr, 2 * pr + 1)
                    gw0, gw1 = bank_guard[bks[0]], bank_guard[bks[1]]
                    p_dn = None
                    for k in range(KI):
                        for h in range(2):
                            last = (k == KI - 1 and h == 1)
                            p_dn = p.emit("tensor", lambda e, k=k, h=h, t=t, dj=dj, bks=bks: e.matmul(
                                psb[bks[h]][:, 0:256],
                                arena[:, off_h + k * GT + t * 128: off_h + k * GT + t * 128 + 128],
                                a_view("d", dj, k * 512 + h * 256, 256),
                                start=(k == 0), stop=(k == KI - 1)),
                                waits=[(f"ld_d{dj}", wd), ("dv", hbuf_ready), gw0, gw1],
                                inc="pe" if last else None)
                    p_dn_last = p_dn
                    for h in range(2):
                        rcol = t * H + n_ * 512 + h * 256
                        if shared:
                            dvv = p.emit("vector", lambda e, h=h, rcol=rcol, bks=bks: e.scalar_tensor_tensor(
                                rbuf[:, rcol: rcol + 256], psb[bks[h]][:, 0:256],
                                0.5, rbuf[:, rcol: rcol + 256], ALU.mult, ALU.add),
                                waits=[("pe", p_dn)])
                        else:
                            dvv = p.emit("vector", lambda e, h=h, rcol=rcol, bks=bks, ex=ex, t=t: e.scalar_tensor_tensor(
                                rbuf[:, rcol: rcol + 256], psb[bks[h]][:, 0:256],
                                c_buf[:, t * 8 + ex: t * 8 + ex + 1],
                                rbuf[:, rcol: rcol + 256], ALU.mult, ALU.add),
                                waits=[("pe", p_dn)])
                        bank_guard[bks[h]] = ("dv", dvv)
                    if ex == 9:
                        rc2 = t * H + n_ * 512
                        fin_tiles[t] = p.emit("vector", lambda e, rc2=rc2, n_=n_: e.scalar_tensor_tensor(
                            rbuf[:, rc2: rc2 + 512],
                            bsdb_sb[:, n_ * 512: n_ * 512 + 512], 0.5,
                            rbuf[:, rc2: rc2 + 512], ALU.mult, ALU.add),
                            waits=[("ld_y", CONSTS)])
                buf_lastuse[("d", dj)] = p_dn_last
        return fin_tiles

    def attn_phase(g, uT_ready):
        for sg_i in range(2):
            for m in range(32):
                wj = m % 3
                wv = wdma("wi", wj, "wi", m)
                pr = m % 4
                bks = (2 * pr, 2 * pr + 1)
                gg = [bank_guard[bks[0]], bank_guard[bks[1]]]
                p_xt = None
                for k in range(KH):
                    for q in range(2):
                        last = (k == KH - 1 and q == 1)
                        p_xt = p.emit("tensor", lambda e, k=k, q=q, wj=wj, bks=bks, sg_i=sg_i: e.matmul(
                            psb[bks[q]][:, 0:NQ],
                            a_view("wi", wj, k * 128, 128),
                            uT[:, k * GT + sg_i * SGT + q * NQ: k * GT + sg_i * SGT + (q + 1) * NQ],
                            start=(k == 0), stop=(k == KH - 1)),
                            waits=[(f"ld_wi{wj}", wv), uT_ready[0], uT_ready[1], gg[0], gg[1]],
                            inc="pe" if last else None)
                buf_lastuse[("wi", wj)] = p_xt
                for q in range(2):
                    gv = p.emit("vector", lambda e, m=m, q=q, bks=bks: e.tensor_copy(
                        arena[:, off_xT + m * SGT + q * NQ: off_xT + m * SGT + (q + 1) * NQ],
                        psb[bks[q]][:, 0:NQ]),
                        waits=[("pe", p_xt)])
                    bank_guard[bks[q]] = ("dv", gv)
            xT_ready = p.cnt["dv"]
            pB_last = 0
            for m in range(32):
                cj = m % 2
                zj = m % 3
                wc = wdma("cvb", cj, "cv", m)
                wz = wdma("wi", zj, "wi", 32 + m)
                zpr = (2 * m) % 4
                cpr = (2 * m + 1) % 4
                bz = (2 * zpr, 2 * zpr + 1)
                bc = (2 * cpr, 2 * cpr + 1)
                gg = [bank_guard[bz[0]], bank_guard[bz[1]]]
                p_z = None
                for k in range(KH):
                    for q in range(2):
                        last = (k == KH - 1 and q == 1)
                        p_z = p.emit("tensor", lambda e, k=k, q=q, zj=zj, bz=bz, sg_i=sg_i: e.matmul(
                            psb[bz[q]][:, 0:NQ],
                            a_view("wi", zj, k * 128, 128),
                            uT[:, k * GT + sg_i * SGT + q * NQ: k * GT + sg_i * SGT + (q + 1) * NQ],
                            start=(k == 0), stop=(k == KH - 1)),
                            waits=[(f"ld_wi{zj}", wz), gg[0], gg[1]],
                            inc="pe" if last else None)
                buf_lastuse[("wi", zj)] = p_z
                gg = [bank_guard[bc[0]], bank_guard[bc[1]]]
                p_xc = None
                for k in range(32):
                    for q in range(2):
                        last = (k == 31 and q == 1)
                        p_xc = p.emit("tensor", lambda e, k=k, q=q, cj=cj, bc=bc: e.matmul(
                            psb[bc[q]][:, 0:NQ],
                            a_view("cvb", cj, k * 128, 128),
                            arena[:, off_xT + k * SGT + q * NQ: off_xT + k * SGT + (q + 1) * NQ],
                            start=(k == 0), stop=(k == 31)),
                            waits=[(f"ld_cvb{cj}", wc), ("dv", xT_ready), gg[0], gg[1]],
                            inc="pe" if last else None)
                buf_lastuse[("cvb", cj)] = p_xc
                pB_last = p_xc
                slot = (m % 2) * SGT
                for q in range(2):
                    tsw = tsil_guard[m % 2]
                    a_z = p.emit("scalar", lambda e, slot=slot, q=q, bz=bz: e.activation(
                        tsil[:, slot + q * NQ: slot + (q + 1) * NQ],
                        psb[bz[q]][:, 0:NQ], AF.Silu),
                        waits=[("pe", p_z), ("dv", tsw), uT_ready[0], uT_ready[1]])
                    bank_guard[bz[q]] = ("ac", a_z)
                    gv = p.emit("vector", lambda e, m=m, slot=slot, q=q, bc=bc: e.scalar_tensor_tensor(
                        arena[:, off_sb_ + m * SGT + q * NQ: off_sb_ + m * SGT + (q + 1) * NQ],
                        psb[bc[q]][:, 0:NQ], dvec_sb[:, m: m + 1],
                        tsil[:, slot + q * NQ: slot + (q + 1) * NQ], ALU.mult, ALU.mult),
                        waits=[("pe", p_xc), ("ac", a_z), ("ld_y", CONSTS)])
                    bank_guard[bc[q]] = ("dv", gv)
                    tsil_guard[m % 2] = gv
            sb_ready = p.cnt["dv"]
            for n in range(4):
                wvs = []
                for kh in range(2):
                    wj = (n * 2 + kh) % 3
                    wvs.append((wj, wdma("wob", wj, "wo", n * 2 + kh,
                                         extra_pe_guard=pB_last)))
                for t in range(STT):
                    pc = (n * STT + t) % 4
                    bks = (2 * pc, 2 * pc + 1)
                    gg = [bank_guard[bks[0]], bank_guard[bks[1]]]
                    p_o = None
                    for k in range(32):
                        kh = k // 16
                        wj, wv = wvs[kh]
                        for q in range(2):
                            last = (k == 31 and q == 1)
                            p_o = p.emit("tensor", lambda e, k=k, q=q, t=t, wj=wj, bks=bks: e.matmul(
                                psb[bks[q]][:, 0:256],
                                arena[:, off_sb_ + k * SGT + t * 128: off_sb_ + k * SGT + t * 128 + 128],
                                a_view("wob", wj, (k % 16) * 512 + q * 256, 256),
                                start=(k == 0), stop=(k == 31)),
                                waits=[(f"ld_wob{wj}", wv), ("dv", sb_ready), gg[0], gg[1]],
                                inc="pe" if last else None)
                    buf_lastuse[("wob", wvs[0][0])] = p_o
                    buf_lastuse[("wob", wvs[1][0])] = p_o
                    rcol = (sg_i * STT + t) * H + n * 512
                    for q in range(2):
                        gv = p.emit("vector", lambda e, rcol=rcol, q=q, bks=bks: e.tensor_tensor(
                            rbuf[:, rcol + q * 256: rcol + (q + 1) * 256],
                            psb[bks[q]][:, 0:256],
                            rbuf[:, rcol + q * 256: rcol + (q + 1) * 256], ALU.add),
                            waits=[("pe", p_o)])
                        bank_guard[bks[q]] = ("dv", gv)
            af_sg = p.cnt["pe"]
            for key in (("wi", 0), ("wi", 1), ("wi", 2), ("cvb", 0), ("cvb", 1)):
                buf_lastuse[key] = max(buf_lastuse[key], af_sg)
        af = p.cnt["pe"]
        for key in (("g", 0), ("g", 1), ("u", 0), ("u", 1), ("d", 0), ("d", 1)):
            buf_lastuse[key] = max(buf_lastuse[key], af)

    st_cnt = 0
    fin_tiles = [0] * TT
    for g in range(NG):
        NPIECE = min(4, TT)
        for piece in range(NPIECE):
            waits = []
            if g > 0 and piece == 0:
                waits.append(("st", st_cnt))
            if piece > 0:
                waits.append(("ld_x", (g * NPIECE + piece) * 16))
            p.emit("sync", lambda e, g=g, piece=piece, NPIECE=NPIECE: e.dma_start(
                out=rbuf[:, piece * (TT // NPIECE) * H:(piece + 1) * (TT // NPIECE) * H]
                    .rearrange("q (t h) -> q t h", h=H),
                in_=ap["x"][g * GT + piece * (GT // NPIECE): g * GT + (piece + 1) * (GT // NPIECE), :]
                    .rearrange("(t q) h -> q t h", q=128)),
                waits=waits, inc="ld_x", amt=16)
        for sub in range(3):
            SUBIDX[0] = sub
            uT_ready = ln_phase(g)
            if sub in (0, 2):
                router_phase(uT_ready)
                fin_tiles = ffn_phase(g, uT_ready)
            else:
                attn_phase(g, uT_ready)
        for piece in range(2):
            fin = max(fin_tiles[piece * (TT // 2): (piece + 1) * (TT // 2)])
            st_cnt = p.emit("sync", lambda e, g=g, piece=piece: e.dma_start(
                out=out_ap[g * GT + piece * (GT // 2): g * GT + (piece + 1) * (GT // 2), :]
                    .rearrange("(t q) h -> q t h", q=128),
                in_=rbuf[:, piece * (TT // 2) * H:(piece + 1) * (TT // 2) * H]
                    .rearrange("q (t h) -> q t h", h=H)),
                waits=[("dv", fin)], inc="st", amt=16)
    p.emit("sync", None, waits=[("st", st_cnt)])

    with es:
        block = es.enter_context(nc.Block())
        for eng in Plan.ENGINES:
            ops = p.ops[eng]
            if not ops:
                continue
            def body(e, ops=ops):
                hwm = {}
                for waits, fn, inc, amt in ops:
                    for s, v in waits:
                        if hwm.get(s, 0) < v:
                            e.wait_ge(sems[s], v)
                            hwm[s] = v
                    if fn is not None:
                        ins = fn(e)
                        if inc is not None:
                            ins.then_inc(sems[inc], amt)
            getattr(block, eng)(body)
    return nc


def pack_inputs(i):
    c = np.ascontiguousarray
    d = {}
    d["gw"] = c(np.asarray(i["L0_ffn_gate_w"]).reshape(8, 16, 128, 11, 128)
                .transpose(0, 3, 2, 1, 4).reshape(88, 128, 2048))
    d["uw"] = c(np.asarray(i["L0_ffn_up_w"]).reshape(8, 16, 128, 11, 128)
                .transpose(0, 3, 2, 1, 4).reshape(88, 128, 2048))
    d["dw"] = c(np.asarray(i["L0_ffn_down_w"]).reshape(8, 11, 128, 4, 512)
                .transpose(0, 3, 2, 1, 4).reshape(32, 128, 5632))
    d["sg"] = c(np.asarray(i["L0_ffn_s_gate_w"]).reshape(16, 128, 2, 11, 128)
                .transpose(2, 3, 1, 0, 4).reshape(22, 128, 2048))
    d["su"] = c(np.asarray(i["L0_ffn_s_up_w"]).reshape(16, 128, 2, 11, 128)
                .transpose(2, 3, 1, 0, 4).reshape(22, 128, 2048))
    d["sd"] = c(np.asarray(i["L0_ffn_s_down_w"]).reshape(2, 11, 128, 4, 512)
                .transpose(0, 3, 2, 1, 4).reshape(8, 128, 5632))
    d["wi"] = c(np.asarray(i["L0_attn_W_in"]).reshape(64, 128, 16, 128)
                .transpose(0, 3, 2, 1).reshape(64, 128, 2048))
    d["cv"] = c(np.asarray(i["L0_attn_conv_w"]).reshape(32, 128, 32, 128)
                .transpose(0, 3, 2, 1).reshape(32, 128, 4096))
    d["wo"] = c(np.asarray(i["L0_attn_W_out"]).reshape(4, 512, 2, 16, 128)
                .transpose(0, 2, 4, 3, 1).reshape(8, 128, 8192))
    d["rw"] = c(np.asarray(i["L0_ffn_router_w"]).reshape(8, 16, 128)
                .transpose(2, 1, 0).reshape(128, 128))
    d["dvec"] = c(np.asarray(i["L0_attn_D"]).reshape(32, 128).T)
    d["bsg"] = c(np.asarray(i["L0_ffn_s_gate_b"]).reshape(22, 128).T)
    d["bsu"] = c(np.asarray(i["L0_ffn_s_up_b"]).reshape(22, 128).T)
    d["bsdb"] = c(np.broadcast_to(np.asarray(i["L0_ffn_s_down_b"]), (128, 2048)))
    return d


def kernel(**inputs):
    shared = pack_inputs(inputs)
    x = np.ascontiguousarray(np.asarray(inputs["L0_attn_hidden_states"],
                                        dtype=np.float32).reshape(16384, 2048))
    nc = build_program(GT=1024, TOKC=2048)
    in_maps = []
    for cidx in range(N_CORES):
        m = dict(shared)
        m["x"] = np.ascontiguousarray(x[cidx * 2048:(cidx + 1) * 2048])
        in_maps.append(m)
    res = run_bass_kernel_spmd(nc, in_maps, core_ids=list(range(N_CORES)))
    out = np.concatenate([np.asarray(r["out"]) for r in res.results], axis=0)
    return out.reshape(1, 16384, 2048).astype(np.float32)


# version 04
# speedup vs baseline: 1.0877x; 1.0877x over previous
import numpy as np
import ml_dtypes
from contextlib import ExitStack

import concourse.bass as bass
import concourse.mybir as mybir
from concourse.bass_utils import run_bass_kernel_spmd

F32 = mybir.dt.float32
BF16 = mybir.dt.bfloat16
AF = mybir.ActivationFunctionType
ALU = mybir.AluOpType
AX = mybir.AxisListType
BFNP = ml_dtypes.bfloat16

H = 2048
KH = H // 128
I_EXP = 1408
KI = I_EXP // 128
N_CORES = 8


class Plan:
    ENGINES = ("tensor", "vector", "scalar", "gpsimd", "sync")
    SELF = {"vector": "dv", "scalar": "ac"}

    def __init__(self):
        self.ops = {e: [] for e in self.ENGINES}
        self.cnt = {}

    def sem(self, name):
        self.cnt.setdefault(name, 0)

    def emit(self, eng, fn, waits=(), inc=None, amt=1):
        if eng in self.SELF and fn is not None and (inc is None or inc == self.SELF[eng]):
            s = self.SELF[eng]
            waits = tuple(waits) + ((s, self.cnt[s]),)
            if inc is None:
                inc, amt = s, 1
        for s, v in waits:
            assert s in self.cnt, s
            assert v <= self.cnt[s], ("wait ahead of plan", s, v, self.cnt[s])
        self.ops[eng].append((tuple(waits), fn, inc, amt))
        if inc is not None:
            self.cnt[inc] += amt
            return self.cnt[inc]
        return None


def build_program(GT=1024, TOKC=2048):
    NG = TOKC // GT
    TT = GT // 128
    SGT = GT // 2
    STT = TT // 2
    NQ = GT // 4
    HGT = GT // 2

    nc = bass.Bass("TRN2", target_bir_lowering=False)
    ap = {}
    def din(name, shape, dt=F32):
        ap[name] = nc.dram_tensor(name, shape, dt, kind="ExternalInput").ap()
    din("x", [TOKC, H])
    din("gw", [88, 128, KH * 128])
    din("uw", [88, 128, KH * 128])
    din("dw", [32, 128, KI * 512])
    din("sg", [22, 128, KH * 128])
    din("su", [22, 128, KH * 128])
    din("sd", [8, 128, KI * 512])
    din("wi", [64, 128, KH * 128])
    din("cv", [32, 128, 32 * 128])
    din("wo", [8, 128, 16 * 512])
    din("rw", [128, 128])
    din("dvec", [128, 32])
    din("bsg", [128, 22])
    din("bsu", [128, 22])
    din("bsdb", [128, H])
    din("idn", [128, 128], BF16)
    out_ap = nc.dram_tensor("out", [TOKC, H], F32, kind="ExternalOutput").ap()
    scr = {}
    scr["wi"] = nc.dram_tensor("scr_wi", [64, 128, KH * 128], BF16, kind="Internal").ap()
    scr["cv"] = nc.dram_tensor("scr_cv", [32, 128, 32 * 128], BF16, kind="Internal").ap()
    scr["wo"] = nc.dram_tensor("scr_wo", [8, 128, 16 * 512], BF16, kind="Internal").ap()

    p = Plan()
    for s in ("pe", "dv", "ac", "st", "ld_y", "ld_x", "ld_c", "scr_c"):
        p.sem(s)
    BUFN = (("g", 2), ("u", 2), ("d", 2), ("wi", 3), ("cvb", 2), ("wob", 3))
    for pref, n in BUFN:
        for i in range(n):
            p.sem(f"ld_{pref}{i}")

    es = ExitStack()
    def sbuf(name, cols, dt):
        return es.enter_context(nc.sbuf_tensor(name, [128, cols], dt))

    rbuf = sbuf("rbuf", TT * H, F32)
    uT = sbuf("uT", KH * GT, BF16)
    ubf = sbuf("ubf", H, BF16)
    tsil = sbuf("tsil", 2048, BF16)
    bsdb_sb = sbuf("bsdb_sb", H, F32)
    stats = sbuf("stats", 8 * TT, F32)
    lgsb = sbuf("lgsb", 8 * TT, F32)
    esb = sbuf("esb", 8 * TT, F32)
    tmp8 = sbuf("tmp8", 8 * TT, F32)
    tmp8b = sbuf("tmp8b", 8 * TT, F32)
    st2 = sbuf("st2", 8 * TT, F32)
    c_buf = sbuf("c_buf", 8 * TT, F32)
    rw_bf = sbuf("rw_bf", 128, BF16)
    idn_sb = sbuf("idn_sb", 128, BF16)
    dvec_sb = sbuf("dvec_sb", 32, F32)
    bsg_sb = sbuf("bsg_sb", 22, F32)
    bsu_sb = sbuf("bsu_sb", 22, F32)

    off_h = 0
    sz_h = KI * GT
    off_g = [sz_h + j * 2048 for j in range(2)]
    off_u = [sz_h + 2 * 2048 + j * 2048 for j in range(2)]
    off_d = [sz_h + 4 * 2048 + j * (KI * 512) for j in range(2)]
    ffn_end = sz_h + 4 * 2048 + 2 * KI * 512
    off_xT = 0
    off_sb_ = 32 * SGT
    off_wi = [max(ffn_end, 64 * SGT) + j * 2048 for j in range(3)]
    off_cv = [off_wi[0] + 3 * 2048 + j * 4096 for j in range(2)]
    attn_end = off_cv[0] + 2 * 4096
    if GT == 1024:
        off_wo = [0, 8192, off_wi[0]]
        arena_cols = attn_end
    else:
        off_wo = [attn_end + j * 8192 for j in range(3)]
        arena_cols = attn_end + 3 * 8192
    arena = sbuf("arena", arena_cols, BF16)

    psb = [es.enter_context(nc.psum_tensor(f"pb{i}", [128, 512], F32)) for i in range(6)]
    ptb = [es.enter_context(nc.psum_tensor(f"pt{i}", [128, 1024], BF16)) for i in range(2)]
    sems = {}
    for name in p.cnt:
        sems[name] = es.enter_context(nc.semaphore(name))

    bank_guard = {i: ("dv", 0) for i in range(6)}
    ptb_guard = [("dv", 0), ("dv", 0)]
    buf_lastuse = {}
    buf_gen = {}
    for pref, n in BUFN:
        for j in range(n):
            buf_lastuse[(pref, j)] = 0
            buf_gen[(pref, j)] = 0
    tsil_guard = {0: 0, 1: 0}
    CONSTS = 5 * 16
    SCRT = [0]

    p.emit("sync", lambda e: e.dma_start(out=dvec_sb[:, :], in_=ap["dvec"]), inc="ld_y", amt=16)
    p.emit("sync", lambda e: e.dma_start(out=bsg_sb[:, :], in_=ap["bsg"]), inc="ld_y", amt=16)
    p.emit("sync", lambda e: e.dma_start(out=bsu_sb[:, :], in_=ap["bsu"]), inc="ld_y", amt=16)
    p.emit("sync", lambda e: e.dma_start(out=bsdb_sb[:, :], in_=ap["bsdb"]), inc="ld_y", amt=16)
    p.emit("sync", lambda e: e.dma_start(out=idn_sb[:, :], in_=ap["idn"]), inc="ld_y", amt=16)
    p.emit("gpsimd", lambda e: e.dma_start(out=rw_bf[:, :], in_=ap["rw"]), inc="ld_c", amt=16)
    for name in ("wi", "cv", "wo"):
        SCRT[0] = p.emit("gpsimd", lambda e, name=name: e.dma_start(
            out=scr[name], in_=ap[name]), inc="scr_c", amt=16)

    WQ = {"g": "gpsimd", "u": "gpsimd", "d": "gpsimd",
          "wi": "sync", "cvb": "sync", "wob": "sync"}
    WSRC_SCR = {"wi": "wi", "cvb": "cv", "wob": "wo"}
    OFFS = {"g": off_g, "u": off_u, "d": off_d, "wi": off_wi, "cvb": off_cv, "wob": off_wo}
    NCOLS = {"g": 2048, "u": 2048, "d": KI * 512, "wi": 2048, "cvb": 4096, "wob": 8192}

    def wdma(stream, j, src_name, blk, extra_pe_guard=0):
        gen = buf_gen[(stream, j)] = buf_gen[(stream, j)] + 1
        guard = max(buf_lastuse[(stream, j)], extra_pe_guard)
        o = OFFS[stream][j]
        ncols = NCOLS[stream]
        waits = []
        if guard:
            waits.append(("pe", guard))
        if stream in WSRC_SCR:
            waits.append(("scr_c", SCRT[0]))
            srcap = scr[WSRC_SCR[stream]]
        else:
            srcap = ap[src_name]
        p.emit(WQ[stream],
               lambda e, o=o, n=ncols, s=srcap, b=blk: e.dma_start(
                   out=arena[:, o:o + n], in_=s[b]),
               waits=waits, inc=f"ld_{stream}{j}", amt=16)
        return gen * 16

    def a_view(stream, j, lo, n):
        o = OFFS[stream][j]
        return arena[:, o + lo: o + lo + n]

    SUBIDX = [0]
    lnbuf = [ubf, tsil]
    lnbuf_tr_guard = [0, 0]
    def ln_phase(g):
        acg = p.cnt["ac"]
        v_negm = [0] * TT
        a_sq = [0] * TT
        v_var = [0] * TT
        a_sd = [0] * TT
        NPIECE = min(4, TT)
        TPP = TT // NPIECE
        for t in range(TT):
            w0 = [("ld_x", (g * NPIECE + t // TPP + 1) * 16)] if SUBIDX[0] == 0 else []
            p.emit("vector", lambda e, t=t: e.reduce_sum(
                stats[:, t * 8: t * 8 + 1], rbuf[:, t * H:(t + 1) * H], axis=AX.X),
                waits=w0)
        for t in range(TT):
            v_negm[t] = p.emit("vector", lambda e, t=t: e.tensor_scalar(
                stats[:, t * 8 + 1: t * 8 + 2], stats[:, t * 8: t * 8 + 1],
                -1.0 / H, None, ALU.mult))
        for t in range(TT):
            buf = lnbuf[t % 2]
            a_sq[t] = p.emit("scalar", lambda e, t=t, buf=buf: e.activation(
                buf[:, :], rbuf[:, t * H:(t + 1) * H], AF.Square,
                bias=stats[:, t * 8 + 1: t * 8 + 2], scale=1.0,
                accum_out=stats[:, t * 8 + 2: t * 8 + 3]),
                waits=[("dv", v_negm[t]), ("pe", lnbuf_tr_guard[t % 2])])
        for t in range(TT):
            v_var[t] = p.emit("vector", lambda e, t=t: e.tensor_scalar(
                stats[:, t * 8 + 3: t * 8 + 4], stats[:, t * 8 + 2: t * 8 + 3],
                1.0 / H, 1e-5, ALU.mult, ALU.add), waits=[("ac", a_sq[t])])
        for t in range(TT):
            a_sd[t] = p.emit("scalar", lambda e, t=t: e.activation(
                stats[:, t * 8 + 4: t * 8 + 5], stats[:, t * 8 + 3: t * 8 + 4],
                AF.Sqrt), waits=[("dv", v_var[t])])
        for t in range(TT):
            p.emit("vector", lambda e, t=t: e.reciprocal(
                stats[:, t * 8 + 5: t * 8 + 6], stats[:, t * 8 + 4: t * 8 + 5]),
                waits=[("ac", a_sd[t])])
        for t in range(TT):
            buf = lnbuf[t % 2]
            v_ubf = p.emit("vector", lambda e, t=t, buf=buf: e.tensor_scalar(
                buf[:, :], rbuf[:, t * H:(t + 1) * H], stats[:, t * 8 + 1: t * 8 + 2],
                stats[:, t * 8 + 5: t * 8 + 6], ALU.add, ALU.mult),
                waits=[("pe", lnbuf_tr_guard[t % 2]), ("ac", acg)])
            p_tr = [0, 0]
            for half in range(2):
                s0, v0 = ptb_guard[half]
                for c in range(8):
                    cc = half * 8 + c
                    p_tr[half] = p.emit("tensor", lambda e, cc=cc, half=half, c=c, buf=buf: e.transpose(
                        ptb[half][:, c * 128:(c + 1) * 128],
                        buf[:, cc * 128:(cc + 1) * 128], idn_sb[:, :]),
                        waits=[("dv", v_ubf), (s0, v0), ("ld_y", CONSTS)],
                        inc="pe" if c == 7 else None)
            for half in range(2):
                gv = p.emit("vector", lambda e, half=half, t=t: e.tensor_copy(
                    uT[:, :].rearrange("z (c g) -> z c g", g=GT)[:, half * 8:half * 8 + 8,
                                                                t * 128:t * 128 + 128],
                    ptb[half][:, :].rearrange("z (c g) -> z c g", g=128)),
                    waits=[("pe", p_tr[half])])
                ptb_guard[half] = ("dv", gv)
            lnbuf_tr_guard[t % 2] = p_tr[1]
            if t % 2 == 1:
                tsil_guard[0] = max(tsil_guard[0], v_ubf)
                tsil_guard[1] = max(tsil_guard[1], v_ubf)
        return (("pe", p.cnt["pe"]), ("dv", p.cnt["dv"]))

    def router_phase(uT_ready):
        for t in range(TT):
            bk = t % 2
            s0, v0 = bank_guard[bk]
            ins_l = None
            for c in range(KH):
                ins_l = p.emit("tensor", lambda e, c=c, t=t, bk=bk: e.matmul(
                    psb[bk][:, 0:8], uT[:, c * GT + t * 128: c * GT + t * 128 + 128],
                    rw_bf[:, c * 8:(c + 1) * 8], start=(c == 0), stop=(c == KH - 1)),
                    waits=[uT_ready[0], uT_ready[1], (s0, v0), ("ld_c", 16)],
                    inc="pe" if c == KH - 1 else None)
            p_lg = ins_l
            gv = p.emit("vector", lambda e, t=t, bk=bk: e.tensor_copy(
                lgsb[:, t * 8:(t + 1) * 8], psb[bk][:, 0:8]),
                waits=[("pe", p_lg)])
            bank_guard[bk] = ("dv", gv)
            p.emit("vector", lambda e, t=t: e.reduce_max(
                st2[:, t * 8: t * 8 + 1], lgsb[:, t * 8:(t + 1) * 8], axis=AX.X))
            v_ngm = p.emit("vector", lambda e, t=t: e.tensor_scalar(
                st2[:, t * 8 + 1: t * 8 + 2], st2[:, t * 8: t * 8 + 1],
                -1.0, None, ALU.mult))
            a_exp = p.emit("scalar", lambda e, t=t: e.activation(
                esb[:, t * 8:(t + 1) * 8], lgsb[:, t * 8:(t + 1) * 8], AF.Exp,
                bias=st2[:, t * 8 + 1: t * 8 + 2], scale=1.0),
                waits=[("dv", v_ngm)])
            p.emit("vector", lambda e, t=t: e.tensor_scalar(
                tmp8[:, t * 8:(t + 1) * 8], esb[:, t * 8:(t + 1) * 8],
                1.0, -2.0, ALU.is_ge, ALU.mult), waits=[("ac", a_exp)])
            p.emit("vector", lambda e, t=t: e.tensor_tensor(
                tmp8[:, t * 8:(t + 1) * 8], tmp8[:, t * 8:(t + 1) * 8],
                esb[:, t * 8:(t + 1) * 8], ALU.add))
            v_m2 = p.emit("vector", lambda e, t=t: e.reduce_max(
                st2[:, t * 8 + 2: t * 8 + 3], tmp8[:, t * 8:(t + 1) * 8], axis=AX.X))
            p.emit("vector", lambda e, t=t: e.tensor_scalar(
                st2[:, t * 8 + 3: t * 8 + 4], st2[:, t * 8 + 2: t * 8 + 3],
                1.0, None, ALU.add))
            v_inv = p.emit("vector", lambda e, t=t: e.reciprocal(
                st2[:, t * 8 + 4: t * 8 + 5], st2[:, t * 8 + 3: t * 8 + 4]))
            p.emit("vector", lambda e, t=t: e.tensor_scalar(
                tmp8b[:, t * 8:(t + 1) * 8], esb[:, t * 8:(t + 1) * 8],
                st2[:, t * 8 + 2: t * 8 + 3], None, ALU.is_ge),
                waits=[("dv", v_m2)])
            p.emit("vector", lambda e, t=t: e.tensor_tensor(
                tmp8b[:, t * 8:(t + 1) * 8], tmp8b[:, t * 8:(t + 1) * 8],
                esb[:, t * 8:(t + 1) * 8], ALU.mult))
            p.emit("vector", lambda e, t=t: e.tensor_scalar(
                c_buf[:, t * 8:(t + 1) * 8], tmp8b[:, t * 8:(t + 1) * 8],
                st2[:, t * 8 + 4: t * 8 + 5], 0.5, ALU.mult, ALU.mult),
                waits=[("dv", v_inv)])

    def ffn_phase(g, uT_ready, pair_ctr=[0]):
        fin_tiles = [0] * TT
        for ex in range(10):
            shared = ex >= 8
            hf = ex - 8
            gsrc = "sg" if shared else "gw"
            usrc = "su" if shared else "uw"
            bbase = (hf if shared else ex) * KI
            for m in range(KI):
                gj = (ex * KI + m) % 2
                wg = wdma("g", gj, gsrc, bbase + m)
                wu = wdma("u", gj, usrc, bbase + m)
                pr = pair_ctr[0] % 3
                pair_ctr[0] += 1
                pu = pair_ctr[0] % 3
                pair_ctr[0] += 1
                bG = (2 * pr, 2 * pr + 1)
                bU = (2 * pu, 2 * pu + 1)
                gw0, gw1 = bank_guard[bG[0]], bank_guard[bG[1]]
                p_G = None
                for k in range(KH):
                    for h in range(2):
                        last = (k == KH - 1 and h == 1)
                        p_G = p.emit("tensor", lambda e, k=k, h=h, gj=gj, bG=bG: e.matmul(
                            psb[bG[h]][:, 0:HGT], a_view("g", gj, k * 128, 128),
                            uT[:, k * GT + h * HGT: k * GT + (h + 1) * HGT],
                            start=(k == 0), stop=(k == KH - 1)),
                            waits=[(f"ld_g{gj}", wg), uT_ready[0], uT_ready[1], gw0, gw1],
                            inc="pe" if last else None)
                gw0, gw1 = bank_guard[bU[0]], bank_guard[bU[1]]
                p_U = None
                for k in range(KH):
                    for h in range(2):
                        last = (k == KH - 1 and h == 1)
                        p_U = p.emit("tensor", lambda e, k=k, h=h, gj=gj, bU=bU: e.matmul(
                            psb[bU[h]][:, 0:HGT], a_view("u", gj, k * 128, 128),
                            uT[:, k * GT + h * HGT: k * GT + (h + 1) * HGT],
                            start=(k == 0), stop=(k == KH - 1)),
                            waits=[(f"ld_u{gj}", wu), gw0, gw1],
                            inc="pe" if last else None)
                buf_lastuse[("g", gj)] = p_G
                buf_lastuse[("u", gj)] = p_U
                for h in range(2):
                    slot = (m % 2) * 1024 + h * 512
                    tsw = tsil_guard[m % 2]
                    sw = [("pe", p_G), ("dv", tsw), uT_ready[0], uT_ready[1], ("ld_y", CONSTS)]
                    if shared:
                        col = hf * KI + m
                        a_sil = p.emit("scalar", lambda e, h=h, slot=slot, bG=bG, col=col: e.activation(
                            tsil[:, slot: slot + HGT], psb[bG[h]][:, 0:HGT], AF.Silu,
                            bias=bsg_sb[:, col: col + 1], scale=1.0), waits=sw)
                    else:
                        a_sil = p.emit("scalar", lambda e, h=h, slot=slot, bG=bG: e.activation(
                            tsil[:, slot: slot + HGT], psb[bG[h]][:, 0:HGT], AF.Silu),
                            waits=sw)
                    bank_guard[bG[h]] = ("ac", a_sil)
                    hv = m * GT + h * HGT
                    if shared:
                        col = hf * KI + m
                        dvv = p.emit("vector", lambda e, h=h, slot=slot, bU=bU, hv=hv, col=col: e.scalar_tensor_tensor(
                            arena[:, off_h + hv: off_h + hv + HGT], psb[bU[h]][:, 0:HGT],
                            bsu_sb[:, col: col + 1], tsil[:, slot: slot + HGT],
                            ALU.add, ALU.mult),
                            waits=[("pe", p_U), ("ac", a_sil), ("ld_y", CONSTS)])
                    else:
                        dvv = p.emit("vector", lambda e, h=h, slot=slot, bU=bU, hv=hv: e.tensor_tensor(
                            arena[:, off_h + hv: off_h + hv + HGT], psb[bU[h]][:, 0:HGT],
                            tsil[:, slot: slot + HGT], ALU.mult),
                            waits=[("pe", p_U), ("ac", a_sil)])
                    bank_guard[bU[h]] = ("dv", dvv)
                    tsil_guard[m % 2] = dvv
            hbuf_ready = p.cnt["dv"]
            for n_ in range(4):
                dj = (ex * 4 + n_) % 2
                wd = wdma("d", dj, "sd" if shared else "dw",
                          (hf if shared else ex) * 4 + n_)
                p_dn_last = 0
                for t in range(TT):
                    pr = pair_ctr[0] % 3
                    pair_ctr[0] += 1
                    bks = (2 * pr, 2 * pr + 1)
                    gw0, gw1 = bank_guard[bks[0]], bank_guard[bks[1]]
                    p_dn = None
                    for k in range(KI):
                        for h in range(2):
                            last = (k == KI - 1 and h == 1)
                            p_dn = p.emit("tensor", lambda e, k=k, h=h, t=t, dj=dj, bks=bks: e.matmul(
                                psb[bks[h]][:, 0:256],
                                arena[:, off_h + k * GT + t * 128: off_h + k * GT + t * 128 + 128],
                                a_view("d", dj, k * 512 + h * 256, 256),
                                start=(k == 0), stop=(k == KI - 1)),
                                waits=[(f"ld_d{dj}", wd), ("dv", hbuf_ready), gw0, gw1],
                                inc="pe" if last else None)
                    p_dn_last = p_dn
                    for h in range(2):
                        rcol = t * H + n_ * 512 + h * 256
                        if shared:
                            dvv = p.emit("vector", lambda e, h=h, rcol=rcol, bks=bks: e.scalar_tensor_tensor(
                                rbuf[:, rcol: rcol + 256], psb[bks[h]][:, 0:256],
                                0.5, rbuf[:, rcol: rcol + 256], ALU.mult, ALU.add),
                                waits=[("pe", p_dn)])
                        else:
                            dvv = p.emit("vector", lambda e, h=h, rcol=rcol, bks=bks, ex=ex, t=t: e.scalar_tensor_tensor(
                                rbuf[:, rcol: rcol + 256], psb[bks[h]][:, 0:256],
                                c_buf[:, t * 8 + ex: t * 8 + ex + 1],
                                rbuf[:, rcol: rcol + 256], ALU.mult, ALU.add),
                                waits=[("pe", p_dn)])
                        bank_guard[bks[h]] = ("dv", dvv)
                    if ex == 9:
                        rc2 = t * H + n_ * 512
                        fin_tiles[t] = p.emit("vector", lambda e, rc2=rc2, n_=n_: e.scalar_tensor_tensor(
                            rbuf[:, rc2: rc2 + 512],
                            bsdb_sb[:, n_ * 512: n_ * 512 + 512], 0.5,
                            rbuf[:, rc2: rc2 + 512], ALU.mult, ALU.add),
                            waits=[("ld_y", CONSTS)])
                buf_lastuse[("d", dj)] = p_dn_last
        return fin_tiles

    def attn_phase(g, uT_ready):
        for sg_i in range(2):
            for m in range(32):
                wj = m % 3
                wv = wdma("wi", wj, "wi", m)
                pr = m % 3
                bks = (2 * pr, 2 * pr + 1)
                gg = [bank_guard[bks[0]], bank_guard[bks[1]]]
                p_xt = None
                for k in range(KH):
                    for q in range(2):
                        last = (k == KH - 1 and q == 1)
                        p_xt = p.emit("tensor", lambda e, k=k, q=q, wj=wj, bks=bks, sg_i=sg_i: e.matmul(
                            psb[bks[q]][:, 0:NQ],
                            a_view("wi", wj, k * 128, 128),
                            uT[:, k * GT + sg_i * SGT + q * NQ: k * GT + sg_i * SGT + (q + 1) * NQ],
                            start=(k == 0), stop=(k == KH - 1)),
                            waits=[(f"ld_wi{wj}", wv), uT_ready[0], uT_ready[1], gg[0], gg[1]],
                            inc="pe" if last else None)
                buf_lastuse[("wi", wj)] = p_xt
                for q in range(2):
                    gv = p.emit("vector", lambda e, m=m, q=q, bks=bks: e.tensor_copy(
                        arena[:, off_xT + m * SGT + q * NQ: off_xT + m * SGT + (q + 1) * NQ],
                        psb[bks[q]][:, 0:NQ]),
                        waits=[("pe", p_xt)])
                    bank_guard[bks[q]] = ("dv", gv)
            xT_ready = p.cnt["dv"]
            pB_last = 0
            for m in range(32):
                cj = m % 2
                zj = m % 3
                wc = wdma("cvb", cj, "cv", m)
                wz = wdma("wi", zj, "wi", 32 + m)
                zpr = (2 * m) % 3
                cpr = (2 * m + 1) % 3
                bz = (2 * zpr, 2 * zpr + 1)
                bc = (2 * cpr, 2 * cpr + 1)
                gg = [bank_guard[bz[0]], bank_guard[bz[1]]]
                p_z = None
                for k in range(KH):
                    for q in range(2):
                        last = (k == KH - 1 and q == 1)
                        p_z = p.emit("tensor", lambda e, k=k, q=q, zj=zj, bz=bz, sg_i=sg_i: e.matmul(
                            psb[bz[q]][:, 0:NQ],
                            a_view("wi", zj, k * 128, 128),
                            uT[:, k * GT + sg_i * SGT + q * NQ: k * GT + sg_i * SGT + (q + 1) * NQ],
                            start=(k == 0), stop=(k == KH - 1)),
                            waits=[(f"ld_wi{zj}", wz), gg[0], gg[1]],
                            inc="pe" if last else None)
                buf_lastuse[("wi", zj)] = p_z
                gg = [bank_guard[bc[0]], bank_guard[bc[1]]]
                p_xc = None
                for k in range(32):
                    for q in range(2):
                        last = (k == 31 and q == 1)
                        p_xc = p.emit("tensor", lambda e, k=k, q=q, cj=cj, bc=bc: e.matmul(
                            psb[bc[q]][:, 0:NQ],
                            a_view("cvb", cj, k * 128, 128),
                            arena[:, off_xT + k * SGT + q * NQ: off_xT + k * SGT + (q + 1) * NQ],
                            start=(k == 0), stop=(k == 31)),
                            waits=[(f"ld_cvb{cj}", wc), ("dv", xT_ready), gg[0], gg[1]],
                            inc="pe" if last else None)
                buf_lastuse[("cvb", cj)] = p_xc
                pB_last = p_xc
                slot = (m % 2) * SGT
                for q in range(2):
                    tsw = tsil_guard[m % 2]
                    a_z = p.emit("scalar", lambda e, slot=slot, q=q, bz=bz: e.activation(
                        tsil[:, slot + q * NQ: slot + (q + 1) * NQ],
                        psb[bz[q]][:, 0:NQ], AF.Silu),
                        waits=[("pe", p_z), ("dv", tsw), uT_ready[0], uT_ready[1]])
                    bank_guard[bz[q]] = ("ac", a_z)
                    gv = p.emit("vector", lambda e, m=m, slot=slot, q=q, bc=bc: e.scalar_tensor_tensor(
                        arena[:, off_sb_ + m * SGT + q * NQ: off_sb_ + m * SGT + (q + 1) * NQ],
                        psb[bc[q]][:, 0:NQ], dvec_sb[:, m: m + 1],
                        tsil[:, slot + q * NQ: slot + (q + 1) * NQ], ALU.mult, ALU.mult),
                        waits=[("pe", p_xc), ("ac", a_z), ("ld_y", CONSTS)])
                    bank_guard[bc[q]] = ("dv", gv)
                    tsil_guard[m % 2] = gv
            sb_ready = p.cnt["dv"]
            for n in range(4):
                wvs = []
                for kh in range(2):
                    wj = (n * 2 + kh) % 3
                    wvs.append((wj, wdma("wob", wj, "wo", n * 2 + kh,
                                         extra_pe_guard=pB_last)))
                for t in range(STT):
                    pc = (n * STT + t) % 3
                    bks = (2 * pc, 2 * pc + 1)
                    gg = [bank_guard[bks[0]], bank_guard[bks[1]]]
                    p_o = None
                    for k in range(32):
                        kh = k // 16
                        wj, wv = wvs[kh]
                        for q in range(2):
                            last = (k == 31 and q == 1)
                            p_o = p.emit("tensor", lambda e, k=k, q=q, t=t, wj=wj, bks=bks: e.matmul(
                                psb[bks[q]][:, 0:256],
                                arena[:, off_sb_ + k * SGT + t * 128: off_sb_ + k * SGT + t * 128 + 128],
                                a_view("wob", wj, (k % 16) * 512 + q * 256, 256),
                                start=(k == 0), stop=(k == 31)),
                                waits=[(f"ld_wob{wj}", wv), ("dv", sb_ready), gg[0], gg[1]],
                                inc="pe" if last else None)
                    buf_lastuse[("wob", wvs[0][0])] = p_o
                    buf_lastuse[("wob", wvs[1][0])] = p_o
                    rcol = (sg_i * STT + t) * H + n * 512
                    for q in range(2):
                        gv = p.emit("vector", lambda e, rcol=rcol, q=q, bks=bks: e.tensor_tensor(
                            rbuf[:, rcol + q * 256: rcol + (q + 1) * 256],
                            psb[bks[q]][:, 0:256],
                            rbuf[:, rcol + q * 256: rcol + (q + 1) * 256], ALU.add),
                            waits=[("pe", p_o)])
                        bank_guard[bks[q]] = ("dv", gv)
            af_sg = p.cnt["pe"]
            for key in (("wi", 0), ("wi", 1), ("wi", 2), ("cvb", 0), ("cvb", 1)):
                buf_lastuse[key] = max(buf_lastuse[key], af_sg)
        af = p.cnt["pe"]
        for key in (("g", 0), ("g", 1), ("u", 0), ("u", 1), ("d", 0), ("d", 1)):
            buf_lastuse[key] = max(buf_lastuse[key], af)

    st_cnt = 0
    fin_tiles = [0] * TT
    for g in range(NG):
        NPIECE = min(4, TT)
        for piece in range(NPIECE):
            waits = []
            if g > 0 and piece == 0:
                waits.append(("st", st_cnt))
            if piece > 0:
                waits.append(("ld_x", (g * NPIECE + piece) * 16))
            p.emit("sync", lambda e, g=g, piece=piece, NPIECE=NPIECE: e.dma_start(
                out=rbuf[:, piece * (TT // NPIECE) * H:(piece + 1) * (TT // NPIECE) * H]
                    .rearrange("q (t h) -> q t h", h=H),
                in_=ap["x"][g * GT + piece * (GT // NPIECE): g * GT + (piece + 1) * (GT // NPIECE), :]
                    .rearrange("(t q) h -> q t h", q=128)),
                waits=waits, inc="ld_x", amt=16)
        for sub in range(3):
            SUBIDX[0] = sub
            uT_ready = ln_phase(g)
            if sub in (0, 2):
                router_phase(uT_ready)
                fin_tiles = ffn_phase(g, uT_ready)
            else:
                attn_phase(g, uT_ready)
        for piece in range(2):
            fin = max(fin_tiles[piece * (TT // 2): (piece + 1) * (TT // 2)])
            st_cnt = p.emit("sync", lambda e, g=g, piece=piece: e.dma_start(
                out=out_ap[g * GT + piece * (GT // 2): g * GT + (piece + 1) * (GT // 2), :]
                    .rearrange("(t q) h -> q t h", q=128),
                in_=rbuf[:, piece * (TT // 2) * H:(piece + 1) * (TT // 2) * H]
                    .rearrange("q (t h) -> q t h", h=H)),
                waits=[("dv", fin)], inc="st", amt=16)
    p.emit("sync", None, waits=[("st", st_cnt)])

    with es:
        block = es.enter_context(nc.Block())
        for eng in Plan.ENGINES:
            ops = p.ops[eng]
            if not ops:
                continue
            def body(e, ops=ops):
                hwm = {}
                for waits, fn, inc, amt in ops:
                    for s, v in waits:
                        if hwm.get(s, 0) < v:
                            e.wait_ge(sems[s], v)
                            hwm[s] = v
                    if fn is not None:
                        ins = fn(e)
                        if inc is not None:
                            ins.then_inc(sems[inc], amt)
            getattr(block, eng)(body)
    return nc


def pack_inputs(i):
    c = np.ascontiguousarray
    d = {}
    d["gw"] = c(np.asarray(i["L0_ffn_gate_w"]).reshape(8, 16, 128, 11, 128)
                .transpose(0, 3, 2, 1, 4).reshape(88, 128, 2048))
    d["uw"] = c(np.asarray(i["L0_ffn_up_w"]).reshape(8, 16, 128, 11, 128)
                .transpose(0, 3, 2, 1, 4).reshape(88, 128, 2048))
    d["dw"] = c(np.asarray(i["L0_ffn_down_w"]).reshape(8, 11, 128, 4, 512)
                .transpose(0, 3, 2, 1, 4).reshape(32, 128, 5632))
    d["sg"] = c(np.asarray(i["L0_ffn_s_gate_w"]).reshape(16, 128, 2, 11, 128)
                .transpose(2, 3, 1, 0, 4).reshape(22, 128, 2048))
    d["su"] = c(np.asarray(i["L0_ffn_s_up_w"]).reshape(16, 128, 2, 11, 128)
                .transpose(2, 3, 1, 0, 4).reshape(22, 128, 2048))
    d["sd"] = c(np.asarray(i["L0_ffn_s_down_w"]).reshape(2, 11, 128, 4, 512)
                .transpose(0, 3, 2, 1, 4).reshape(8, 128, 5632))
    d["wi"] = c(np.asarray(i["L0_attn_W_in"]).reshape(64, 128, 16, 128)
                .transpose(0, 3, 2, 1).reshape(64, 128, 2048))
    d["cv"] = c(np.asarray(i["L0_attn_conv_w"]).reshape(32, 128, 32, 128)
                .transpose(0, 3, 2, 1).reshape(32, 128, 4096))
    d["wo"] = c(np.asarray(i["L0_attn_W_out"]).reshape(4, 512, 2, 16, 128)
                .transpose(0, 2, 4, 3, 1).reshape(8, 128, 8192))
    d["rw"] = c(np.asarray(i["L0_ffn_router_w"]).reshape(8, 16, 128)
                .transpose(2, 1, 0).reshape(128, 128))
    d["dvec"] = c(np.asarray(i["L0_attn_D"]).reshape(32, 128).T)
    d["bsg"] = c(np.asarray(i["L0_ffn_s_gate_b"]).reshape(22, 128).T)
    d["bsu"] = c(np.asarray(i["L0_ffn_s_up_b"]).reshape(22, 128).T)
    d["bsdb"] = c(np.broadcast_to(np.asarray(i["L0_ffn_s_down_b"]), (128, 2048)))
    d["idn"] = np.eye(128, dtype=BFNP)
    return d


def kernel(**inputs):
    shared = pack_inputs(inputs)
    x = np.ascontiguousarray(np.asarray(inputs["L0_attn_hidden_states"],
                                        dtype=np.float32).reshape(16384, 2048))
    nc = build_program(GT=1024, TOKC=2048)
    in_maps = []
    for cidx in range(N_CORES):
        m = dict(shared)
        m["x"] = np.ascontiguousarray(x[cidx * 2048:(cidx + 1) * 2048])
        in_maps.append(m)
    res = run_bass_kernel_spmd(nc, in_maps, core_ids=list(range(N_CORES)))
    out = np.concatenate([np.asarray(r["out"]) for r in res.results], axis=0)
    return out.reshape(1, 16384, 2048).astype(np.float32)


# version 05
# speedup vs baseline: 1.2173x; 1.1192x over previous
import numpy as np
import ml_dtypes
from contextlib import ExitStack

import concourse.bass as bass
import concourse.mybir as mybir
from concourse.bass_utils import run_bass_kernel_spmd

F32 = mybir.dt.float32
BF16 = mybir.dt.bfloat16
AF = mybir.ActivationFunctionType
ALU = mybir.AluOpType
AX = mybir.AxisListType
BFNP = ml_dtypes.bfloat16

H = 2048
KH = H // 128
I_EXP = 1408
KI = I_EXP // 128
N_CORES = 8


class Plan:
    ENGINES = ("tensor", "vector", "scalar", "gpsimd", "sync")
    SELF = {"vector": "dv", "scalar": "ac"}

    def __init__(self):
        self.ops = {e: [] for e in self.ENGINES}
        self.cnt = {}

    def sem(self, name):
        self.cnt.setdefault(name, 0)

    def emit(self, eng, fn, waits=(), inc=None, amt=1):
        if eng in self.SELF and fn is not None and (inc is None or inc == self.SELF[eng]):
            s = self.SELF[eng]
            waits = tuple(waits) + ((s, self.cnt[s]),)
            if inc is None:
                inc, amt = s, 1
        for s, v in waits:
            assert s in self.cnt, s
            assert v <= self.cnt[s], ("wait ahead of plan", s, v, self.cnt[s])
        self.ops[eng].append((tuple(waits), fn, inc, amt))
        if inc is not None:
            self.cnt[inc] += amt
            return self.cnt[inc]
        return None


def build_program(GT=1024, TOKC=2048):
    NG = TOKC // GT
    TT = GT // 128
    SGT = GT // 2
    STT = TT // 2
    NQ = GT // 4
    HGT = GT // 2

    nc = bass.Bass("TRN2", target_bir_lowering=False)
    ap = {}
    def din(name, shape, dt=F32):
        ap[name] = nc.dram_tensor(name, shape, dt, kind="ExternalInput").ap()
    din("x", [TOKC, H])
    din("gw", [88, 128, KH * 128])
    din("uw", [88, 128, KH * 128])
    din("dw", [32, 128, KI * 512])
    din("sg", [22, 128, KH * 128])
    din("su", [22, 128, KH * 128])
    din("sd", [8, 128, KI * 512])
    din("wi", [64, 128, KH * 128])
    din("cv", [32, 128, 32 * 128])
    din("wo", [8, 128, 16 * 512])
    din("rw", [128, 128])
    din("dvec", [128, 32])
    din("bsg", [128, 22])
    din("bsu", [128, 22])
    din("bsdb", [128, H])
    din("idn", [128, 128], BF16)
    out_ap = nc.dram_tensor("out", [TOKC, H], F32, kind="ExternalOutput").ap()
    scr = {}
    scr["wi"] = nc.dram_tensor("scr_wi", [64, 128, KH * 128], BF16, kind="Internal").ap()
    scr["cv"] = nc.dram_tensor("scr_cv", [32, 128, 32 * 128], BF16, kind="Internal").ap()
    scr["wo"] = nc.dram_tensor("scr_wo", [8, 128, 16 * 512], BF16, kind="Internal").ap()

    p = Plan()
    for s in ("pe", "dv", "ac", "st", "ld_y", "ld_x", "ld_c", "scr_c"):
        p.sem(s)
    BUFN = (("g", 2), ("u", 2), ("d", 2), ("wi", 3), ("cvb", 2), ("wob", 3))
    for pref, n in BUFN:
        for i in range(n):
            p.sem(f"ld_{pref}{i}")

    es = ExitStack()
    def sbuf(name, cols, dt):
        return es.enter_context(nc.sbuf_tensor(name, [128, cols], dt))

    rbuf = sbuf("rbuf", TT * H, F32)
    uT = sbuf("uT", KH * GT, BF16)
    ubf = sbuf("ubf", H, BF16)
    tsil = sbuf("tsil", 2048, BF16)
    bsdb_sb = sbuf("bsdb_sb", H, F32)
    stats = sbuf("stats", 8 * TT, F32)
    lgsb = sbuf("lgsb", 8 * TT, F32)
    esb = sbuf("esb", 8 * TT, F32)
    tmp8 = sbuf("tmp8", 8 * TT, F32)
    tmp8b = sbuf("tmp8b", 8 * TT, F32)
    st2 = sbuf("st2", 8 * TT, F32)
    c_buf = sbuf("c_buf", 8 * TT, F32)
    rw_bf = sbuf("rw_bf", 128, BF16)
    idn_sb = sbuf("idn_sb", 128, BF16)
    dvec_sb = sbuf("dvec_sb", 32, F32)
    bsg_sb = sbuf("bsg_sb", 22, F32)
    bsu_sb = sbuf("bsu_sb", 22, F32)

    off_h = 0
    sz_h = KI * GT
    off_g = [sz_h + j * 2048 for j in range(2)]
    off_u = [sz_h + 2 * 2048 + j * 2048 for j in range(2)]
    off_d = [sz_h + 4 * 2048 + j * (KI * 512) for j in range(2)]
    ffn_end = sz_h + 4 * 2048 + 2 * KI * 512
    off_xT = 0
    off_sb_ = 32 * SGT
    off_wi = [max(ffn_end, 64 * SGT) + j * 2048 for j in range(3)]
    off_cv = [off_wi[0] + 3 * 2048 + j * 4096 for j in range(2)]
    attn_end = off_cv[0] + 2 * 4096
    if GT == 1024:
        off_wo = [0, 8192, off_wi[0]]
        arena_cols = attn_end
    else:
        off_wo = [attn_end + j * 8192 for j in range(3)]
        arena_cols = attn_end + 3 * 8192
    arena = sbuf("arena", arena_cols, BF16)

    psb = [es.enter_context(nc.psum_tensor(f"pb{i}", [128, 512], F32)) for i in range(6)]
    ptb = [es.enter_context(nc.psum_tensor(f"pt{i}", [128, 1024], BF16)) for i in range(2)]
    sems = {}
    for name in p.cnt:
        sems[name] = es.enter_context(nc.semaphore(name))

    bank_guard = {i: ("dv", 0) for i in range(6)}
    ptb_guard = [("dv", 0), ("dv", 0)]
    buf_lastuse = {}
    buf_gen = {}
    for pref, n in BUFN:
        for j in range(n):
            buf_lastuse[(pref, j)] = 0
            buf_gen[(pref, j)] = 0
    tsil_guard = {0: 0, 1: 0}
    CONSTS = 5 * 16
    SCRT = [0]

    p.emit("sync", lambda e: e.dma_start(out=dvec_sb[:, :], in_=ap["dvec"]), inc="ld_y", amt=16)
    p.emit("sync", lambda e: e.dma_start(out=bsg_sb[:, :], in_=ap["bsg"]), inc="ld_y", amt=16)
    p.emit("sync", lambda e: e.dma_start(out=bsu_sb[:, :], in_=ap["bsu"]), inc="ld_y", amt=16)
    p.emit("sync", lambda e: e.dma_start(out=bsdb_sb[:, :], in_=ap["bsdb"]), inc="ld_y", amt=16)
    p.emit("sync", lambda e: e.dma_start(out=idn_sb[:, :], in_=ap["idn"]), inc="ld_y", amt=16)
    p.emit("gpsimd", lambda e: e.dma_start(out=rw_bf[:, :], in_=ap["rw"]), inc="ld_c", amt=16)
    scr_jobs = []
    for name, nblk, step in (("wi", 64, 2), ("cv", 32, 1), ("wo", 8, 1)):
        for a in range(0, nblk, step):
            scr_jobs.append((name, a, min(a + step, nblk)))
    SCRT[0] = len(scr_jobs) * 16

    def emit_scr_piece():
        if scr_jobs:
            name, a, bnd = scr_jobs.pop(0)
            p.emit("gpsimd", lambda e, name=name, a=a, bnd=bnd: e.dma_start(
                out=scr[name][a:bnd], in_=ap[name][a:bnd]), inc="scr_c", amt=16)

    WQ = {"g": "gpsimd", "u": "gpsimd", "d": "gpsimd",
          "wi": "sync", "cvb": "sync", "wob": "sync"}
    WSRC_SCR = {"wi": "wi", "cvb": "cv", "wob": "wo"}
    OFFS = {"g": off_g, "u": off_u, "d": off_d, "wi": off_wi, "cvb": off_cv, "wob": off_wo}
    NCOLS = {"g": 2048, "u": 2048, "d": KI * 512, "wi": 2048, "cvb": 4096, "wob": 8192}

    def wdma(stream, j, src_name, blk, extra_pe_guard=0):
        gen = buf_gen[(stream, j)] = buf_gen[(stream, j)] + 1
        guard = max(buf_lastuse[(stream, j)], extra_pe_guard)
        o = OFFS[stream][j]
        ncols = NCOLS[stream]
        waits = []
        if guard:
            waits.append(("pe", guard))
        if stream in WSRC_SCR:
            waits.append(("scr_c", SCRT[0]))
            srcap = scr[WSRC_SCR[stream]]
        else:
            srcap = ap[src_name]
        p.emit(WQ[stream],
               lambda e, o=o, n=ncols, s=srcap, b=blk: e.dma_start(
                   out=arena[:, o:o + n], in_=s[b]),
               waits=waits, inc=f"ld_{stream}{j}", amt=16)
        return gen * 16

    def a_view(stream, j, lo, n):
        o = OFFS[stream][j]
        return arena[:, o + lo: o + lo + n]

    SUBIDX = [0]
    lnbuf = [ubf, tsil]
    lnbuf_tr_guard = [0, 0]
    def ln_phase(g):
        acg = p.cnt["ac"]
        dvg = p.cnt["dv"]
        NPIECE = min(4, TT)
        TPP = TT // NPIECE
        for t in range(TT):
            par = t % 2
            buf = lnbuf[par]
            w0 = [("dv", dvg), ("pe", lnbuf_tr_guard[par])]
            if SUBIDX[0] == 0:
                w0.append(("ld_x", (g * NPIECE + t // TPP + 1) * 16))
            a_sum = p.emit("scalar", lambda e, t=t, buf=buf: e.activation(
                buf[:, :], rbuf[:, t * H:(t + 1) * H], AF.Copy,
                accum_out=stats[:, t * 8: t * 8 + 1]), waits=w0)
            v_negm = p.emit("vector", lambda e, t=t: e.tensor_scalar(
                stats[:, t * 8 + 1: t * 8 + 2], stats[:, t * 8: t * 8 + 1],
                -1.0 / H, None, ALU.mult), waits=[("ac", a_sum)])
            a_sq = p.emit("scalar", lambda e, t=t, buf=buf: e.activation(
                buf[:, :], rbuf[:, t * H:(t + 1) * H], AF.Square,
                bias=stats[:, t * 8 + 1: t * 8 + 2], scale=1.0,
                accum_out=stats[:, t * 8 + 2: t * 8 + 3]),
                waits=[("dv", v_negm)])
            v_var = p.emit("vector", lambda e, t=t: e.tensor_scalar(
                stats[:, t * 8 + 3: t * 8 + 4], stats[:, t * 8 + 2: t * 8 + 3],
                1.0 / H, 1e-5, ALU.mult, ALU.add), waits=[("ac", a_sq)])
            a_sd = p.emit("scalar", lambda e, t=t: e.activation(
                stats[:, t * 8 + 4: t * 8 + 5], stats[:, t * 8 + 3: t * 8 + 4],
                AF.Sqrt), waits=[("dv", v_var)])
            p.emit("vector", lambda e, t=t: e.reciprocal(
                stats[:, t * 8 + 5: t * 8 + 6], stats[:, t * 8 + 4: t * 8 + 5]),
                waits=[("ac", a_sd)])
            v_ubf = p.emit("vector", lambda e, t=t, buf=buf: e.tensor_scalar(
                buf[:, :], rbuf[:, t * H:(t + 1) * H], stats[:, t * 8 + 1: t * 8 + 2],
                stats[:, t * 8 + 5: t * 8 + 6], ALU.add, ALU.mult),
                waits=[("pe", lnbuf_tr_guard[par]), ("ac", acg)])
            p_tr = [0, 0]
            for half in range(2):
                s0, v0 = ptb_guard[half]
                for c in range(8):
                    cc = half * 8 + c
                    p_tr[half] = p.emit("tensor", lambda e, cc=cc, half=half, c=c, buf=buf: e.transpose(
                        ptb[half][:, c * 128:(c + 1) * 128],
                        buf[:, cc * 128:(cc + 1) * 128], idn_sb[:, :]),
                        waits=[("dv", v_ubf), (s0, v0), ("ld_y", CONSTS)],
                        inc="pe" if c == 7 else None)
            for half in range(2):
                gv = p.emit("vector", lambda e, half=half, t=t: e.tensor_copy(
                    uT[:, :].rearrange("z (c g) -> z c g", g=GT)[:, half * 8:half * 8 + 8,
                                                                t * 128:t * 128 + 128],
                    ptb[half][:, :].rearrange("z (c g) -> z c g", g=128)),
                    waits=[("pe", p_tr[half])])
                ptb_guard[half] = ("dv", gv)
            lnbuf_tr_guard[par] = p_tr[1]
            if par == 1:
                tsil_guard[0] = max(tsil_guard[0], v_ubf)
                tsil_guard[1] = max(tsil_guard[1], v_ubf)
        return (("pe", p.cnt["pe"]), ("dv", p.cnt["dv"]))

    def router_phase(uT_ready):
        for t in range(TT):
            bk = t % 2
            s0, v0 = bank_guard[bk]
            ins_l = None
            for c in range(KH):
                ins_l = p.emit("tensor", lambda e, c=c, t=t, bk=bk: e.matmul(
                    psb[bk][:, 0:8], uT[:, c * GT + t * 128: c * GT + t * 128 + 128],
                    rw_bf[:, c * 8:(c + 1) * 8], start=(c == 0), stop=(c == KH - 1)),
                    waits=[uT_ready[0], uT_ready[1], (s0, v0), ("ld_c", 16)],
                    inc="pe" if c == KH - 1 else None)
            p_lg = ins_l
            gv = p.emit("vector", lambda e, t=t, bk=bk: e.tensor_copy(
                lgsb[:, t * 8:(t + 1) * 8], psb[bk][:, 0:8]),
                waits=[("pe", p_lg)])
            bank_guard[bk] = ("dv", gv)
            p.emit("vector", lambda e, t=t: e.reduce_max(
                st2[:, t * 8: t * 8 + 1], lgsb[:, t * 8:(t + 1) * 8], axis=AX.X))
            v_ngm = p.emit("vector", lambda e, t=t: e.tensor_scalar(
                st2[:, t * 8 + 1: t * 8 + 2], st2[:, t * 8: t * 8 + 1],
                -1.0, None, ALU.mult))
            a_exp = p.emit("scalar", lambda e, t=t: e.activation(
                esb[:, t * 8:(t + 1) * 8], lgsb[:, t * 8:(t + 1) * 8], AF.Exp,
                bias=st2[:, t * 8 + 1: t * 8 + 2], scale=1.0),
                waits=[("dv", v_ngm)])
            p.emit("vector", lambda e, t=t: e.tensor_scalar(
                tmp8[:, t * 8:(t + 1) * 8], esb[:, t * 8:(t + 1) * 8],
                1.0, -2.0, ALU.is_ge, ALU.mult), waits=[("ac", a_exp)])
            p.emit("vector", lambda e, t=t: e.tensor_tensor(
                tmp8[:, t * 8:(t + 1) * 8], tmp8[:, t * 8:(t + 1) * 8],
                esb[:, t * 8:(t + 1) * 8], ALU.add))
            v_m2 = p.emit("vector", lambda e, t=t: e.reduce_max(
                st2[:, t * 8 + 2: t * 8 + 3], tmp8[:, t * 8:(t + 1) * 8], axis=AX.X))
            p.emit("vector", lambda e, t=t: e.tensor_scalar(
                st2[:, t * 8 + 3: t * 8 + 4], st2[:, t * 8 + 2: t * 8 + 3],
                1.0, None, ALU.add))
            v_inv = p.emit("vector", lambda e, t=t: e.reciprocal(
                st2[:, t * 8 + 4: t * 8 + 5], st2[:, t * 8 + 3: t * 8 + 4]))
            p.emit("vector", lambda e, t=t: e.tensor_scalar(
                tmp8b[:, t * 8:(t + 1) * 8], esb[:, t * 8:(t + 1) * 8],
                st2[:, t * 8 + 2: t * 8 + 3], None, ALU.is_ge),
                waits=[("dv", v_m2)])
            p.emit("vector", lambda e, t=t: e.tensor_tensor(
                tmp8b[:, t * 8:(t + 1) * 8], tmp8b[:, t * 8:(t + 1) * 8],
                esb[:, t * 8:(t + 1) * 8], ALU.mult))
            p.emit("vector", lambda e, t=t: e.tensor_scalar(
                c_buf[:, t * 8:(t + 1) * 8], tmp8b[:, t * 8:(t + 1) * 8],
                st2[:, t * 8 + 4: t * 8 + 5], 0.5, ALU.mult, ALU.mult),
                waits=[("dv", v_inv)])

    def ffn_phase(g, uT_ready, pair_ctr=[0]):
        fin_tiles = [0] * TT
        for ex in range(10):
            shared = ex >= 8
            hf = ex - 8
            gsrc = "sg" if shared else "gw"
            usrc = "su" if shared else "uw"
            bbase = (hf if shared else ex) * KI
            wd_pre = {}
            for m in range(KI):
                gj = (ex * KI + m) % 2
                wg = wdma("g", gj, gsrc, bbase + m)
                wu = wdma("u", gj, usrc, bbase + m)
                emit_scr_piece()
                if m == 8 or m == 9:
                    n_ = m - 8
                    wd_pre[n_] = wdma("d", (ex * 4 + n_) % 2,
                                      "sd" if shared else "dw",
                                      (hf if shared else ex) * 4 + n_)
                pr = pair_ctr[0] % 3
                pair_ctr[0] += 1
                pu = pair_ctr[0] % 3
                pair_ctr[0] += 1
                bG = (2 * pr, 2 * pr + 1)
                bU = (2 * pu, 2 * pu + 1)
                gw0, gw1 = bank_guard[bG[0]], bank_guard[bG[1]]
                p_G = None
                for k in range(KH):
                    for h in range(2):
                        last = (k == KH - 1 and h == 1)
                        p_G = p.emit("tensor", lambda e, k=k, h=h, gj=gj, bG=bG: e.matmul(
                            psb[bG[h]][:, 0:HGT], a_view("g", gj, k * 128, 128),
                            uT[:, k * GT + h * HGT: k * GT + (h + 1) * HGT],
                            start=(k == 0), stop=(k == KH - 1)),
                            waits=[(f"ld_g{gj}", wg), uT_ready[0], uT_ready[1], gw0, gw1],
                            inc="pe" if last else None)
                gw0, gw1 = bank_guard[bU[0]], bank_guard[bU[1]]
                p_U = None
                for k in range(KH):
                    for h in range(2):
                        last = (k == KH - 1 and h == 1)
                        p_U = p.emit("tensor", lambda e, k=k, h=h, gj=gj, bU=bU: e.matmul(
                            psb[bU[h]][:, 0:HGT], a_view("u", gj, k * 128, 128),
                            uT[:, k * GT + h * HGT: k * GT + (h + 1) * HGT],
                            start=(k == 0), stop=(k == KH - 1)),
                            waits=[(f"ld_u{gj}", wu), gw0, gw1],
                            inc="pe" if last else None)
                buf_lastuse[("g", gj)] = p_G
                buf_lastuse[("u", gj)] = p_U
                for h in range(2):
                    slot = (m % 2) * 1024 + h * 512
                    tsw = tsil_guard[m % 2]
                    sw = [("pe", p_G), ("dv", tsw), uT_ready[0], uT_ready[1], ("ld_y", CONSTS)]
                    if shared:
                        col = hf * KI + m
                        a_sil = p.emit("scalar", lambda e, h=h, slot=slot, bG=bG, col=col: e.activation(
                            tsil[:, slot: slot + HGT], psb[bG[h]][:, 0:HGT], AF.Silu,
                            bias=bsg_sb[:, col: col + 1], scale=1.0), waits=sw)
                    else:
                        a_sil = p.emit("scalar", lambda e, h=h, slot=slot, bG=bG: e.activation(
                            tsil[:, slot: slot + HGT], psb[bG[h]][:, 0:HGT], AF.Silu),
                            waits=sw)
                    bank_guard[bG[h]] = ("ac", a_sil)
                    hv = m * GT + h * HGT
                    if shared:
                        col = hf * KI + m
                        dvv = p.emit("vector", lambda e, h=h, slot=slot, bU=bU, hv=hv, col=col: e.scalar_tensor_tensor(
                            arena[:, off_h + hv: off_h + hv + HGT], psb[bU[h]][:, 0:HGT],
                            bsu_sb[:, col: col + 1], tsil[:, slot: slot + HGT],
                            ALU.add, ALU.mult),
                            waits=[("pe", p_U), ("ac", a_sil), ("ld_y", CONSTS)])
                    else:
                        dvv = p.emit("vector", lambda e, h=h, slot=slot, bU=bU, hv=hv: e.tensor_tensor(
                            arena[:, off_h + hv: off_h + hv + HGT], psb[bU[h]][:, 0:HGT],
                            tsil[:, slot: slot + HGT], ALU.mult),
                            waits=[("pe", p_U), ("ac", a_sil)])
                    bank_guard[bU[h]] = ("dv", dvv)
                    tsil_guard[m % 2] = dvv
            hbuf_ready = p.cnt["dv"]
            for n_ in range(4):
                dj = (ex * 4 + n_) % 2
                if n_ in wd_pre:
                    wd = wd_pre[n_]
                else:
                    wd = wdma("d", dj, "sd" if shared else "dw",
                              (hf if shared else ex) * 4 + n_)
                p_dn_last = 0
                for t in range(TT):
                    pr = pair_ctr[0] % 3
                    pair_ctr[0] += 1
                    bks = (2 * pr, 2 * pr + 1)
                    gw0, gw1 = bank_guard[bks[0]], bank_guard[bks[1]]
                    p_dn = None
                    for k in range(KI):
                        for h in range(2):
                            last = (k == KI - 1 and h == 1)
                            p_dn = p.emit("tensor", lambda e, k=k, h=h, t=t, dj=dj, bks=bks: e.matmul(
                                psb[bks[h]][:, 0:256],
                                arena[:, off_h + k * GT + t * 128: off_h + k * GT + t * 128 + 128],
                                a_view("d", dj, k * 512 + h * 256, 256),
                                start=(k == 0), stop=(k == KI - 1)),
                                waits=[(f"ld_d{dj}", wd), ("dv", hbuf_ready), gw0, gw1],
                                inc="pe" if last else None)
                    p_dn_last = p_dn
                    for h in range(2):
                        rcol = t * H + n_ * 512 + h * 256
                        if shared:
                            dvv = p.emit("vector", lambda e, h=h, rcol=rcol, bks=bks: e.scalar_tensor_tensor(
                                rbuf[:, rcol: rcol + 256], psb[bks[h]][:, 0:256],
                                0.5, rbuf[:, rcol: rcol + 256], ALU.mult, ALU.add),
                                waits=[("pe", p_dn)])
                        else:
                            dvv = p.emit("vector", lambda e, h=h, rcol=rcol, bks=bks, ex=ex, t=t: e.scalar_tensor_tensor(
                                rbuf[:, rcol: rcol + 256], psb[bks[h]][:, 0:256],
                                c_buf[:, t * 8 + ex: t * 8 + ex + 1],
                                rbuf[:, rcol: rcol + 256], ALU.mult, ALU.add),
                                waits=[("pe", p_dn)])
                        bank_guard[bks[h]] = ("dv", dvv)
                    if ex == 9:
                        rc2 = t * H + n_ * 512
                        fin_tiles[t] = p.emit("vector", lambda e, rc2=rc2, n_=n_: e.scalar_tensor_tensor(
                            rbuf[:, rc2: rc2 + 512],
                            bsdb_sb[:, n_ * 512: n_ * 512 + 512], 0.5,
                            rbuf[:, rc2: rc2 + 512], ALU.mult, ALU.add),
                            waits=[("ld_y", CONSTS)])
                buf_lastuse[("d", dj)] = p_dn_last
        return fin_tiles

    def attn_phase(g, uT_ready):
        for sg_i in range(2):
            for m in range(32):
                wj = m % 3
                wv = wdma("wi", wj, "wi", m)
                pr = m % 3
                bks = (2 * pr, 2 * pr + 1)
                gg = [bank_guard[bks[0]], bank_guard[bks[1]]]
                p_xt = None
                for k in range(KH):
                    for q in range(2):
                        last = (k == KH - 1 and q == 1)
                        p_xt = p.emit("tensor", lambda e, k=k, q=q, wj=wj, bks=bks, sg_i=sg_i: e.matmul(
                            psb[bks[q]][:, 0:NQ],
                            a_view("wi", wj, k * 128, 128),
                            uT[:, k * GT + sg_i * SGT + q * NQ: k * GT + sg_i * SGT + (q + 1) * NQ],
                            start=(k == 0), stop=(k == KH - 1)),
                            waits=[(f"ld_wi{wj}", wv), uT_ready[0], uT_ready[1], gg[0], gg[1]],
                            inc="pe" if last else None)
                buf_lastuse[("wi", wj)] = p_xt
                for q in range(2):
                    gv = p.emit("vector", lambda e, m=m, q=q, bks=bks: e.tensor_copy(
                        arena[:, off_xT + m * SGT + q * NQ: off_xT + m * SGT + (q + 1) * NQ],
                        psb[bks[q]][:, 0:NQ]),
                        waits=[("pe", p_xt)])
                    bank_guard[bks[q]] = ("dv", gv)
            xT_ready = p.cnt["dv"]
            pB_last = 0
            for m in range(32):
                cj = m % 2
                zj = m % 3
                wc = wdma("cvb", cj, "cv", m)
                wz = wdma("wi", zj, "wi", 32 + m)
                zpr = (2 * m) % 3
                cpr = (2 * m + 1) % 3
                bz = (2 * zpr, 2 * zpr + 1)
                bc = (2 * cpr, 2 * cpr + 1)
                gg = [bank_guard[bz[0]], bank_guard[bz[1]]]
                p_z = None
                for k in range(KH):
                    for q in range(2):
                        last = (k == KH - 1 and q == 1)
                        p_z = p.emit("tensor", lambda e, k=k, q=q, zj=zj, bz=bz, sg_i=sg_i: e.matmul(
                            psb[bz[q]][:, 0:NQ],
                            a_view("wi", zj, k * 128, 128),
                            uT[:, k * GT + sg_i * SGT + q * NQ: k * GT + sg_i * SGT + (q + 1) * NQ],
                            start=(k == 0), stop=(k == KH - 1)),
                            waits=[(f"ld_wi{zj}", wz), gg[0], gg[1]],
                            inc="pe" if last else None)
                buf_lastuse[("wi", zj)] = p_z
                gg = [bank_guard[bc[0]], bank_guard[bc[1]]]
                p_xc = None
                for k in range(32):
                    for q in range(2):
                        last = (k == 31 and q == 1)
                        p_xc = p.emit("tensor", lambda e, k=k, q=q, cj=cj, bc=bc: e.matmul(
                            psb[bc[q]][:, 0:NQ],
                            a_view("cvb", cj, k * 128, 128),
                            arena[:, off_xT + k * SGT + q * NQ: off_xT + k * SGT + (q + 1) * NQ],
                            start=(k == 0), stop=(k == 31)),
                            waits=[(f"ld_cvb{cj}", wc), ("dv", xT_ready), gg[0], gg[1]],
                            inc="pe" if last else None)
                buf_lastuse[("cvb", cj)] = p_xc
                pB_last = p_xc
                slot = (m % 2) * SGT
                for q in range(2):
                    tsw = tsil_guard[m % 2]
                    a_z = p.emit("scalar", lambda e, slot=slot, q=q, bz=bz: e.activation(
                        tsil[:, slot + q * NQ: slot + (q + 1) * NQ],
                        psb[bz[q]][:, 0:NQ], AF.Silu),
                        waits=[("pe", p_z), ("dv", tsw), uT_ready[0], uT_ready[1]])
                    bank_guard[bz[q]] = ("ac", a_z)
                    gv = p.emit("vector", lambda e, m=m, slot=slot, q=q, bc=bc: e.scalar_tensor_tensor(
                        arena[:, off_sb_ + m * SGT + q * NQ: off_sb_ + m * SGT + (q + 1) * NQ],
                        psb[bc[q]][:, 0:NQ], dvec_sb[:, m: m + 1],
                        tsil[:, slot + q * NQ: slot + (q + 1) * NQ], ALU.mult, ALU.mult),
                        waits=[("pe", p_xc), ("ac", a_z), ("ld_y", CONSTS)])
                    bank_guard[bc[q]] = ("dv", gv)
                    tsil_guard[m % 2] = gv
            sb_ready = p.cnt["dv"]
            wo_all = {}
            def emit_wo(ip):
                if ip <= 7:
                    wj = ip % 3
                    wo_all[ip] = (wj, wdma("wob", wj, "wo", ip,
                                           extra_pe_guard=pB_last))
            for ip in range(3):
                emit_wo(ip)
            for n in range(4):
                wvs = [wo_all[n * 2], wo_all[n * 2 + 1]]
                for t in range(STT):
                    pc = (n * STT + t) % 3
                    bks = (2 * pc, 2 * pc + 1)
                    gg = [bank_guard[bks[0]], bank_guard[bks[1]]]
                    p_o = None
                    for k in range(32):
                        kh = k // 16
                        wj, wv = wvs[kh]
                        for q in range(2):
                            last = (k == 31 and q == 1)
                            p_o = p.emit("tensor", lambda e, k=k, q=q, t=t, wj=wj, bks=bks: e.matmul(
                                psb[bks[q]][:, 0:256],
                                arena[:, off_sb_ + k * SGT + t * 128: off_sb_ + k * SGT + t * 128 + 128],
                                a_view("wob", wj, (k % 16) * 512 + q * 256, 256),
                                start=(k == 0), stop=(k == 31)),
                                waits=[(f"ld_wob{wj}", wv), ("dv", sb_ready), gg[0], gg[1]],
                                inc="pe" if last else None)
                    buf_lastuse[("wob", wvs[0][0])] = p_o
                    buf_lastuse[("wob", wvs[1][0])] = p_o
                    rcol = (sg_i * STT + t) * H + n * 512
                    for q in range(2):
                        gv = p.emit("vector", lambda e, rcol=rcol, q=q, bks=bks: e.tensor_tensor(
                            rbuf[:, rcol + q * 256: rcol + (q + 1) * 256],
                            psb[bks[q]][:, 0:256],
                            rbuf[:, rcol + q * 256: rcol + (q + 1) * 256], ALU.add),
                            waits=[("pe", p_o)])
                        bank_guard[bks[q]] = ("dv", gv)
                emit_wo(n * 2 + 3)
                emit_wo(n * 2 + 4)
            af_sg = p.cnt["pe"]
            for key in (("wi", 0), ("wi", 1), ("wi", 2), ("cvb", 0), ("cvb", 1)):
                buf_lastuse[key] = max(buf_lastuse[key], af_sg)
        af = p.cnt["pe"]
        for key in (("g", 0), ("g", 1), ("u", 0), ("u", 1), ("d", 0), ("d", 1)):
            buf_lastuse[key] = max(buf_lastuse[key], af)

    st_cnt = 0
    fin_tiles = [0] * TT
    for g in range(NG):
        NPIECE = min(4, TT)
        for piece in range(NPIECE):
            waits = []
            if g > 0 and piece == 0:
                waits.append(("st", st_cnt))
            if piece > 0:
                waits.append(("ld_x", (g * NPIECE + piece) * 16))
            p.emit("sync", lambda e, g=g, piece=piece, NPIECE=NPIECE: e.dma_start(
                out=rbuf[:, piece * (TT // NPIECE) * H:(piece + 1) * (TT // NPIECE) * H]
                    .rearrange("q (t h) -> q t h", h=H),
                in_=ap["x"][g * GT + piece * (GT // NPIECE): g * GT + (piece + 1) * (GT // NPIECE), :]
                    .rearrange("(t q) h -> q t h", q=128)),
                waits=waits, inc="ld_x", amt=16)
        for sub in range(3):
            SUBIDX[0] = sub
            uT_ready = ln_phase(g)
            if sub in (0, 2):
                router_phase(uT_ready)
                fin_tiles = ffn_phase(g, uT_ready)
            else:
                attn_phase(g, uT_ready)
        for piece in range(2):
            fin = max(fin_tiles[piece * (TT // 2): (piece + 1) * (TT // 2)])
            st_cnt = p.emit("sync", lambda e, g=g, piece=piece: e.dma_start(
                out=out_ap[g * GT + piece * (GT // 2): g * GT + (piece + 1) * (GT // 2), :]
                    .rearrange("(t q) h -> q t h", q=128),
                in_=rbuf[:, piece * (TT // 2) * H:(piece + 1) * (TT // 2) * H]
                    .rearrange("q (t h) -> q t h", h=H)),
                waits=[("dv", fin)], inc="st", amt=16)
    p.emit("sync", None, waits=[("st", st_cnt)])

    with es:
        block = es.enter_context(nc.Block())
        for eng in Plan.ENGINES:
            ops = p.ops[eng]
            if not ops:
                continue
            def body(e, ops=ops):
                hwm = {}
                for waits, fn, inc, amt in ops:
                    for s, v in waits:
                        if hwm.get(s, 0) < v:
                            e.wait_ge(sems[s], v)
                            hwm[s] = v
                    if fn is not None:
                        ins = fn(e)
                        if inc is not None:
                            ins.then_inc(sems[inc], amt)
            getattr(block, eng)(body)
    return nc


def pack_inputs(i):
    c = np.ascontiguousarray
    d = {}
    d["gw"] = c(np.asarray(i["L0_ffn_gate_w"]).reshape(8, 16, 128, 11, 128)
                .transpose(0, 3, 2, 1, 4).reshape(88, 128, 2048))
    d["uw"] = c(np.asarray(i["L0_ffn_up_w"]).reshape(8, 16, 128, 11, 128)
                .transpose(0, 3, 2, 1, 4).reshape(88, 128, 2048))
    d["dw"] = c(np.asarray(i["L0_ffn_down_w"]).reshape(8, 11, 128, 4, 512)
                .transpose(0, 3, 2, 1, 4).reshape(32, 128, 5632))
    d["sg"] = c(np.asarray(i["L0_ffn_s_gate_w"]).reshape(16, 128, 2, 11, 128)
                .transpose(2, 3, 1, 0, 4).reshape(22, 128, 2048))
    d["su"] = c(np.asarray(i["L0_ffn_s_up_w"]).reshape(16, 128, 2, 11, 128)
                .transpose(2, 3, 1, 0, 4).reshape(22, 128, 2048))
    d["sd"] = c(np.asarray(i["L0_ffn_s_down_w"]).reshape(2, 11, 128, 4, 512)
                .transpose(0, 3, 2, 1, 4).reshape(8, 128, 5632))
    d["wi"] = c(np.asarray(i["L0_attn_W_in"]).reshape(64, 128, 16, 128)
                .transpose(0, 3, 2, 1).reshape(64, 128, 2048))
    d["cv"] = c(np.asarray(i["L0_attn_conv_w"]).reshape(32, 128, 32, 128)
                .transpose(0, 3, 2, 1).reshape(32, 128, 4096))
    d["wo"] = c(np.asarray(i["L0_attn_W_out"]).reshape(4, 512, 2, 16, 128)
                .transpose(0, 2, 4, 3, 1).reshape(8, 128, 8192))
    d["rw"] = c(np.asarray(i["L0_ffn_router_w"]).reshape(8, 16, 128)
                .transpose(2, 1, 0).reshape(128, 128))
    d["dvec"] = c(np.asarray(i["L0_attn_D"]).reshape(32, 128).T)
    d["bsg"] = c(np.asarray(i["L0_ffn_s_gate_b"]).reshape(22, 128).T)
    d["bsu"] = c(np.asarray(i["L0_ffn_s_up_b"]).reshape(22, 128).T)
    d["bsdb"] = c(np.broadcast_to(np.asarray(i["L0_ffn_s_down_b"]), (128, 2048)))
    d["idn"] = np.eye(128, dtype=BFNP)
    return d


def kernel(**inputs):
    shared = pack_inputs(inputs)
    x = np.ascontiguousarray(np.asarray(inputs["L0_attn_hidden_states"],
                                        dtype=np.float32).reshape(16384, 2048))
    nc = build_program(GT=1024, TOKC=2048)
    in_maps = []
    for cidx in range(N_CORES):
        m = dict(shared)
        m["x"] = np.ascontiguousarray(x[cidx * 2048:(cidx + 1) * 2048])
        in_maps.append(m)
    res = run_bass_kernel_spmd(nc, in_maps, core_ids=list(range(N_CORES)))
    out = np.concatenate([np.asarray(r["out"]) for r in res.results], axis=0)
    return out.reshape(1, 16384, 2048).astype(np.float32)


# version 08
# speedup vs baseline: 1.3580x; 1.1155x over previous
import numpy as np
import ml_dtypes
from contextlib import ExitStack

import concourse.bass as bass
import concourse.mybir as mybir
from concourse.bass_utils import run_bass_kernel_spmd

F32 = mybir.dt.float32
BF16 = mybir.dt.bfloat16
AF = mybir.ActivationFunctionType
ALU = mybir.AluOpType
AX = mybir.AxisListType
BFNP = ml_dtypes.bfloat16
I32 = mybir.dt.int32

H = 2048
KH = H // 128
I_EXP = 1408
KI = I_EXP // 128
N_CORES = 8


class Plan:
    ENGINES = ("tensor", "vector", "scalar", "gpsimd", "sync")
    SELF = {"vector": "dv", "scalar": "ac"}

    def __init__(self):
        self.ops = {e: [] for e in self.ENGINES}
        self.cnt = {}

    def sem(self, name):
        self.cnt.setdefault(name, 0)

    def emit(self, eng, fn, waits=(), inc=None, amt=1):
        if eng in self.SELF and fn is not None and (inc is None or inc == self.SELF[eng]):
            s = self.SELF[eng]
            waits = tuple(waits) + ((s, self.cnt[s]),)
            if inc is None:
                inc, amt = s, 1
        for s, v in waits:
            assert s in self.cnt, s
            assert v <= self.cnt[s], ("wait ahead of plan", s, v, self.cnt[s])
        self.ops[eng].append((tuple(waits), fn, inc, amt))
        if inc is not None:
            self.cnt[inc] += amt
            return self.cnt[inc]
        return None


def build_program(GT=1024, TOKC=2048):
    NG = TOKC // GT
    TT = GT // 128
    SGT = GT // 2
    STT = TT // 2
    NQ = GT // 4
    HGT = GT // 2

    nc = bass.Bass("TRN2", target_bir_lowering=False)
    ap = {}
    def din(name, shape, dt=F32):
        ap[name] = nc.dram_tensor(name, shape, dt, kind="ExternalInput").ap()
    din("x", [TOKC, H])
    din("gw", [88, 128, KH * 128])
    din("uw", [88, 128, KH * 128])
    din("dw", [32, 128, KI * 512])
    din("sg", [22, 128, KH * 128])
    din("su", [22, 128, KH * 128])
    din("sd", [8, 128, KI * 512])
    din("wi", [64, 128, KH * 128])
    din("cv", [32, 128, 32 * 128])
    din("wo", [8, 128, 16 * 512])
    din("rw", [128, 128])
    din("dvec", [128, 32])
    din("bsg", [128, 22])
    din("bsu", [128, 22])
    din("bsdb", [128, H])
    din("idn", [128, 128], BF16)
    din("idnf", [128, 128])
    din("toki", [128, TT])
    din("ecst", [128, 1])
    din("idn8", [128, 8])
    out_ap = nc.dram_tensor("out", [TOKC, H], F32, kind="ExternalOutput").ap()
    import os as _os
    _capo = _os.environ.get("KCAP")
    CAP = int(_capo) if _capo else (320 if GT == 1024 else GT // 2)
    CTILES = []
    _b = 0
    while _b < CAP:
        CTILES.append((_b, min(128, CAP - _b)))
        _b += 128
    u_scr = nc.dram_tensor("u_scr", [GT, H], BF16, kind="Internal").ap()
    ug_dram = nc.dram_tensor("ug_dram", [8 * CAP + 128, H], BF16, kind="Internal").ap()
    ogd = nc.dram_tensor("ogd", [8 * CAP + 128, H], BF16, kind="Internal").ap()
    bsdh = nc.dram_tensor("bsdh", [128, H], F32, kind="Internal").ap()
    scr = {}
    scr["wi"] = nc.dram_tensor("scr_wi", [64, 128, KH * 128], BF16, kind="Internal").ap()
    scr["cv"] = nc.dram_tensor("scr_cv", [32, 128, 32 * 128], BF16, kind="Internal").ap()
    scr["wo"] = nc.dram_tensor("scr_wo", [8, 128, 16 * 512], BF16, kind="Internal").ap()

    p = Plan()
    for s in ("pe", "dv", "ac", "st", "ld_y", "ld_x", "ld_c", "scr_c", "ga", "sct", "bsx", "us0", "us1", "sc0", "sc1", "dg0", "dg1", "gg0", "gg1", "zin"):
        p.sem(s)
    BUFN = (("g", 2), ("u", 2), ("d", 2), ("wi", 3), ("cvb", 2), ("wob", 3))
    for pref, n in BUFN:
        for i in range(n):
            p.sem(f"ld_{pref}{i}")

    es = ExitStack()
    def sbuf(name, cols, dt):
        return es.enter_context(nc.sbuf_tensor(name, [128, cols], dt))

    rbuf = sbuf("rbuf", TT * H, F32)
    uT = sbuf("uT", KH * GT, BF16)
    ubf = sbuf("ubf", H, BF16)
    tsil = sbuf("tsil", 2048, BF16)
    spr = sbuf("spr", 1024, F32)
    dstg = sbuf("dstg", 1024, BF16)
    dcf = sbuf("dcf", 8 * TT, F32)
    dsti = sbuf("dsti", 2 * TT, I32)
    dmpc = sbuf("dmpc", 1, F32)
    stats = sbuf("stats", 8 * TT, F32)
    lgsb = sbuf("lgsb", 8 * TT, F32)
    esb = sbuf("esb", 8 * TT, F32)
    tmp8 = sbuf("tmp8", 8 * TT, F32)
    tmp8b = sbuf("tmp8b", 8 * TT, F32)
    st2 = sbuf("st2", 8 * TT, F32)
    c_buf = sbuf("c_buf", 8 * TT, F32)
    rw_bf = sbuf("rw_bf", 128, BF16)
    idn_sb = sbuf("idn_sb", 128, BF16)
    idnf_sb = sbuf("idnf_sb", 128, F32)
    toki_sb = sbuf("toki_sb", TT, F32)
    ecst_sb = sbuf("ecst_sb", 1, F32)
    idn8_sb = sbuf("idn8_sb", 8, F32)
    dvec_sb = sbuf("dvec_sb", 32, F32)
    bsg_sb = sbuf("bsg_sb", 22, F32)
    bsu_sb = sbuf("bsu_sb", 22, F32)

    off_h = 0
    off_utg = KI * CAP
    off_ug = off_utg + KH * CAP
    sz_h = max(KI * GT, off_ug + 2 * 2048)
    off_g = [sz_h + j * 2048 for j in range(2)]
    off_u = [sz_h + 2 * 2048 + j * 2048 for j in range(2)]
    off_d = [sz_h + 4 * 2048 + j * (KI * 512) for j in range(2)]
    ffn_end = sz_h + 4 * 2048 + 2 * KI * 512
    off_xT = 0
    off_sb_ = 32 * SGT
    off_wi = [max(ffn_end, 64 * SGT) + j * 2048 for j in range(3)]
    off_cv = [off_wi[0] + 3 * 2048 + j * 4096 for j in range(2)]
    attn_end = off_cv[0] + 2 * 4096
    if GT == 1024:
        off_wo = [0, 8192, off_wi[0]]
        arena_cols = attn_end
    else:
        off_wo = [attn_end + j * 8192 for j in range(3)]
        arena_cols = attn_end + 3 * 8192
    arena = sbuf("arena", arena_cols, BF16)

    psb = [es.enter_context(nc.psum_tensor(f"pb{i}", [128, 512], F32)) for i in range(6)]
    ptb = [es.enter_context(nc.psum_tensor(f"pt{i}", [128, 1024], BF16)) for i in range(2)]
    sems = {}
    for name in p.cnt:
        sems[name] = es.enter_context(nc.semaphore(name))

    bank_guard = {i: ("dv", 0) for i in range(6)}
    ptb_guard = [("dv", 0), ("dv", 0)]
    buf_lastuse = {}
    buf_gen = {}
    for pref, n in BUFN:
        for j in range(n):
            buf_lastuse[(pref, j)] = 0
            buf_gen[(pref, j)] = 0
    tsil_guard = {0: 0, 1: 0}
    CONSTS = 8 * 16
    SCRT = [0]

    p.emit("sync", lambda e: e.dma_start(out=dvec_sb[:, :], in_=ap["dvec"]), inc="ld_y", amt=16)
    p.emit("sync", lambda e: e.dma_start(out=bsg_sb[:, :], in_=ap["bsg"]), inc="ld_y", amt=16)
    p.emit("sync", lambda e: e.dma_start(out=bsu_sb[:, :], in_=ap["bsu"]), inc="ld_y", amt=16)
    p.emit("sync", lambda e: e.dma_start(out=idn_sb[:, :], in_=ap["idn"]), inc="ld_y", amt=16)
    p.emit("sync", lambda e: e.dma_start(out=idnf_sb[:, :], in_=ap["idnf"]), inc="ld_y", amt=16)
    p.emit("sync", lambda e: e.dma_start(out=toki_sb[:, :], in_=ap["toki"]), inc="ld_y", amt=16)
    p.emit("sync", lambda e: e.dma_start(out=ecst_sb[:, :], in_=ap["ecst"]), inc="ld_y", amt=16)
    p.emit("sync", lambda e: e.dma_start(out=idn8_sb[:, :], in_=ap["idn8"]), inc="ld_y", amt=16)
    p.emit("gpsimd", lambda e: e.dma_start(out=rw_bf[:, :], in_=ap["rw"]), inc="ld_c", amt=16)
    scr_jobs = []
    for name, nblk, step in (("wi", 64, 2), ("cv", 32, 1), ("wo", 8, 1)):
        for a in range(0, nblk, step):
            scr_jobs.append((name, a, min(a + step, nblk)))
    SCRT[0] = len(scr_jobs) * 16

    def emit_scr_piece():
        if scr_jobs:
            name, a, bnd = scr_jobs.pop(0)
            p.emit("gpsimd", lambda e, name=name, a=a, bnd=bnd: e.dma_start(
                out=scr[name][a:bnd], in_=ap[name][a:bnd]), inc="scr_c", amt=16)

    p.emit("vector", lambda e: e.tensor_scalar(
        dmpc[:, 0:1], toki_sb[:, 0:1], float(8 * CAP), None, ALU.add),
        waits=[("ld_y", CONSTS)])
    BSDH_W = 0
    for hh in range(2):
        lv = p.emit("sync", lambda e, hh=hh: e.dma_start(
            out=spr[:, 0:1024], in_=ap["bsdb"][:, hh * 1024:(hh + 1) * 1024]),
            waits=[("bsx", BSDH_W)], inc="bsx", amt=16)
        v0 = p.emit("vector", lambda e: e.tensor_scalar(
            spr[:, 0:1024], spr[:, 0:1024], 0.5, None, ALU.mult),
            waits=[("bsx", lv)])
        BSDH_W = p.emit("sync", lambda e, hh=hh: e.dma_start(
            out=bsdh[:, hh * 1024:(hh + 1) * 1024], in_=spr[:, 0:1024]),
            waits=[("dv", v0)], inc="bsx", amt=16)

    vz = p.emit("vector", lambda e: e.memset(
        arena[:, off_ug: off_ug + 2048], 0.0))
    zr = 0
    nrow = 8 * CAP + 128
    a0 = 0
    while a0 < nrow:
        b0 = min(a0 + 128, nrow)
        for tgt in (ug_dram, ogd):
            zr = p.emit("sync", lambda e, a0=a0, b0=b0, tgt=tgt: e.dma_start(
                out=tgt[a0:b0, :], in_=arena[0:b0 - a0, off_ug: off_ug + 2048]),
                waits=[("dv", vz)], inc="zin", amt=16)
        a0 += 128
    ZINIT = zr

    WQ = {"g": "gpsimd", "u": "gpsimd", "d": "gpsimd",
          "wi": "sync", "cvb": "sync", "wob": "sync"}
    WSRC_SCR = {"wi": "wi", "cvb": "cv", "wob": "wo"}
    OFFS = {"g": off_g, "u": off_u, "d": off_d, "wi": off_wi, "cvb": off_cv, "wob": off_wo}
    NCOLS = {"g": 2048, "u": 2048, "d": KI * 512, "wi": 2048, "cvb": 4096, "wob": 8192}

    def wdma(stream, j, src_name, blk, extra_pe_guard=0):
        gen = buf_gen[(stream, j)] = buf_gen[(stream, j)] + 1
        guard = max(buf_lastuse[(stream, j)], extra_pe_guard)
        o = OFFS[stream][j]
        ncols = NCOLS[stream]
        waits = []
        if guard:
            waits.append(("pe", guard))
        if stream in WSRC_SCR:
            waits.append(("scr_c", SCRT[0]))
            srcap = scr[WSRC_SCR[stream]]
        else:
            srcap = ap[src_name]
        p.emit(WQ[stream],
               lambda e, o=o, n=ncols, s=srcap, b=blk: e.dma_start(
                   out=arena[:, o:o + n], in_=s[b]),
               waits=waits, inc=f"ld_{stream}{j}", amt=16)
        return gen * 16

    def a_view(stream, j, lo, n):
        o = OFFS[stream][j]
        return arena[:, o + lo: o + lo + n]

    SUBIDX = [0]
    lnbuf = [ubf, tsil]
    lnbuf_tr_guard = [0, 0]
    LNUS = [0, 0]
    USCT = [(0, 0)]
    def ln_phase(g):
        acg = p.cnt["ac"]
        dvg = p.cnt["dv"]
        NPIECE = min(4, TT)
        TPP = TT // NPIECE
        for t in range(TT):
            par = t % 2
            buf = lnbuf[par]
            w0 = [("dv", dvg), ("pe", lnbuf_tr_guard[par]),
                  (f"us{par}", LNUS[par])]
            if SUBIDX[0] == 0:
                w0.append(("ld_x", (g * NPIECE + t // TPP + 1) * 16))
            a_sum = p.emit("scalar", lambda e, t=t, buf=buf: e.activation(
                buf[:, :], rbuf[:, t * H:(t + 1) * H], AF.Copy,
                accum_out=stats[:, t * 8: t * 8 + 1]),
                waits=w0 + [("sct", ACCG[0])])
            v_negm = p.emit("vector", lambda e, t=t: e.tensor_scalar(
                stats[:, t * 8 + 1: t * 8 + 2], stats[:, t * 8: t * 8 + 1],
                -1.0 / H, None, ALU.mult), waits=[("ac", a_sum)])
            a_sq = p.emit("scalar", lambda e, t=t, buf=buf: e.activation(
                buf[:, :], rbuf[:, t * H:(t + 1) * H], AF.Square,
                bias=stats[:, t * 8 + 1: t * 8 + 2], scale=1.0,
                accum_out=stats[:, t * 8 + 2: t * 8 + 3]),
                waits=[("dv", v_negm), (f"us{t % 2}", LNUS[t % 2])])
            v_var = p.emit("vector", lambda e, t=t: e.tensor_scalar(
                stats[:, t * 8 + 3: t * 8 + 4], stats[:, t * 8 + 2: t * 8 + 3],
                1.0 / H, 1e-5, ALU.mult, ALU.add), waits=[("ac", a_sq)])
            a_sd = p.emit("scalar", lambda e, t=t: e.activation(
                stats[:, t * 8 + 4: t * 8 + 5], stats[:, t * 8 + 3: t * 8 + 4],
                AF.Sqrt), waits=[("dv", v_var)])
            p.emit("vector", lambda e, t=t: e.reciprocal(
                stats[:, t * 8 + 5: t * 8 + 6], stats[:, t * 8 + 4: t * 8 + 5]),
                waits=[("ac", a_sd)])
            v_ubf = p.emit("vector", lambda e, t=t, buf=buf: e.tensor_scalar(
                buf[:, :], rbuf[:, t * H:(t + 1) * H], stats[:, t * 8 + 1: t * 8 + 2],
                stats[:, t * 8 + 5: t * 8 + 6], ALU.add, ALU.mult),
                waits=[("pe", lnbuf_tr_guard[par]), ("ac", acg)])
            if SUBIDX[0] != 1:
                LNUS[par] = p.emit("sync", lambda e, t=t, buf=buf: e.dma_start(
                    out=u_scr[t * 128:(t + 1) * 128, :], in_=buf[:, :]),
                    waits=[("dv", v_ubf), ("us0", USCT[0][0]), ("us1", USCT[0][1])],
                    inc=f"us{par}", amt=16)
            p_tr = [0, 0]
            for half in range(2):
                s0, v0 = ptb_guard[half]
                for c in range(8):
                    cc = half * 8 + c
                    p_tr[half] = p.emit("tensor", lambda e, cc=cc, half=half, c=c, buf=buf: e.transpose(
                        ptb[half][:, c * 128:(c + 1) * 128],
                        buf[:, cc * 128:(cc + 1) * 128], idn_sb[:, :]),
                        waits=[("dv", v_ubf), (s0, v0), ("ld_y", CONSTS)],
                        inc="pe" if c == 7 else None)
            for half in range(2):
                gv = p.emit("vector", lambda e, half=half, t=t: e.tensor_copy(
                    uT[:, :].rearrange("z (c g) -> z c g", g=GT)[:, half * 8:half * 8 + 8,
                                                                t * 128:t * 128 + 128],
                    ptb[half][:, :].rearrange("z (c g) -> z c g", g=128)),
                    waits=[("pe", p_tr[half])])
                ptb_guard[half] = ("dv", gv)
            lnbuf_tr_guard[par] = p_tr[1]
            if par == 1:
                tsil_guard[0] = max(tsil_guard[0], v_ubf)
                tsil_guard[1] = max(tsil_guard[1], v_ubf)
        return (("pe", p.cnt["pe"]), ("dv", p.cnt["dv"]))

    def router_phase(uT_ready):
        for t in range(TT):
            bk = t % 2
            s0, v0 = bank_guard[bk]
            ins_l = None
            for c in range(KH):
                ins_l = p.emit("tensor", lambda e, c=c, t=t, bk=bk: e.matmul(
                    psb[bk][:, 0:8], uT[:, c * GT + t * 128: c * GT + t * 128 + 128],
                    rw_bf[:, c * 8:(c + 1) * 8], start=(c == 0), stop=(c == KH - 1)),
                    waits=[uT_ready[0], uT_ready[1], (s0, v0), ("ld_c", 16)],
                    inc="pe" if c == KH - 1 else None)
            p_lg = ins_l
            gv = p.emit("vector", lambda e, t=t, bk=bk: e.tensor_copy(
                lgsb[:, t * 8:(t + 1) * 8], psb[bk][:, 0:8]),
                waits=[("pe", p_lg)])
            bank_guard[bk] = ("dv", gv)
            p.emit("vector", lambda e, t=t: e.reduce_max(
                st2[:, t * 8: t * 8 + 1], lgsb[:, t * 8:(t + 1) * 8], axis=AX.X))
            v_ngm = p.emit("vector", lambda e, t=t: e.tensor_scalar(
                st2[:, t * 8 + 1: t * 8 + 2], st2[:, t * 8: t * 8 + 1],
                -1.0, None, ALU.mult))
            a_exp = p.emit("scalar", lambda e, t=t: e.activation(
                esb[:, t * 8:(t + 1) * 8], lgsb[:, t * 8:(t + 1) * 8], AF.Exp,
                bias=st2[:, t * 8 + 1: t * 8 + 2], scale=1.0),
                waits=[("dv", v_ngm)])
            p.emit("vector", lambda e, t=t: e.tensor_scalar(
                tmp8[:, t * 8:(t + 1) * 8], esb[:, t * 8:(t + 1) * 8],
                1.0, -2.0, ALU.is_ge, ALU.mult), waits=[("ac", a_exp)])
            p.emit("vector", lambda e, t=t: e.tensor_tensor(
                tmp8[:, t * 8:(t + 1) * 8], tmp8[:, t * 8:(t + 1) * 8],
                esb[:, t * 8:(t + 1) * 8], ALU.add))
            v_m2 = p.emit("vector", lambda e, t=t: e.reduce_max(
                st2[:, t * 8 + 2: t * 8 + 3], tmp8[:, t * 8:(t + 1) * 8], axis=AX.X))
            p.emit("vector", lambda e, t=t: e.tensor_scalar(
                st2[:, t * 8 + 3: t * 8 + 4], st2[:, t * 8 + 2: t * 8 + 3],
                1.0, None, ALU.add))
            v_inv = p.emit("vector", lambda e, t=t: e.reciprocal(
                st2[:, t * 8 + 4: t * 8 + 5], st2[:, t * 8 + 3: t * 8 + 4]))
            p.emit("vector", lambda e, t=t: e.tensor_scalar(
                tmp8b[:, t * 8:(t + 1) * 8], esb[:, t * 8:(t + 1) * 8],
                st2[:, t * 8 + 2: t * 8 + 3], None, ALU.is_ge),
                waits=[("dv", v_m2)])
            p.emit("vector", lambda e, t=t: e.tensor_tensor(
                tmp8b[:, t * 8:(t + 1) * 8], tmp8b[:, t * 8:(t + 1) * 8],
                esb[:, t * 8:(t + 1) * 8], ALU.mult))
            p.emit("vector", lambda e, t=t: e.tensor_scalar(
                c_buf[:, t * 8:(t + 1) * 8], tmp8b[:, t * 8:(t + 1) * 8],
                st2[:, t * 8 + 4: t * 8 + 5], 0.5, ALU.mult, ALU.mult),
                waits=[("dv", v_inv)])

    BIG = 60000.0

    def compaction(g, spr_guard):
        if not hasattr(compaction, "slot_guard"):
            compaction.slot_guard = [0, 0]
        for t in range(TT):
            v_m = p.emit("vector", lambda e, t=t: e.tensor_scalar(
                tmp8b[:, t * 8:(t + 1) * 8], c_buf[:, t * 8:(t + 1) * 8],
                0.0, None, ALU.is_gt),
                waits=[("bsx", spr_guard), ("pe", SPRG[0])])
            bk = t % 2
            s0, v0 = bank_guard[bk]
            p_t = p.emit("tensor", lambda e, t=t, bk=bk: e.transpose(
                psb[bk][0:8, 0:128], tmp8b[:, t * 8:(t + 1) * 8], idnf_sb[:, :]),
                waits=[("dv", v_m), (s0, v0), ("ld_y", CONSTS)], inc="pe")
            gv = p.emit("vector", lambda e, t=t, bk=bk: e.tensor_copy(
                spr[0:8, t * 128:(t + 1) * 128], psb[bk][0:8, 0:128]),
                waits=[("pe", p_t)])
            bank_guard[bk] = ("dv", gv)
        p.emit("vector", lambda e: e.tensor_tensor_scan(
            spr[32:40, 0:GT], spr[0:8, 0:GT], spr[0:8, 0:GT], 0.0,
            ALU.add, ALU.bypass), waits=[("pe", SPRG[0])])
        p.emit("vector", lambda e: e.tensor_scalar(
            spr[32:40, 0:GT], spr[32:40, 0:GT], -1.0, None, ALU.add))
        v_dst = 0
        CHW = min(512, GT)
        for ch in range(GT // CHW):
            cs = slice(ch * CHW, (ch + 1) * CHW)
            bka = ch % 2
            bkb = 2 + ch % 2
            sa0, sa1 = bank_guard[bka]
            sb0, sb1 = bank_guard[bkb]
            p.emit("vector", lambda e, cs=cs, bka=bka: e.tensor_scalar(
                psb[bka][0:8, 0:CHW], spr[32:40, cs], float(CAP) - 0.5,
                None, ALU.is_le), waits=[(sa0, sa1)])
            p.emit("vector", lambda e, cs=cs, bka=bka, bkb=bkb: e.tensor_tensor(
                psb[bkb][0:8, 0:CHW], spr[0:8, cs], psb[bka][0:8, 0:CHW],
                ALU.mult), waits=[(sb0, sb1)])
            p.emit("vector", lambda e, cs=cs: e.tensor_scalar(
                spr[32:40, cs], spr[32:40, cs], ecst_sb[32:40, 0:1], None, ALU.add))
            p.emit("vector", lambda e, cs=cs, bkb=bkb: e.tensor_tensor(
                spr[32:40, cs], spr[32:40, cs], psb[bkb][0:8, 0:CHW], ALU.mult))
            p.emit("vector", lambda e, cs=cs, bkb=bkb: e.tensor_scalar(
                psb[bkb][0:8, 0:CHW], psb[bkb][0:8, 0:CHW], -BIG, BIG,
                ALU.mult, ALU.add))
            v_dst = p.emit("vector", lambda e, cs=cs, bkb=bkb: e.tensor_tensor(
                spr[32:40, cs], spr[32:40, cs], psb[bkb][0:8, 0:CHW], ALU.add))
            bank_guard[bka] = ("dv", v_dst)
            bank_guard[bkb] = ("dv", v_dst)
        usc_marks = []
        usc_slot_guard = compaction.slot_guard
        uwr_totals = (p.cnt["us0"], p.cnt["us1"])
        for t in range(TT):
            vi_marks = [0, 0]
            bk = t % 2
            s0, v0 = bank_guard[bk]
            p_t = p.emit("tensor", lambda e, t=t, bk=bk: e.transpose(
                psb[bk][:, 0:8], spr[32:40, t * 128:(t + 1) * 128], idn8_sb[32:40, 0:8]),
                waits=[("dv", v_dst), (s0, v0)], inc="pe")
            gv = p.emit("vector", lambda e, t=t, bk=bk: e.tensor_copy(
                tmp8b[:, t * 8:(t + 1) * 8], psb[bk][:, 0:8]),
                waits=[("pe", p_t)])
            bank_guard[bk] = ("dv", gv)
            p.emit("vector", lambda e, t=t: e.tensor_reduce(
                dcf[:, t * 8: t * 8 + 1], tmp8b[:, t * 8:(t + 1) * 8],
                axis=AX.X, op=ALU.min))
            p.emit("vector", lambda e, t=t: e.tensor_scalar(
                tmp8[:, t * 8:(t + 1) * 8], tmp8b[:, t * 8:(t + 1) * 8],
                dcf[:, t * 8: t * 8 + 1], BIG, ALU.is_le, ALU.mult))
            p.emit("vector", lambda e, t=t: e.tensor_tensor(
                tmp8[:, t * 8:(t + 1) * 8], tmp8[:, t * 8:(t + 1) * 8],
                tmp8b[:, t * 8:(t + 1) * 8], ALU.add))
            p.emit("vector", lambda e, t=t: e.tensor_reduce(
                dcf[:, t * 8 + 1: t * 8 + 2], tmp8[:, t * 8:(t + 1) * 8],
                axis=AX.X, op=ALU.min))
            for r_ in range(2):
                p.emit("vector", lambda e, t=t, r_=r_: e.tensor_scalar(
                    tmp8[:, t * 8:(t + 1) * 8], tmp8b[:, t * 8:(t + 1) * 8],
                    dcf[:, t * 8 + r_: t * 8 + r_ + 1], None, ALU.is_equal))
                p.emit("vector", lambda e, t=t: e.tensor_tensor(
                    tmp8[:, t * 8:(t + 1) * 8], tmp8[:, t * 8:(t + 1) * 8],
                    c_buf[:, t * 8:(t + 1) * 8], ALU.mult))
                p.emit("vector", lambda e, t=t, r_=r_: e.reduce_sum(
                    dcf[:, t * 8 + 6 + r_: t * 8 + 7 + r_],
                    tmp8[:, t * 8:(t + 1) * 8], axis=AX.X))
                p.emit("vector", lambda e, t=t, r_=r_: e.tensor_scalar(
                    tmp8[:, t * 8: t * 8 + 1], dcf[:, t * 8 + r_: t * 8 + r_ + 1],
                    float(8 * CAP) - 0.5, None, ALU.is_le))
                p.emit("vector", lambda e, t=t, r_=r_: e.tensor_tensor(
                    dcf[:, t * 8 + 6 + r_: t * 8 + 7 + r_],
                    dcf[:, t * 8 + 6 + r_: t * 8 + 7 + r_],
                    tmp8[:, t * 8: t * 8 + 1], ALU.mult))
                p.emit("vector", lambda e, t=t, r_=r_: e.tensor_tensor(
                    dcf[:, t * 8 + r_: t * 8 + r_ + 1],
                    dcf[:, t * 8 + r_: t * 8 + r_ + 1],
                    dmpc[:, 0:1], ALU.min))
                vi_marks[r_] = p.emit("vector", lambda e, t=t, r_=r_: e.tensor_copy(
                    dsti[:, t * 2 + r_: t * 2 + r_ + 1],
                    dcf[:, t * 8 + r_: t * 8 + r_ + 1]))
            slot = t % 2
            gv_l = p.emit("sync", lambda e, t=t, slot=slot: e.dma_start(
                out=arena[:, off_ug + slot * 2048: off_ug + slot * 2048 + 2048],
                in_=u_scr[t * 128:(t + 1) * 128, :]),
                waits=[(f"sc{slot}", usc_slot_guard[slot]),
                       ("us0", uwr_totals[0]), ("us1", uwr_totals[1]),
                       ("zin", ZINIT),
                       ("pe", ug_lastuse[slot])], inc=f"us{slot}", amt=16)
            for r_ in range(2):
                usc_marks.append(p.emit("gpsimd", lambda e, t=t, r_=r_, slot=slot: e.indirect_dma_start(
                    out=ug_dram,
                    out_offset=bass.IndirectOffsetOnAxis(dsti[:, t * 2 + r_: t * 2 + r_ + 1], 0),
                    in_=arena[:, off_ug + slot * 2048: off_ug + slot * 2048 + 2048],
                    in_offset=None),
                    waits=[("dv", vi_marks[r_]), (f"us{slot}", gv_l),
                           ("sc0", p.cnt["sc0"]), ("sc1", p.cnt["sc1"])],
                    inc=f"sc{slot}", amt=16))
            usc_slot_guard[slot] = p.cnt[f"sc{slot}"]
        USCT[0] = (p.cnt["us0"], p.cnt["us1"])
        return (("sc0", p.cnt["sc0"]), ("sc1", p.cnt["sc1"]))

    ug_lastuse = [0, 0]
    dstg_guard = [("dv", 0), ("dv", 0)]

    def sparse_expert(ex, usc_total, uT_ready, pair_ctr):
        gj_src = "gw"
        utg_ready = 0
        for ti, (base, rows) in enumerate(CTILES):
            slot = ti % 2
            gv = p.emit("sync", lambda e, ex=ex, base=base, rows=rows, slot=slot: e.dma_start(
                out=arena[0:rows, off_ug + slot * 2048: off_ug + slot * 2048 + 2048],
                in_=ug_dram[ex * CAP + base: ex * CAP + base + rows, :]),
                waits=[usc_total[0], usc_total[1], (f"us{slot}", p.cnt[f"us{slot}"]),
                       ("pe", ug_lastuse[slot])], inc=f"us{slot}", amt=16)
            p_t = 0
            for c in range(KH):
                half = c // 8
                s0, v0 = ptb_guard[half]
                p_t = p.emit("tensor", lambda e, c=c, rows=rows, slot=slot, half=half: e.transpose(
                    ptb[half][:, (c % 8) * 128:(c % 8) * 128 + rows],
                    arena[0:rows, off_ug + slot * 2048 + c * 128: off_ug + slot * 2048 + (c + 1) * 128],
                    idn_sb[0:rows, 0:rows]),
                    waits=[(f"us{slot}", gv), (s0, v0), ("ld_y", CONSTS)],
                    inc="pe" if c % 8 == 7 else None)
                if c % 8 == 7:
                    hv = p.emit("vector", lambda e, c=c, base=base, rows=rows, half=half: e.tensor_copy(
                        arena[:, off_utg: off_utg + KH * CAP].rearrange(
                            "z (cc q) -> z cc q", q=CAP)[:, c - 7:c + 1, base:base + rows],
                        ptb[half][:, :].rearrange("z (cc q) -> z cc q", q=128)[:, :, 0:rows]),
                        waits=[("pe", p_t)])
                    ptb_guard[half] = ("dv", hv)
                    utg_ready = hv
            ug_lastuse[slot] = p_t
        hb_marks = []
        for m in range(KI):
            gj = (ex * KI + m) % 2
            wg = wdma("g", gj, "gw", ex * KI + m)
            wu = wdma("u", gj, "uw", ex * KI + m)
            emit_scr_piece()
            NH = CAP // 2
            pr = pair_ctr[0] % 3
            pair_ctr[0] += 1
            pu = pair_ctr[0] % 3
            pair_ctr[0] += 1
            bG = (2 * pr, 2 * pr + 1)
            bU = (2 * pu, 2 * pu + 1)
            gw0, gw1 = bank_guard[bG[0]], bank_guard[bG[1]]
            p_G = None
            for k in range(KH):
                for h in range(2):
                    last = (k == KH - 1 and h == 1)
                    p_G = p.emit("tensor", lambda e, k=k, h=h, gj=gj, bG=bG, NH=NH: e.matmul(
                        psb[bG[h]][:, 0:NH], a_view("g", gj, k * 128, 128),
                        arena[:, off_utg + k * CAP + h * NH: off_utg + k * CAP + (h + 1) * NH],
                        start=(k == 0), stop=(k == KH - 1)),
                        waits=[(f"ld_g{gj}", wg), ("dv", utg_ready), gw0, gw1],
                        inc="pe" if last else None)
            gw0, gw1 = bank_guard[bU[0]], bank_guard[bU[1]]
            p_U = None
            for k in range(KH):
                for h in range(2):
                    last = (k == KH - 1 and h == 1)
                    p_U = p.emit("tensor", lambda e, k=k, h=h, gj=gj, bU=bU, NH=NH: e.matmul(
                        psb[bU[h]][:, 0:NH], a_view("u", gj, k * 128, 128),
                        arena[:, off_utg + k * CAP + h * NH: off_utg + k * CAP + (h + 1) * NH],
                        start=(k == 0), stop=(k == KH - 1)),
                        waits=[(f"ld_u{gj}", wu), gw0, gw1],
                        inc="pe" if last else None)
            buf_lastuse[("g", gj)] = p_G
            buf_lastuse[("u", gj)] = p_U
            for h in range(2):
                slot = (m % 2) * 1024 + h * 512
                tsw = tsil_guard[m % 2]
                a_sil = p.emit("scalar", lambda e, h=h, slot=slot, bG=bG, NH=NH: e.activation(
                    tsil[:, slot: slot + NH], psb[bG[h]][:, 0:NH], AF.Silu),
                    waits=[("pe", p_G), ("dv", tsw), ("ld_y", CONSTS)])
                bank_guard[bG[h]] = ("ac", a_sil)
                hv2 = m * CAP + h * NH
                dvv = p.emit("vector", lambda e, h=h, slot=slot, bU=bU, hv2=hv2, NH=NH: e.tensor_tensor(
                    arena[:, off_h + hv2: off_h + hv2 + NH], psb[bU[h]][:, 0:NH],
                    tsil[:, slot: slot + NH], ALU.mult),
                    waits=[("pe", p_U), ("ac", a_sil)])
                bank_guard[bU[h]] = ("dv", dvv)
                tsil_guard[m % 2] = dvv
        hbuf_ready = p.cnt["dv"]
        for n_ in range(4):
            dj = (ex * 4 + n_) % 2
            wd = wdma("d", dj, "dw", ex * 4 + n_)
            p_dn_last = 0
            for ti, (base, rows) in enumerate(CTILES):
                pr = pair_ctr[0] % 3
                pair_ctr[0] += 1
                bks = (2 * pr, 2 * pr + 1)
                gw0, gw1 = bank_guard[bks[0]], bank_guard[bks[1]]
                p_dn = None
                for k in range(KI):
                    for h in range(2):
                        last = (k == KI - 1 and h == 1)
                        p_dn = p.emit("tensor", lambda e, k=k, h=h, base=base, rows=rows, dj=dj, bks=bks: e.matmul(
                            psb[bks[h]][0:rows, 0:256],
                            arena[:, off_h + k * CAP + base: off_h + k * CAP + base + rows],
                            a_view("d", dj, k * 512 + h * 256, 256),
                            start=(k == 0), stop=(k == KI - 1)),
                            waits=[(f"ld_d{dj}", wd), ("dv", hbuf_ready), gw0, gw1],
                            inc="pe" if last else None)
                p_dn_last = p_dn
                slot = (ex * 4 + n_ + ti) % 2
                sg0, sv0 = dstg_guard[slot]
                cva = p.emit("vector", lambda e, rows=rows, slot=slot, bks=bks: e.tensor_copy(
                    dstg[0:rows, slot * 512: slot * 512 + 256], psb[bks[0]][0:rows, 0:256]),
                    waits=[("pe", p_dn), (sg0, sv0)])
                cvb = p.emit("vector", lambda e, rows=rows, slot=slot, bks=bks: e.tensor_copy(
                    dstg[0:rows, slot * 512 + 256: slot * 512 + 512], psb[bks[1]][0:rows, 0:256]),
                    waits=[("pe", p_dn)])
                bank_guard[bks[0]] = ("dv", cva)
                bank_guard[bks[1]] = ("dv", cvb)
                gv = p.emit("sync", lambda e, ex=ex, base=base, rows=rows, n_=n_, slot=slot: e.dma_start(
                    out=ogd[ex * CAP + base: ex * CAP + base + rows, n_ * 512:(n_ + 1) * 512],
                    in_=dstg[0:rows, slot * 512: slot * 512 + 512]),
                    waits=[("dv", cvb)], inc=f"dg{slot}", amt=16)
                dstg_guard[slot] = (f"dg{slot}", gv)
            buf_lastuse[("d", dj)] = p_dn_last

    def shared_expert(hf, uT_ready, pair_ctr):
        fin_tiles = [0] * TT
        for m in range(KI):
            gj = ((8 + hf) * KI + m) % 2
            wg = wdma("g", gj, "sg", hf * KI + m)
            wu = wdma("u", gj, "su", hf * KI + m)
            emit_scr_piece()
            pr = pair_ctr[0] % 3
            pair_ctr[0] += 1
            pu = pair_ctr[0] % 3
            pair_ctr[0] += 1
            bG = (2 * pr, 2 * pr + 1)
            bU = (2 * pu, 2 * pu + 1)
            gw0, gw1 = bank_guard[bG[0]], bank_guard[bG[1]]
            p_G = None
            for k in range(KH):
                for h in range(2):
                    last = (k == KH - 1 and h == 1)
                    p_G = p.emit("tensor", lambda e, k=k, h=h, gj=gj, bG=bG: e.matmul(
                        psb[bG[h]][:, 0:HGT], a_view("g", gj, k * 128, 128),
                        uT[:, k * GT + h * HGT: k * GT + (h + 1) * HGT],
                        start=(k == 0), stop=(k == KH - 1)),
                        waits=[(f"ld_g{gj}", wg), uT_ready[0], uT_ready[1], gw0, gw1],
                        inc="pe" if last else None)
            gw0, gw1 = bank_guard[bU[0]], bank_guard[bU[1]]
            p_U = None
            for k in range(KH):
                for h in range(2):
                    last = (k == KH - 1 and h == 1)
                    p_U = p.emit("tensor", lambda e, k=k, h=h, gj=gj, bU=bU: e.matmul(
                        psb[bU[h]][:, 0:HGT], a_view("u", gj, k * 128, 128),
                        uT[:, k * GT + h * HGT: k * GT + (h + 1) * HGT],
                        start=(k == 0), stop=(k == KH - 1)),
                        waits=[(f"ld_u{gj}", wu), gw0, gw1],
                        inc="pe" if last else None)
            buf_lastuse[("g", gj)] = p_G
            buf_lastuse[("u", gj)] = p_U
            for h in range(2):
                slot = (m % 2) * 1024 + h * 512
                tsw = tsil_guard[m % 2]
                col = hf * KI + m
                a_sil = p.emit("scalar", lambda e, h=h, slot=slot, bG=bG, col=col: e.activation(
                    tsil[:, slot: slot + HGT], psb[bG[h]][:, 0:HGT], AF.Silu,
                    bias=bsg_sb[:, col: col + 1], scale=1.0),
                    waits=[("pe", p_G), ("dv", tsw), ("ld_y", CONSTS)])
                bank_guard[bG[h]] = ("ac", a_sil)
                hv = m * GT + h * HGT
                dvv = p.emit("vector", lambda e, h=h, slot=slot, bU=bU, hv=hv, col=col: e.scalar_tensor_tensor(
                    arena[:, off_h + hv: off_h + hv + HGT], psb[bU[h]][:, 0:HGT],
                    bsu_sb[:, col: col + 1], tsil[:, slot: slot + HGT],
                    ALU.add, ALU.mult),
                    waits=[("pe", p_U), ("ac", a_sil), ("ld_y", CONSTS)])
                bank_guard[bU[h]] = ("dv", dvv)
                tsil_guard[m % 2] = dvv
        hbuf_ready = p.cnt["dv"]
        for n_ in range(4):
            dj = n_ % 2
            wd = wdma("d", dj, "sd", hf * 4 + n_)
            p_dn_last = 0
            for t in range(TT):
                pr = pair_ctr[0] % 3
                pair_ctr[0] += 1
                bks = (2 * pr, 2 * pr + 1)
                gw0, gw1 = bank_guard[bks[0]], bank_guard[bks[1]]
                p_dn = None
                for k in range(KI):
                    for h in range(2):
                        last = (k == KI - 1 and h == 1)
                        p_dn = p.emit("tensor", lambda e, k=k, h=h, t=t, dj=dj, bks=bks: e.matmul(
                            psb[bks[h]][:, 0:256],
                            arena[:, off_h + k * GT + t * 128: off_h + k * GT + t * 128 + 128],
                            a_view("d", dj, k * 512 + h * 256, 256),
                            start=(k == 0), stop=(k == KI - 1)),
                            waits=[(f"ld_d{dj}", wd), ("dv", hbuf_ready), gw0, gw1],
                            inc="pe" if last else None)
                p_dn_last = p_dn
                for h in range(2):
                    rcol = t * H + n_ * 512 + h * 256
                    dvv = p.emit("vector", lambda e, h=h, rcol=rcol, bks=bks: e.scalar_tensor_tensor(
                        rbuf[:, rcol: rcol + 256], psb[bks[h]][:, 0:256],
                        0.5, rbuf[:, rcol: rcol + 256], ALU.mult, ALU.add),
                        waits=[("pe", p_dn)])
                    bank_guard[bks[h]] = ("dv", dvv)
                fin_tiles[t] = p.cnt["dv"]
            buf_lastuse[("d", dj)] = p_dn_last
        return fin_tiles

    SPRG = [0]
    ACCG = [0]
    def ffn_phase(g, uT_ready, pair_ctr=[0]):
        usc_total = compaction(g, BSDH_W)
        SPRG[0] = p.cnt["pe"]
        for ex in range(8):
            sparse_expert(ex, usc_total, uT_ready, pair_ctr)
        ogd_total = (p.cnt["dg0"], p.cnt["dg1"])
        for t in range(TT):
            for r_ in range(2):
                for n_ in range(4):
                    slot = (t * 8 + r_ * 4 + n_) % 2
                    sg0, sv0 = dstg_guard[slot]
                    gv = p.emit("gpsimd", lambda e, t=t, r_=r_, n_=n_, slot=slot: e.indirect_dma_start(
                        out=dstg[:, slot * 512: slot * 512 + 512],
                        out_offset=None,
                        in_=ogd,
                        in_offset=bass.IndirectOffsetOnAxis(dsti[:, t * 2 + r_: t * 2 + r_ + 1], 0),
                        element_offset=n_ * 512),
                        waits=[("dg0", ogd_total[0]), ("dg1", ogd_total[1]),
                               (sg0, sv0)], inc=f"gg{slot}", amt=16)
                    dvv = p.emit("vector", lambda e, t=t, r_=r_, n_=n_, slot=slot: e.scalar_tensor_tensor(
                        rbuf[:, t * H + n_ * 512: t * H + (n_ + 1) * 512],
                        dstg[:, slot * 512: slot * 512 + 512],
                        dcf[:, t * 8 + 6 + r_: t * 8 + 7 + r_],
                        rbuf[:, t * H + n_ * 512: t * H + (n_ + 1) * 512],
                        ALU.mult, ALU.add),
                        waits=[(f"gg{slot}", gv)])
                    dstg_guard[slot] = ("dv", dvv)
        for hf in range(2):
            fin_tiles = shared_expert(hf, uT_ready, pair_ctr)
        ug_lastuse[0] = max(ug_lastuse[0], p.cnt["pe"])
        ug_lastuse[1] = max(ug_lastuse[1], p.cnt["pe"])
        acc_marks = [0] * TT
        for t in range(TT):
            acc_marks[t] = p.emit("gpsimd", lambda e, t=t: e.dma_start(
                out=rbuf[:, t * H:(t + 1) * H], in_=bsdh, accum_op=ALU.add),
                waits=[("dv", fin_tiles[t]), ("bsx", BSDH_W)], inc="sct", amt=16)
        ACCG[0] = acc_marks[-1]
        return fin_tiles, [acc_marks[-1]] * TT

    def attn_phase(g, uT_ready):
        for sg_i in range(2):
            for m in range(32):
                wj = m % 3
                wv = wdma("wi", wj, "wi", m)
                pr = m % 3
                bks = (2 * pr, 2 * pr + 1)
                gg = [bank_guard[bks[0]], bank_guard[bks[1]]]
                p_xt = None
                for k in range(KH):
                    for q in range(2):
                        last = (k == KH - 1 and q == 1)
                        p_xt = p.emit("tensor", lambda e, k=k, q=q, wj=wj, bks=bks, sg_i=sg_i: e.matmul(
                            psb[bks[q]][:, 0:NQ],
                            a_view("wi", wj, k * 128, 128),
                            uT[:, k * GT + sg_i * SGT + q * NQ: k * GT + sg_i * SGT + (q + 1) * NQ],
                            start=(k == 0), stop=(k == KH - 1)),
                            waits=[(f"ld_wi{wj}", wv), uT_ready[0], uT_ready[1], gg[0], gg[1]],
                            inc="pe" if last else None)
                buf_lastuse[("wi", wj)] = p_xt
                for q in range(2):
                    gv = p.emit("vector", lambda e, m=m, q=q, bks=bks: e.tensor_copy(
                        arena[:, off_xT + m * SGT + q * NQ: off_xT + m * SGT + (q + 1) * NQ],
                        psb[bks[q]][:, 0:NQ]),
                        waits=[("pe", p_xt)])
                    bank_guard[bks[q]] = ("dv", gv)
            xT_ready = p.cnt["dv"]
            pB_last = 0
            for m in range(32):
                cj = m % 2
                zj = m % 3
                wc = wdma("cvb", cj, "cv", m)
                wz = wdma("wi", zj, "wi", 32 + m)
                zpr = (2 * m) % 3
                cpr = (2 * m + 1) % 3
                bz = (2 * zpr, 2 * zpr + 1)
                bc = (2 * cpr, 2 * cpr + 1)
                gg = [bank_guard[bz[0]], bank_guard[bz[1]]]
                p_z = None
                for k in range(KH):
                    for q in range(2):
                        last = (k == KH - 1 and q == 1)
                        p_z = p.emit("tensor", lambda e, k=k, q=q, zj=zj, bz=bz, sg_i=sg_i: e.matmul(
                            psb[bz[q]][:, 0:NQ],
                            a_view("wi", zj, k * 128, 128),
                            uT[:, k * GT + sg_i * SGT + q * NQ: k * GT + sg_i * SGT + (q + 1) * NQ],
                            start=(k == 0), stop=(k == KH - 1)),
                            waits=[(f"ld_wi{zj}", wz), gg[0], gg[1]],
                            inc="pe" if last else None)
                buf_lastuse[("wi", zj)] = p_z
                gg = [bank_guard[bc[0]], bank_guard[bc[1]]]
                p_xc = None
                for k in range(32):
                    for q in range(2):
                        last = (k == 31 and q == 1)
                        p_xc = p.emit("tensor", lambda e, k=k, q=q, cj=cj, bc=bc: e.matmul(
                            psb[bc[q]][:, 0:NQ],
                            a_view("cvb", cj, k * 128, 128),
                            arena[:, off_xT + k * SGT + q * NQ: off_xT + k * SGT + (q + 1) * NQ],
                            start=(k == 0), stop=(k == 31)),
                            waits=[(f"ld_cvb{cj}", wc), ("dv", xT_ready), gg[0], gg[1]],
                            inc="pe" if last else None)
                buf_lastuse[("cvb", cj)] = p_xc
                pB_last = p_xc
                slot = (m % 2) * SGT
                for q in range(2):
                    tsw = tsil_guard[m % 2]
                    a_z = p.emit("scalar", lambda e, slot=slot, q=q, bz=bz: e.activation(
                        tsil[:, slot + q * NQ: slot + (q + 1) * NQ],
                        psb[bz[q]][:, 0:NQ], AF.Silu),
                        waits=[("pe", p_z), ("dv", tsw), uT_ready[0], uT_ready[1]])
                    bank_guard[bz[q]] = ("ac", a_z)
                    gv = p.emit("vector", lambda e, m=m, slot=slot, q=q, bc=bc: e.scalar_tensor_tensor(
                        arena[:, off_sb_ + m * SGT + q * NQ: off_sb_ + m * SGT + (q + 1) * NQ],
                        psb[bc[q]][:, 0:NQ], dvec_sb[:, m: m + 1],
                        tsil[:, slot + q * NQ: slot + (q + 1) * NQ], ALU.mult, ALU.mult),
                        waits=[("pe", p_xc), ("ac", a_z), ("ld_y", CONSTS)])
                    bank_guard[bc[q]] = ("dv", gv)
                    tsil_guard[m % 2] = gv
            sb_ready = p.cnt["dv"]
            wo_all = {}
            def emit_wo(ip):
                if ip <= 7:
                    wj = ip % 3
                    wo_all[ip] = (wj, wdma("wob", wj, "wo", ip,
                                           extra_pe_guard=pB_last))
            for ip in range(3):
                emit_wo(ip)
            for n in range(4):
                wvs = [wo_all[n * 2], wo_all[n * 2 + 1]]
                for t in range(STT):
                    pc = (n * STT + t) % 3
                    bks = (2 * pc, 2 * pc + 1)
                    gg = [bank_guard[bks[0]], bank_guard[bks[1]]]
                    p_o = None
                    for k in range(32):
                        kh = k // 16
                        wj, wv = wvs[kh]
                        for q in range(2):
                            last = (k == 31 and q == 1)
                            p_o = p.emit("tensor", lambda e, k=k, q=q, t=t, wj=wj, bks=bks: e.matmul(
                                psb[bks[q]][:, 0:256],
                                arena[:, off_sb_ + k * SGT + t * 128: off_sb_ + k * SGT + t * 128 + 128],
                                a_view("wob", wj, (k % 16) * 512 + q * 256, 256),
                                start=(k == 0), stop=(k == 31)),
                                waits=[(f"ld_wob{wj}", wv), ("dv", sb_ready), gg[0], gg[1]],
                                inc="pe" if last else None)
                    buf_lastuse[("wob", wvs[0][0])] = p_o
                    buf_lastuse[("wob", wvs[1][0])] = p_o
                    rcol = (sg_i * STT + t) * H + n * 512
                    for q in range(2):
                        gv = p.emit("vector", lambda e, rcol=rcol, q=q, bks=bks: e.tensor_tensor(
                            rbuf[:, rcol + q * 256: rcol + (q + 1) * 256],
                            psb[bks[q]][:, 0:256],
                            rbuf[:, rcol + q * 256: rcol + (q + 1) * 256], ALU.add),
                            waits=[("pe", p_o), ("sct", ACCG[0])])
                        bank_guard[bks[q]] = ("dv", gv)
                emit_wo(n * 2 + 3)
                emit_wo(n * 2 + 4)
            af_sg = p.cnt["pe"]
            for key in (("wi", 0), ("wi", 1), ("wi", 2), ("cvb", 0), ("cvb", 1)):
                buf_lastuse[key] = max(buf_lastuse[key], af_sg)
        af = p.cnt["pe"]
        for key in (("g", 0), ("g", 1), ("u", 0), ("u", 1), ("d", 0), ("d", 1)):
            buf_lastuse[key] = max(buf_lastuse[key], af)

    st_cnt = 0
    fin_tiles = [0] * TT
    for g in range(NG):
        NPIECE = min(4, TT)
        for piece in range(NPIECE):
            waits = []
            if g > 0 and piece == 0:
                waits.append(("st", st_cnt))
            if piece > 0:
                waits.append(("ld_x", (g * NPIECE + piece) * 16))
            p.emit("sync", lambda e, g=g, piece=piece, NPIECE=NPIECE: e.dma_start(
                out=rbuf[:, piece * (TT // NPIECE) * H:(piece + 1) * (TT // NPIECE) * H]
                    .rearrange("q (t h) -> q t h", h=H),
                in_=ap["x"][g * GT + piece * (GT // NPIECE): g * GT + (piece + 1) * (GT // NPIECE), :]
                    .rearrange("(t q) h -> q t h", q=128)),
                waits=waits, inc="ld_x", amt=16)
        for sub in range(3):
            SUBIDX[0] = sub
            uT_ready = ln_phase(g)
            if sub in (0, 2):
                router_phase(uT_ready)
                fin_tiles, acc_marks = ffn_phase(g, uT_ready)
            else:
                attn_phase(g, uT_ready)
        for piece in range(2):
            fin = max(fin_tiles[piece * (TT // 2): (piece + 1) * (TT // 2)])
            am = max(acc_marks[piece * (TT // 2): (piece + 1) * (TT // 2)])
            st_cnt = p.emit("sync", lambda e, g=g, piece=piece: e.dma_start(
                out=out_ap[g * GT + piece * (GT // 2): g * GT + (piece + 1) * (GT // 2), :]
                    .rearrange("(t q) h -> q t h", q=128),
                in_=rbuf[:, piece * (TT // 2) * H:(piece + 1) * (TT // 2) * H]
                    .rearrange("q (t h) -> q t h", h=H)),
                waits=[("dv", fin), ("sct", am)], inc="st", amt=16)
    p.emit("sync", None, waits=[("st", st_cnt)])

    with es:
        block = es.enter_context(nc.Block())
        for eng in Plan.ENGINES:
            ops = p.ops[eng]
            if not ops:
                continue
            def body(e, ops=ops):
                hwm = {}
                for waits, fn, inc, amt in ops:
                    for s, v in waits:
                        if hwm.get(s, 0) < v:
                            e.wait_ge(sems[s], v)
                            hwm[s] = v
                    if fn is not None:
                        ins = fn(e)
                        if inc is not None:
                            ins.then_inc(sems[inc], amt)
            getattr(block, eng)(body)
    return nc


def pack_inputs(i, GT=1024):
    import os as _os
    TT = GT // 128
    _capo = _os.environ.get("KCAP")
    CAP = int(_capo) if _capo else (320 if GT == 1024 else GT // 2)
    c = np.ascontiguousarray
    d = {}
    d["gw"] = c(np.asarray(i["L0_ffn_gate_w"]).reshape(8, 16, 128, 11, 128)
                .transpose(0, 3, 2, 1, 4).reshape(88, 128, 2048))
    d["uw"] = c(np.asarray(i["L0_ffn_up_w"]).reshape(8, 16, 128, 11, 128)
                .transpose(0, 3, 2, 1, 4).reshape(88, 128, 2048))
    d["dw"] = c(np.asarray(i["L0_ffn_down_w"]).reshape(8, 11, 128, 4, 512)
                .transpose(0, 3, 2, 1, 4).reshape(32, 128, 5632))
    d["sg"] = c(np.asarray(i["L0_ffn_s_gate_w"]).reshape(16, 128, 2, 11, 128)
                .transpose(2, 3, 1, 0, 4).reshape(22, 128, 2048))
    d["su"] = c(np.asarray(i["L0_ffn_s_up_w"]).reshape(16, 128, 2, 11, 128)
                .transpose(2, 3, 1, 0, 4).reshape(22, 128, 2048))
    d["sd"] = c(np.asarray(i["L0_ffn_s_down_w"]).reshape(2, 11, 128, 4, 512)
                .transpose(0, 3, 2, 1, 4).reshape(8, 128, 5632))
    d["wi"] = c(np.asarray(i["L0_attn_W_in"]).reshape(64, 128, 16, 128)
                .transpose(0, 3, 2, 1).reshape(64, 128, 2048))
    d["cv"] = c(np.asarray(i["L0_attn_conv_w"]).reshape(32, 128, 32, 128)
                .transpose(0, 3, 2, 1).reshape(32, 128, 4096))
    d["wo"] = c(np.asarray(i["L0_attn_W_out"]).reshape(4, 512, 2, 16, 128)
                .transpose(0, 2, 4, 3, 1).reshape(8, 128, 8192))
    d["rw"] = c(np.asarray(i["L0_ffn_router_w"]).reshape(8, 16, 128)
                .transpose(2, 1, 0).reshape(128, 128))
    d["dvec"] = c(np.asarray(i["L0_attn_D"]).reshape(32, 128).T)
    d["bsg"] = c(np.asarray(i["L0_ffn_s_gate_b"]).reshape(22, 128).T)
    d["bsu"] = c(np.asarray(i["L0_ffn_s_up_b"]).reshape(22, 128).T)
    d["bsdb"] = c(np.broadcast_to(np.asarray(i["L0_ffn_s_down_b"]), (128, 2048)))
    d["idn"] = np.eye(128, dtype=BFNP)
    d["idnf"] = np.eye(128, dtype=np.float32)
    d["toki"] = c((np.arange(TT)[None, :] * 128
                   + np.arange(128)[:, None]).astype(np.float32))
    d["ecst"] = c(((np.arange(128) % 8) * CAP).astype(np.float32).reshape(128, 1))
    d["idn8"] = c((np.arange(128)[:, None] % 8 == np.arange(8)[None, :])
                  .astype(np.float32))
    return d


def kernel(**inputs):
    shared = pack_inputs(inputs)
    x = np.ascontiguousarray(np.asarray(inputs["L0_attn_hidden_states"],
                                        dtype=np.float32).reshape(16384, 2048))
    nc = build_program(GT=1024, TOKC=2048)
    in_maps = []
    for cidx in range(N_CORES):
        m = dict(shared)
        m["x"] = np.ascontiguousarray(x[cidx * 2048:(cidx + 1) * 2048])
        in_maps.append(m)
    res = run_bass_kernel_spmd(nc, in_maps, core_ids=list(range(N_CORES)))
    out = np.concatenate([np.asarray(r["out"]) for r in res.results], axis=0)
    return out.reshape(1, 16384, 2048).astype(np.float32)


# version 10
# speedup vs baseline: 1.3662x; 1.0061x over previous
import numpy as np
import ml_dtypes
from contextlib import ExitStack

import concourse.bass as bass
import concourse.mybir as mybir
from concourse.bass_utils import run_bass_kernel_spmd

F32 = mybir.dt.float32
BF16 = mybir.dt.bfloat16
AF = mybir.ActivationFunctionType
ALU = mybir.AluOpType
AX = mybir.AxisListType
BFNP = ml_dtypes.bfloat16
I32 = mybir.dt.int32

H = 2048
KH = H // 128
I_EXP = 1408
KI = I_EXP // 128
N_CORES = 8


class Plan:
    ENGINES = ("tensor", "vector", "scalar", "gpsimd", "sync")
    SELF = {"vector": "dv", "scalar": "ac"}

    def __init__(self):
        self.ops = {e: [] for e in self.ENGINES}
        self.cnt = {}

    def sem(self, name):
        self.cnt.setdefault(name, 0)

    def emit(self, eng, fn, waits=(), inc=None, amt=1):
        if eng in self.SELF and fn is not None and (inc is None or inc == self.SELF[eng]):
            s = self.SELF[eng]
            waits = tuple(waits) + ((s, self.cnt[s]),)
            if inc is None:
                inc, amt = s, 1
        for s, v in waits:
            assert s in self.cnt, s
            assert v <= self.cnt[s], ("wait ahead of plan", s, v, self.cnt[s])
        self.ops[eng].append((tuple(waits), fn, inc, amt))
        if inc is not None:
            self.cnt[inc] += amt
            return self.cnt[inc]
        return None


def build_program(GT=1024, TOKC=2048):
    NG = TOKC // GT
    TT = GT // 128
    SGT = GT // 2
    STT = TT // 2
    NQ = GT // 4
    HGT = GT // 2

    nc = bass.Bass("TRN2", target_bir_lowering=False)
    ap = {}
    def din(name, shape, dt=F32):
        ap[name] = nc.dram_tensor(name, shape, dt, kind="ExternalInput").ap()
    din("x", [TOKC, H])
    din("gw", [88, 128, KH * 128])
    din("uw", [88, 128, KH * 128])
    din("dw", [32, 128, KI * 512])
    din("sg", [22, 128, KH * 128])
    din("su", [22, 128, KH * 128])
    din("sd", [8, 128, KI * 512])
    din("wi", [64, 128, KH * 128])
    din("cv", [32, 128, 32 * 128])
    din("wo", [8, 128, 16 * 512])
    din("rw", [128, 128])
    din("dvec", [128, 32])
    din("bsg", [128, 22])
    din("bsu", [128, 22])
    din("bsdb", [128, H])
    din("idn", [128, 128], BF16)
    din("idnf", [128, 128])
    din("toki", [128, TT])
    din("ecst", [128, 1])
    din("idn8", [128, 8])
    out_ap = nc.dram_tensor("out", [TOKC, H], F32, kind="ExternalOutput").ap()
    import os as _os
    _capo = _os.environ.get("KCAP")
    CAP = int(_capo) if _capo else (320 if GT == 1024 else GT // 2)
    CTILES = []
    _b = 0
    while _b < CAP:
        CTILES.append((_b, min(128, CAP - _b)))
        _b += 128
    u_scr = nc.dram_tensor("u_scr", [GT, H], BF16, kind="Internal").ap()
    ug_dram = nc.dram_tensor("ug_dram", [8 * CAP + 128, H], BF16, kind="Internal").ap()
    ogd = nc.dram_tensor("ogd", [8 * CAP + 128, H], BF16, kind="Internal").ap()
    bsdh = nc.dram_tensor("bsdh", [128, H], F32, kind="Internal").ap()
    scr = {}
    scr["wi"] = nc.dram_tensor("scr_wi", [64, 128, KH * 128], BF16, kind="Internal").ap()
    scr["cv"] = nc.dram_tensor("scr_cv", [32, 128, 32 * 128], BF16, kind="Internal").ap()
    scr["wo"] = nc.dram_tensor("scr_wo", [8, 128, 16 * 512], BF16, kind="Internal").ap()

    p = Plan()
    for s in ("pe", "dv", "ac", "st", "ld_y", "ld_x", "ld_c", "scr_c", "ga", "sct", "bsx", "us0", "us1", "sc0", "sc1", "dg0", "dg1", "gg0", "gg1", "zin"):
        p.sem(s)
    BUFN = (("g", 2), ("u", 2), ("d", 2), ("wi", 3), ("cvb", 2), ("wob", 3))
    for pref, n in BUFN:
        for i in range(n):
            p.sem(f"ld_{pref}{i}")

    es = ExitStack()
    def sbuf(name, cols, dt):
        return es.enter_context(nc.sbuf_tensor(name, [128, cols], dt))

    rbuf = sbuf("rbuf", TT * H, F32)
    uT = sbuf("uT", KH * GT, BF16)
    ubf = sbuf("ubf", H, BF16)
    tsil = sbuf("tsil", 2048, BF16)
    spr = sbuf("spr", 1024, F32)
    dstg = sbuf("dstg", 1024, BF16)
    dcf = sbuf("dcf", 8 * TT, F32)
    dsti = sbuf("dsti", 2 * TT, I32)
    dmpc = sbuf("dmpc", 1, F32)
    stats = sbuf("stats", 8 * TT, F32)
    lgsb = sbuf("lgsb", 8 * TT, F32)
    esb = sbuf("esb", 8 * TT, F32)
    tmp8 = sbuf("tmp8", 8 * TT, F32)
    tmp8b = sbuf("tmp8b", 8 * TT, F32)
    st2 = sbuf("st2", 8 * TT, F32)
    c_buf = sbuf("c_buf", 8 * TT, F32)
    rw_bf = sbuf("rw_bf", 128, BF16)
    idn_sb = sbuf("idn_sb", 128, BF16)
    idnf_sb = sbuf("idnf_sb", 128, F32)
    toki_sb = sbuf("toki_sb", TT, F32)
    ecst_sb = sbuf("ecst_sb", 1, F32)
    idn8_sb = sbuf("idn8_sb", 8, F32)
    dvec_sb = sbuf("dvec_sb", 32, F32)
    bsg_sb = sbuf("bsg_sb", 22, F32)
    bsu_sb = sbuf("bsu_sb", 22, F32)

    off_h = 0
    off_utg = KI * CAP
    off_ug = off_utg + KH * CAP
    sz_h = max(KI * GT, off_ug + 2 * 2048)
    off_g = [sz_h + j * 2048 for j in range(2)]
    off_u = [sz_h + 2 * 2048 + j * 2048 for j in range(2)]
    off_d = [sz_h + 4 * 2048 + j * (KI * 512) for j in range(2)]
    ffn_end = sz_h + 4 * 2048 + 2 * KI * 512
    off_xT = 0
    off_sb_ = 32 * SGT
    off_wi = [max(ffn_end, 64 * SGT) + j * 2048 for j in range(3)]
    off_cv = [off_wi[0] + 3 * 2048 + j * 4096 for j in range(2)]
    attn_end = off_cv[0] + 2 * 4096
    if GT == 1024:
        off_wo = [0, 8192, off_wi[0]]
        arena_cols = attn_end
    else:
        off_wo = [attn_end + j * 8192 for j in range(3)]
        arena_cols = attn_end + 3 * 8192
    arena = sbuf("arena", arena_cols, BF16)

    psb = [es.enter_context(nc.psum_tensor(f"pb{i}", [128, 512], F32)) for i in range(6)]
    ptb = [es.enter_context(nc.psum_tensor(f"pt{i}", [128, 1024], BF16)) for i in range(2)]
    sems = {}
    for name in p.cnt:
        sems[name] = es.enter_context(nc.semaphore(name))

    bank_guard = {i: ("dv", 0) for i in range(6)}
    ptb_guard = [("dv", 0), ("dv", 0)]
    buf_lastuse = {}
    buf_gen = {}
    for pref, n in BUFN:
        for j in range(n):
            buf_lastuse[(pref, j)] = 0
            buf_gen[(pref, j)] = 0
    tsil_guard = {0: 0, 1: 0}
    CONSTS = 8 * 16
    SCRT = [0]

    p.emit("sync", lambda e: e.dma_start(out=dvec_sb[:, :], in_=ap["dvec"]), inc="ld_y", amt=16)
    p.emit("sync", lambda e: e.dma_start(out=bsg_sb[:, :], in_=ap["bsg"]), inc="ld_y", amt=16)
    p.emit("sync", lambda e: e.dma_start(out=bsu_sb[:, :], in_=ap["bsu"]), inc="ld_y", amt=16)
    p.emit("sync", lambda e: e.dma_start(out=idn_sb[:, :], in_=ap["idn"]), inc="ld_y", amt=16)
    p.emit("sync", lambda e: e.dma_start(out=idnf_sb[:, :], in_=ap["idnf"]), inc="ld_y", amt=16)
    p.emit("sync", lambda e: e.dma_start(out=toki_sb[:, :], in_=ap["toki"]), inc="ld_y", amt=16)
    p.emit("sync", lambda e: e.dma_start(out=ecst_sb[:, :], in_=ap["ecst"]), inc="ld_y", amt=16)
    p.emit("sync", lambda e: e.dma_start(out=idn8_sb[:, :], in_=ap["idn8"]), inc="ld_y", amt=16)
    p.emit("gpsimd", lambda e: e.dma_start(out=rw_bf[:, :], in_=ap["rw"]), inc="ld_c", amt=16)
    scr_jobs = []
    for name, nblk, step in (("wi", 64, 2), ("cv", 32, 1), ("wo", 8, 1)):
        for a in range(0, nblk, step):
            scr_jobs.append((name, a, min(a + step, nblk)))
    SCRT[0] = len(scr_jobs) * 16

    def emit_scr_piece():
        if scr_jobs:
            name, a, bnd = scr_jobs.pop(0)
            p.emit("gpsimd", lambda e, name=name, a=a, bnd=bnd: e.dma_start(
                out=scr[name][a:bnd], in_=ap[name][a:bnd]), inc="scr_c", amt=16)

    p.emit("vector", lambda e: e.tensor_scalar(
        dmpc[:, 0:1], toki_sb[:, 0:1], float(8 * CAP), None, ALU.add),
        waits=[("ld_y", CONSTS)])
    BSDH_W = 0
    for hh in range(2):
        lv = p.emit("sync", lambda e, hh=hh: e.dma_start(
            out=spr[:, 0:1024], in_=ap["bsdb"][:, hh * 1024:(hh + 1) * 1024]),
            waits=[("bsx", BSDH_W)], inc="bsx", amt=16)
        v0 = p.emit("vector", lambda e: e.tensor_scalar(
            spr[:, 0:1024], spr[:, 0:1024], 0.5, None, ALU.mult),
            waits=[("bsx", lv)])
        BSDH_W = p.emit("sync", lambda e, hh=hh: e.dma_start(
            out=bsdh[:, hh * 1024:(hh + 1) * 1024], in_=spr[:, 0:1024]),
            waits=[("dv", v0)], inc="bsx", amt=16)

    vz = p.emit("vector", lambda e: e.memset(
        arena[:, off_ug: off_ug + 2048], 0.0))
    zr = 0
    nrow = 8 * CAP + 128
    a0 = 0
    while a0 < nrow:
        b0 = min(a0 + 128, nrow)
        for tgt in (ug_dram, ogd):
            zr = p.emit("sync", lambda e, a0=a0, b0=b0, tgt=tgt: e.dma_start(
                out=tgt[a0:b0, :], in_=arena[0:b0 - a0, off_ug: off_ug + 2048]),
                waits=[("dv", vz)], inc="zin", amt=16)
        a0 += 128
    ZINIT = zr

    WQ = {"g": "gpsimd", "u": "gpsimd", "d": "gpsimd",
          "wi": "sync", "cvb": "sync", "wob": "sync"}
    WSRC_SCR = {"wi": "wi", "cvb": "cv", "wob": "wo"}
    OFFS = {"g": off_g, "u": off_u, "d": off_d, "wi": off_wi, "cvb": off_cv, "wob": off_wo}
    NCOLS = {"g": 2048, "u": 2048, "d": KI * 512, "wi": 2048, "cvb": 4096, "wob": 8192}

    def wdma(stream, j, src_name, blk, extra_pe_guard=0):
        gen = buf_gen[(stream, j)] = buf_gen[(stream, j)] + 1
        guard = max(buf_lastuse[(stream, j)], extra_pe_guard)
        o = OFFS[stream][j]
        ncols = NCOLS[stream]
        waits = []
        if guard:
            waits.append(("pe", guard))
        if stream in WSRC_SCR:
            waits.append(("scr_c", SCRT[0]))
            srcap = scr[WSRC_SCR[stream]]
        else:
            srcap = ap[src_name]
        p.emit(WQ[stream],
               lambda e, o=o, n=ncols, s=srcap, b=blk: e.dma_start(
                   out=arena[:, o:o + n], in_=s[b]),
               waits=waits, inc=f"ld_{stream}{j}", amt=16)
        return gen * 16

    def a_view(stream, j, lo, n):
        o = OFFS[stream][j]
        return arena[:, o + lo: o + lo + n]

    SUBIDX = [0]
    lnbuf = [ubf, tsil]
    lnbuf_tr_guard = [0, 0]
    LNUS = [0, 0]
    USCT = [(0, 0)]
    def ln_phase(g):
        acg = p.cnt["ac"]
        dvg = p.cnt["dv"]
        NPIECE = min(4, TT)
        TPP = TT // NPIECE
        for t in range(TT):
            par = t % 2
            buf = lnbuf[par]
            w0 = [("dv", dvg), ("pe", lnbuf_tr_guard[par]),
                  (f"us{par}", LNUS[par])]
            if SUBIDX[0] == 0:
                w0.append(("ld_x", (g * NPIECE + t // TPP + 1) * 16))
            a_sum = p.emit("scalar", lambda e, t=t, buf=buf: e.activation(
                buf[:, :], rbuf[:, t * H:(t + 1) * H], AF.Copy,
                accum_out=stats[:, t * 8: t * 8 + 1]),
                waits=w0 + [("sct", ACCG[0])])
            v_negm = p.emit("vector", lambda e, t=t: e.tensor_scalar(
                stats[:, t * 8 + 1: t * 8 + 2], stats[:, t * 8: t * 8 + 1],
                -1.0 / H, None, ALU.mult), waits=[("ac", a_sum)])
            a_sq = p.emit("scalar", lambda e, t=t, buf=buf: e.activation(
                buf[:, :], rbuf[:, t * H:(t + 1) * H], AF.Square,
                bias=stats[:, t * 8 + 1: t * 8 + 2], scale=1.0,
                accum_out=stats[:, t * 8 + 2: t * 8 + 3]),
                waits=[("dv", v_negm), (f"us{t % 2}", LNUS[t % 2])])
            v_var = p.emit("vector", lambda e, t=t: e.tensor_scalar(
                stats[:, t * 8 + 3: t * 8 + 4], stats[:, t * 8 + 2: t * 8 + 3],
                1.0 / H, 1e-5, ALU.mult, ALU.add), waits=[("ac", a_sq)])
            a_sd = p.emit("scalar", lambda e, t=t: e.activation(
                stats[:, t * 8 + 4: t * 8 + 5], stats[:, t * 8 + 3: t * 8 + 4],
                AF.Sqrt), waits=[("dv", v_var)])
            p.emit("vector", lambda e, t=t: e.reciprocal(
                stats[:, t * 8 + 5: t * 8 + 6], stats[:, t * 8 + 4: t * 8 + 5]),
                waits=[("ac", a_sd)])
            v_ubf = p.emit("vector", lambda e, t=t, buf=buf: e.tensor_scalar(
                buf[:, :], rbuf[:, t * H:(t + 1) * H], stats[:, t * 8 + 1: t * 8 + 2],
                stats[:, t * 8 + 5: t * 8 + 6], ALU.add, ALU.mult),
                waits=[("pe", lnbuf_tr_guard[par]), ("ac", acg)])
            if SUBIDX[0] != 1:
                LNUS[par] = p.emit("sync", lambda e, t=t, buf=buf: e.dma_start(
                    out=u_scr[t * 128:(t + 1) * 128, :], in_=buf[:, :]),
                    waits=[("dv", v_ubf), ("us0", USCT[0][0]), ("us1", USCT[0][1])],
                    inc=f"us{par}", amt=16)
            p_tr = [0, 0]
            for half in range(2):
                s0, v0 = ptb_guard[half]
                for c in range(8):
                    cc = half * 8 + c
                    p_tr[half] = p.emit("tensor", lambda e, cc=cc, half=half, c=c, buf=buf: e.transpose(
                        ptb[half][:, c * 128:(c + 1) * 128],
                        buf[:, cc * 128:(cc + 1) * 128], idn_sb[:, :]),
                        waits=[("dv", v_ubf), (s0, v0), ("ld_y", CONSTS)],
                        inc="pe" if c == 7 else None)
            for half in range(2):
                gv = p.emit("vector", lambda e, half=half, t=t: e.tensor_copy(
                    uT[:, :].rearrange("z (c g) -> z c g", g=GT)[:, half * 8:half * 8 + 8,
                                                                t * 128:t * 128 + 128],
                    ptb[half][:, :].rearrange("z (c g) -> z c g", g=128)),
                    waits=[("pe", p_tr[half])])
                ptb_guard[half] = ("dv", gv)
            lnbuf_tr_guard[par] = p_tr[1]
            if par == 1:
                tsil_guard[0] = max(tsil_guard[0], v_ubf)
                tsil_guard[1] = max(tsil_guard[1], v_ubf)
        return (("pe", p.cnt["pe"]), ("dv", p.cnt["dv"]))

    def router_phase(uT_ready):
        for t in range(TT):
            bk = t % 2
            s0, v0 = bank_guard[bk]
            ins_l = None
            for c in range(KH):
                ins_l = p.emit("tensor", lambda e, c=c, t=t, bk=bk: e.matmul(
                    psb[bk][:, 0:8], uT[:, c * GT + t * 128: c * GT + t * 128 + 128],
                    rw_bf[:, c * 8:(c + 1) * 8], start=(c == 0), stop=(c == KH - 1)),
                    waits=[uT_ready[0], uT_ready[1], (s0, v0), ("ld_c", 16)],
                    inc="pe" if c == KH - 1 else None)
            p_lg = ins_l
            gv = p.emit("vector", lambda e, t=t, bk=bk: e.tensor_copy(
                lgsb[:, t * 8:(t + 1) * 8], psb[bk][:, 0:8]),
                waits=[("pe", p_lg)])
            bank_guard[bk] = ("dv", gv)
            p.emit("vector", lambda e, t=t: e.reduce_max(
                st2[:, t * 8: t * 8 + 1], lgsb[:, t * 8:(t + 1) * 8], axis=AX.X))
            v_ngm = p.emit("vector", lambda e, t=t: e.tensor_scalar(
                st2[:, t * 8 + 1: t * 8 + 2], st2[:, t * 8: t * 8 + 1],
                -1.0, None, ALU.mult))
            a_exp = p.emit("scalar", lambda e, t=t: e.activation(
                esb[:, t * 8:(t + 1) * 8], lgsb[:, t * 8:(t + 1) * 8], AF.Exp,
                bias=st2[:, t * 8 + 1: t * 8 + 2], scale=1.0),
                waits=[("dv", v_ngm)])
            p.emit("vector", lambda e, t=t: e.tensor_scalar(
                tmp8[:, t * 8:(t + 1) * 8], esb[:, t * 8:(t + 1) * 8],
                1.0, -2.0, ALU.is_ge, ALU.mult), waits=[("ac", a_exp)])
            p.emit("vector", lambda e, t=t: e.tensor_tensor(
                tmp8[:, t * 8:(t + 1) * 8], tmp8[:, t * 8:(t + 1) * 8],
                esb[:, t * 8:(t + 1) * 8], ALU.add))
            v_m2 = p.emit("vector", lambda e, t=t: e.reduce_max(
                st2[:, t * 8 + 2: t * 8 + 3], tmp8[:, t * 8:(t + 1) * 8], axis=AX.X))
            p.emit("vector", lambda e, t=t: e.tensor_scalar(
                st2[:, t * 8 + 3: t * 8 + 4], st2[:, t * 8 + 2: t * 8 + 3],
                1.0, None, ALU.add))
            v_inv = p.emit("vector", lambda e, t=t: e.reciprocal(
                st2[:, t * 8 + 4: t * 8 + 5], st2[:, t * 8 + 3: t * 8 + 4]))
            p.emit("vector", lambda e, t=t: e.tensor_scalar(
                tmp8b[:, t * 8:(t + 1) * 8], esb[:, t * 8:(t + 1) * 8],
                st2[:, t * 8 + 2: t * 8 + 3], None, ALU.is_ge),
                waits=[("dv", v_m2)])
            p.emit("vector", lambda e, t=t: e.tensor_tensor(
                tmp8b[:, t * 8:(t + 1) * 8], tmp8b[:, t * 8:(t + 1) * 8],
                esb[:, t * 8:(t + 1) * 8], ALU.mult))
            p.emit("vector", lambda e, t=t: e.tensor_scalar(
                c_buf[:, t * 8:(t + 1) * 8], tmp8b[:, t * 8:(t + 1) * 8],
                st2[:, t * 8 + 4: t * 8 + 5], 0.5, ALU.mult, ALU.mult),
                waits=[("dv", v_inv)])

    BIG = 60000.0

    def compaction(g, spr_guard):
        if not hasattr(compaction, "slot_guard"):
            compaction.slot_guard = [0, 0]
        for t in range(TT):
            v_m = p.emit("vector", lambda e, t=t: e.tensor_scalar(
                tmp8b[:, t * 8:(t + 1) * 8], c_buf[:, t * 8:(t + 1) * 8],
                0.0, None, ALU.is_gt),
                waits=[("bsx", spr_guard), ("pe", SPRG[0])])
            bk = t % 2
            s0, v0 = bank_guard[bk]
            p_t = p.emit("tensor", lambda e, t=t, bk=bk: e.transpose(
                psb[bk][0:8, 0:128], tmp8b[:, t * 8:(t + 1) * 8], idnf_sb[:, :]),
                waits=[("dv", v_m), (s0, v0), ("ld_y", CONSTS)], inc="pe")
            gv = p.emit("vector", lambda e, t=t, bk=bk: e.tensor_copy(
                spr[0:8, t * 128:(t + 1) * 128], psb[bk][0:8, 0:128]),
                waits=[("pe", p_t)])
            bank_guard[bk] = ("dv", gv)
        p.emit("vector", lambda e: e.tensor_tensor_scan(
            spr[32:40, 0:GT], spr[0:8, 0:GT], spr[0:8, 0:GT], 0.0,
            ALU.add, ALU.bypass), waits=[("pe", SPRG[0])])
        p.emit("vector", lambda e: e.tensor_scalar(
            spr[32:40, 0:GT], spr[32:40, 0:GT], -1.0, None, ALU.add))
        v_dst = 0
        CHW = min(512, GT)
        for ch in range(GT // CHW):
            cs = slice(ch * CHW, (ch + 1) * CHW)
            bka = ch % 2
            bkb = 2 + ch % 2
            sa0, sa1 = bank_guard[bka]
            sb0, sb1 = bank_guard[bkb]
            p.emit("vector", lambda e, cs=cs, bka=bka: e.tensor_scalar(
                psb[bka][0:8, 0:CHW], spr[32:40, cs], float(CAP) - 0.5,
                None, ALU.is_le), waits=[(sa0, sa1)])
            p.emit("vector", lambda e, cs=cs, bka=bka, bkb=bkb: e.tensor_tensor(
                psb[bkb][0:8, 0:CHW], spr[0:8, cs], psb[bka][0:8, 0:CHW],
                ALU.mult), waits=[(sb0, sb1)])
            p.emit("vector", lambda e, cs=cs: e.tensor_scalar(
                spr[32:40, cs], spr[32:40, cs], ecst_sb[32:40, 0:1], None, ALU.add))
            p.emit("vector", lambda e, cs=cs, bkb=bkb: e.tensor_tensor(
                spr[32:40, cs], spr[32:40, cs], psb[bkb][0:8, 0:CHW], ALU.mult))
            p.emit("vector", lambda e, cs=cs, bkb=bkb: e.tensor_scalar(
                psb[bkb][0:8, 0:CHW], psb[bkb][0:8, 0:CHW], -BIG, BIG,
                ALU.mult, ALU.add))
            v_dst = p.emit("vector", lambda e, cs=cs, bkb=bkb: e.tensor_tensor(
                spr[32:40, cs], spr[32:40, cs], psb[bkb][0:8, 0:CHW], ALU.add))
            bank_guard[bka] = ("dv", v_dst)
            bank_guard[bkb] = ("dv", v_dst)
        usc_marks = []
        usc_slot_guard = compaction.slot_guard
        uwr_totals = (p.cnt["us0"], p.cnt["us1"])
        for t in range(TT):
            vi_marks = [0, 0]
            bk = t % 2
            s0, v0 = bank_guard[bk]
            p_t = p.emit("tensor", lambda e, t=t, bk=bk: e.transpose(
                psb[bk][:, 0:8], spr[32:40, t * 128:(t + 1) * 128], idn8_sb[32:40, 0:8]),
                waits=[("dv", v_dst), (s0, v0)], inc="pe")
            gv = p.emit("vector", lambda e, t=t, bk=bk: e.tensor_copy(
                tmp8b[:, t * 8:(t + 1) * 8], psb[bk][:, 0:8]),
                waits=[("pe", p_t)])
            bank_guard[bk] = ("dv", gv)
            p.emit("vector", lambda e, t=t: e.tensor_reduce(
                dcf[:, t * 8: t * 8 + 1], tmp8b[:, t * 8:(t + 1) * 8],
                axis=AX.X, op=ALU.min))
            p.emit("vector", lambda e, t=t: e.tensor_scalar(
                tmp8[:, t * 8:(t + 1) * 8], tmp8b[:, t * 8:(t + 1) * 8],
                dcf[:, t * 8: t * 8 + 1], BIG, ALU.is_le, ALU.mult))
            p.emit("vector", lambda e, t=t: e.tensor_tensor(
                tmp8[:, t * 8:(t + 1) * 8], tmp8[:, t * 8:(t + 1) * 8],
                tmp8b[:, t * 8:(t + 1) * 8], ALU.add))
            p.emit("vector", lambda e, t=t: e.tensor_reduce(
                dcf[:, t * 8 + 1: t * 8 + 2], tmp8[:, t * 8:(t + 1) * 8],
                axis=AX.X, op=ALU.min))
            for r_ in range(2):
                p.emit("vector", lambda e, t=t, r_=r_: e.tensor_scalar(
                    tmp8[:, t * 8:(t + 1) * 8], tmp8b[:, t * 8:(t + 1) * 8],
                    dcf[:, t * 8 + r_: t * 8 + r_ + 1], None, ALU.is_equal))
                p.emit("vector", lambda e, t=t: e.tensor_tensor(
                    tmp8[:, t * 8:(t + 1) * 8], tmp8[:, t * 8:(t + 1) * 8],
                    c_buf[:, t * 8:(t + 1) * 8], ALU.mult))
                p.emit("vector", lambda e, t=t, r_=r_: e.reduce_sum(
                    dcf[:, t * 8 + 6 + r_: t * 8 + 7 + r_],
                    tmp8[:, t * 8:(t + 1) * 8], axis=AX.X))
                p.emit("vector", lambda e, t=t, r_=r_: e.tensor_scalar(
                    tmp8[:, t * 8: t * 8 + 1], dcf[:, t * 8 + r_: t * 8 + r_ + 1],
                    float(8 * CAP) - 0.5, None, ALU.is_le))
                p.emit("vector", lambda e, t=t, r_=r_: e.tensor_tensor(
                    dcf[:, t * 8 + 6 + r_: t * 8 + 7 + r_],
                    dcf[:, t * 8 + 6 + r_: t * 8 + 7 + r_],
                    tmp8[:, t * 8: t * 8 + 1], ALU.mult))
                p.emit("vector", lambda e, t=t, r_=r_: e.tensor_tensor(
                    dcf[:, t * 8 + r_: t * 8 + r_ + 1],
                    dcf[:, t * 8 + r_: t * 8 + r_ + 1],
                    dmpc[:, 0:1], ALU.min))
                vi_marks[r_] = p.emit("vector", lambda e, t=t, r_=r_: e.tensor_copy(
                    dsti[:, t * 2 + r_: t * 2 + r_ + 1],
                    dcf[:, t * 8 + r_: t * 8 + r_ + 1]))
            slot = t % 2
            gv_l = p.emit("sync", lambda e, t=t, slot=slot: e.dma_start(
                out=arena[:, off_ug + slot * 2048: off_ug + slot * 2048 + 2048],
                in_=u_scr[t * 128:(t + 1) * 128, :]),
                waits=[(f"sc{slot}", usc_slot_guard[slot]),
                       ("us0", uwr_totals[0]), ("us1", uwr_totals[1]),
                       ("zin", ZINIT),
                       ("pe", ug_lastuse[slot])], inc=f"us{slot}", amt=16)
            for r_ in range(2):
                usc_marks.append(p.emit("gpsimd", lambda e, t=t, r_=r_, slot=slot: e.indirect_dma_start(
                    out=ug_dram,
                    out_offset=bass.IndirectOffsetOnAxis(dsti[:, t * 2 + r_: t * 2 + r_ + 1], 0),
                    in_=arena[:, off_ug + slot * 2048: off_ug + slot * 2048 + 2048],
                    in_offset=None),
                    waits=[("dv", vi_marks[r_]), (f"us{slot}", gv_l),
                           ("sc0", p.cnt["sc0"]), ("sc1", p.cnt["sc1"])],
                    inc=f"sc{slot}", amt=16))
            usc_slot_guard[slot] = p.cnt[f"sc{slot}"]
        USCT[0] = (p.cnt["us0"], p.cnt["us1"])
        return (("sc0", p.cnt["sc0"]), ("sc1", p.cnt["sc1"]))

    ug_lastuse = [0, 0]
    dstg_guard = [("dv", 0), ("dv", 0)]

    def sparse_expert(ex, usc_total, uT_ready, pair_ctr, gu_pre=None):
        gj_src = "gw"
        utg_ready = 0
        for ti, (base, rows) in enumerate(CTILES):
            slot = ti % 2
            gv = p.emit("scalar", lambda e, ex=ex, base=base, rows=rows, slot=slot: e.dma_start(
                out=arena[0:rows, off_ug + slot * 2048: off_ug + slot * 2048 + 2048],
                in_=ug_dram[ex * CAP + base: ex * CAP + base + rows, :]),
                waits=[usc_total[0], usc_total[1], (f"us{slot}", p.cnt[f"us{slot}"]),
                       ("pe", ug_lastuse[slot])], inc=f"us{slot}", amt=16)
            p_t = 0
            for c in range(KH):
                half = c // 8
                s0, v0 = ptb_guard[half]
                p_t = p.emit("tensor", lambda e, c=c, rows=rows, slot=slot, half=half: e.transpose(
                    ptb[half][:, (c % 8) * 128:(c % 8) * 128 + rows],
                    arena[0:rows, off_ug + slot * 2048 + c * 128: off_ug + slot * 2048 + (c + 1) * 128],
                    idn_sb[0:rows, 0:rows]),
                    waits=[(f"us{slot}", gv), (s0, v0), ("ld_y", CONSTS)],
                    inc="pe" if c % 8 == 7 else None)
                if c % 8 == 7:
                    hv = p.emit("vector", lambda e, c=c, base=base, rows=rows, half=half: e.tensor_copy(
                        arena[:, off_utg: off_utg + KH * CAP].rearrange(
                            "z (cc q) -> z cc q", q=CAP)[:, c - 7:c + 1, base:base + rows],
                        ptb[half][:, :].rearrange("z (cc q) -> z cc q", q=128)[:, :, 0:rows]),
                        waits=[("pe", p_t)])
                    ptb_guard[half] = ("dv", hv)
                    utg_ready = hv
            ug_lastuse[slot] = p_t
        wd_pre = {}
        for m in range(KI):
            gj = (ex * KI + m) % 2
            if gu_pre and m in gu_pre:
                wg, wu = gu_pre[m]
            else:
                wg = wdma("g", gj, "gw", ex * KI + m)
                wu = wdma("u", gj, "uw", ex * KI + m)
            emit_scr_piece()
            if m in (8, 9):
                n0 = m - 8
                wd_pre[n0] = wdma("d", (ex * 4 + n0) % 2, "dw", ex * 4 + n0)
            NH = CAP // 2
            pr = pair_ctr[0] % 3
            pair_ctr[0] += 1
            pu = pair_ctr[0] % 3
            pair_ctr[0] += 1
            bG = (2 * pr, 2 * pr + 1)
            bU = (2 * pu, 2 * pu + 1)
            gw0, gw1 = bank_guard[bG[0]], bank_guard[bG[1]]
            p_G = None
            for k in range(KH):
                for h in range(2):
                    last = (k == KH - 1 and h == 1)
                    p_G = p.emit("tensor", lambda e, k=k, h=h, gj=gj, bG=bG, NH=NH: e.matmul(
                        psb[bG[h]][:, 0:NH], a_view("g", gj, k * 128, 128),
                        arena[:, off_utg + k * CAP + h * NH: off_utg + k * CAP + (h + 1) * NH],
                        start=(k == 0), stop=(k == KH - 1)),
                        waits=[(f"ld_g{gj}", wg), ("dv", utg_ready), gw0, gw1],
                        inc="pe" if last else None)
            gw0, gw1 = bank_guard[bU[0]], bank_guard[bU[1]]
            p_U = None
            for k in range(KH):
                for h in range(2):
                    last = (k == KH - 1 and h == 1)
                    p_U = p.emit("tensor", lambda e, k=k, h=h, gj=gj, bU=bU, NH=NH: e.matmul(
                        psb[bU[h]][:, 0:NH], a_view("u", gj, k * 128, 128),
                        arena[:, off_utg + k * CAP + h * NH: off_utg + k * CAP + (h + 1) * NH],
                        start=(k == 0), stop=(k == KH - 1)),
                        waits=[(f"ld_u{gj}", wu), gw0, gw1],
                        inc="pe" if last else None)
            buf_lastuse[("g", gj)] = p_G
            buf_lastuse[("u", gj)] = p_U
            for h in range(2):
                slot = (m % 2) * 1024 + h * 512
                tsw = tsil_guard[m % 2]
                a_sil = p.emit("scalar", lambda e, h=h, slot=slot, bG=bG, NH=NH: e.activation(
                    tsil[:, slot: slot + NH], psb[bG[h]][:, 0:NH], AF.Silu),
                    waits=[("pe", p_G), ("dv", tsw), ("ld_y", CONSTS),
                           ("us1", LNUS[1])])
                bank_guard[bG[h]] = ("ac", a_sil)
                hv2 = m * CAP + h * NH
                dvv = p.emit("vector", lambda e, h=h, slot=slot, bU=bU, hv2=hv2, NH=NH: e.tensor_tensor(
                    arena[:, off_h + hv2: off_h + hv2 + NH], psb[bU[h]][:, 0:NH],
                    tsil[:, slot: slot + NH], ALU.mult),
                    waits=[("pe", p_U), ("ac", a_sil)])
                bank_guard[bU[h]] = ("dv", dvv)
                tsil_guard[m % 2] = dvv
        hbuf_ready = p.cnt["dv"]
        for n_ in range(4):
            dj = (ex * 4 + n_) % 2
            wd = wd_pre[n_] if n_ in wd_pre else wdma("d", dj, "dw", ex * 4 + n_)
            p_dn_last = 0
            for ti, (base, rows) in enumerate(CTILES):
                pr = pair_ctr[0] % 3
                pair_ctr[0] += 1
                bks = (2 * pr, 2 * pr + 1)
                gw0, gw1 = bank_guard[bks[0]], bank_guard[bks[1]]
                p_dn = None
                for k in range(KI):
                    for h in range(2):
                        last = (k == KI - 1 and h == 1)
                        p_dn = p.emit("tensor", lambda e, k=k, h=h, base=base, rows=rows, dj=dj, bks=bks: e.matmul(
                            psb[bks[h]][0:rows, 0:256],
                            arena[:, off_h + k * CAP + base: off_h + k * CAP + base + rows],
                            a_view("d", dj, k * 512 + h * 256, 256),
                            start=(k == 0), stop=(k == KI - 1)),
                            waits=[(f"ld_d{dj}", wd), ("dv", hbuf_ready), gw0, gw1],
                            inc="pe" if last else None)
                p_dn_last = p_dn
                slot = (ex * 4 + n_ + ti) % 2
                sg0, sv0 = dstg_guard[slot]
                cva = p.emit("vector", lambda e, rows=rows, slot=slot, bks=bks: e.tensor_copy(
                    dstg[0:rows, slot * 512: slot * 512 + 256], psb[bks[0]][0:rows, 0:256]),
                    waits=[("pe", p_dn), (sg0, sv0)])
                cvb = p.emit("vector", lambda e, rows=rows, slot=slot, bks=bks: e.tensor_copy(
                    dstg[0:rows, slot * 512 + 256: slot * 512 + 512], psb[bks[1]][0:rows, 0:256]),
                    waits=[("pe", p_dn)])
                bank_guard[bks[0]] = ("dv", cva)
                bank_guard[bks[1]] = ("dv", cvb)
                gv = p.emit("sync", lambda e, ex=ex, base=base, rows=rows, n_=n_, slot=slot: e.dma_start(
                    out=ogd[ex * CAP + base: ex * CAP + base + rows, n_ * 512:(n_ + 1) * 512],
                    in_=dstg[0:rows, slot * 512: slot * 512 + 512]),
                    waits=[("dv", cvb)], inc=f"dg{slot}", amt=16)
                dstg_guard[slot] = (f"dg{slot}", gv)
            buf_lastuse[("d", dj)] = p_dn_last

    def shared_expert(hf, uT_ready, pair_ctr):
        fin_tiles = [0] * TT
        wd_pre = {}
        for m in range(KI):
            gj = ((8 + hf) * KI + m) % 2
            wg = wdma("g", gj, "sg", hf * KI + m)
            wu = wdma("u", gj, "su", hf * KI + m)
            emit_scr_piece()
            if m in (8, 9):
                n0 = m - 8
                wd_pre[n0] = wdma("d", n0 % 2, "sd", hf * 4 + n0)
            pr = pair_ctr[0] % 3
            pair_ctr[0] += 1
            pu = pair_ctr[0] % 3
            pair_ctr[0] += 1
            bG = (2 * pr, 2 * pr + 1)
            bU = (2 * pu, 2 * pu + 1)
            gw0, gw1 = bank_guard[bG[0]], bank_guard[bG[1]]
            p_G = None
            for k in range(KH):
                for h in range(2):
                    last = (k == KH - 1 and h == 1)
                    p_G = p.emit("tensor", lambda e, k=k, h=h, gj=gj, bG=bG: e.matmul(
                        psb[bG[h]][:, 0:HGT], a_view("g", gj, k * 128, 128),
                        uT[:, k * GT + h * HGT: k * GT + (h + 1) * HGT],
                        start=(k == 0), stop=(k == KH - 1)),
                        waits=[(f"ld_g{gj}", wg), uT_ready[0], uT_ready[1], gw0, gw1],
                        inc="pe" if last else None)
            gw0, gw1 = bank_guard[bU[0]], bank_guard[bU[1]]
            p_U = None
            for k in range(KH):
                for h in range(2):
                    last = (k == KH - 1 and h == 1)
                    p_U = p.emit("tensor", lambda e, k=k, h=h, gj=gj, bU=bU: e.matmul(
                        psb[bU[h]][:, 0:HGT], a_view("u", gj, k * 128, 128),
                        uT[:, k * GT + h * HGT: k * GT + (h + 1) * HGT],
                        start=(k == 0), stop=(k == KH - 1)),
                        waits=[(f"ld_u{gj}", wu), gw0, gw1],
                        inc="pe" if last else None)
            buf_lastuse[("g", gj)] = p_G
            buf_lastuse[("u", gj)] = p_U
            for h in range(2):
                slot = (m % 2) * 1024 + h * 512
                tsw = tsil_guard[m % 2]
                col = hf * KI + m
                a_sil = p.emit("scalar", lambda e, h=h, slot=slot, bG=bG, col=col: e.activation(
                    tsil[:, slot: slot + HGT], psb[bG[h]][:, 0:HGT], AF.Silu,
                    bias=bsg_sb[:, col: col + 1], scale=1.0),
                    waits=[("pe", p_G), ("dv", tsw), ("ld_y", CONSTS),
                           ("us1", LNUS[1])])
                bank_guard[bG[h]] = ("ac", a_sil)
                hv = m * GT + h * HGT
                dvv = p.emit("vector", lambda e, h=h, slot=slot, bU=bU, hv=hv, col=col: e.scalar_tensor_tensor(
                    arena[:, off_h + hv: off_h + hv + HGT], psb[bU[h]][:, 0:HGT],
                    bsu_sb[:, col: col + 1], tsil[:, slot: slot + HGT],
                    ALU.add, ALU.mult),
                    waits=[("pe", p_U), ("ac", a_sil), ("ld_y", CONSTS)])
                bank_guard[bU[h]] = ("dv", dvv)
                tsil_guard[m % 2] = dvv
        hbuf_ready = p.cnt["dv"]
        for n_ in range(4):
            dj = n_ % 2
            wd = wd_pre[n_] if n_ in wd_pre else wdma("d", dj, "sd", hf * 4 + n_)
            p_dn_last = 0
            for t in range(TT):
                pr = pair_ctr[0] % 3
                pair_ctr[0] += 1
                bks = (2 * pr, 2 * pr + 1)
                gw0, gw1 = bank_guard[bks[0]], bank_guard[bks[1]]
                p_dn = None
                for k in range(KI):
                    for h in range(2):
                        last = (k == KI - 1 and h == 1)
                        p_dn = p.emit("tensor", lambda e, k=k, h=h, t=t, dj=dj, bks=bks: e.matmul(
                            psb[bks[h]][:, 0:256],
                            arena[:, off_h + k * GT + t * 128: off_h + k * GT + t * 128 + 128],
                            a_view("d", dj, k * 512 + h * 256, 256),
                            start=(k == 0), stop=(k == KI - 1)),
                            waits=[(f"ld_d{dj}", wd), ("dv", hbuf_ready), gw0, gw1],
                            inc="pe" if last else None)
                p_dn_last = p_dn
                for h in range(2):
                    rcol = t * H + n_ * 512 + h * 256
                    dvv = p.emit("vector", lambda e, h=h, rcol=rcol, bks=bks: e.scalar_tensor_tensor(
                        rbuf[:, rcol: rcol + 256], psb[bks[h]][:, 0:256],
                        0.5, rbuf[:, rcol: rcol + 256], ALU.mult, ALU.add),
                        waits=[("pe", p_dn)])
                    bank_guard[bks[h]] = ("dv", dvv)
                fin_tiles[t] = p.cnt["dv"]
            buf_lastuse[("d", dj)] = p_dn_last
        return fin_tiles

    SPRG = [0]
    ACCG = [0]
    def ffn_phase(g, uT_ready, pair_ctr=[0]):
        gu_pre = {}
        for m in range(2):
            gu_pre[m] = (wdma("g", m % 2, "gw", m), wdma("u", m % 2, "uw", m))
        usc_total = compaction(g, BSDH_W)
        SPRG[0] = p.cnt["pe"]
        for ex in range(8):
            sparse_expert(ex, usc_total, uT_ready, pair_ctr,
                          gu_pre=gu_pre if ex == 0 else None)
        ogd_total = (p.cnt["dg0"], p.cnt["dg1"])
        for t in range(TT):
            for r_ in range(2):
                for n_ in range(4):
                    slot = (t * 8 + r_ * 4 + n_) % 2
                    sg0, sv0 = dstg_guard[slot]
                    gv = p.emit("gpsimd", lambda e, t=t, r_=r_, n_=n_, slot=slot: e.indirect_dma_start(
                        out=dstg[:, slot * 512: slot * 512 + 512],
                        out_offset=None,
                        in_=ogd,
                        in_offset=bass.IndirectOffsetOnAxis(dsti[:, t * 2 + r_: t * 2 + r_ + 1], 0),
                        element_offset=n_ * 512),
                        waits=[("dg0", ogd_total[0]), ("dg1", ogd_total[1]),
                               (sg0, sv0)], inc=f"gg{slot}", amt=16)
                    dvv = p.emit("vector", lambda e, t=t, r_=r_, n_=n_, slot=slot: e.scalar_tensor_tensor(
                        rbuf[:, t * H + n_ * 512: t * H + (n_ + 1) * 512],
                        dstg[:, slot * 512: slot * 512 + 512],
                        dcf[:, t * 8 + 6 + r_: t * 8 + 7 + r_],
                        rbuf[:, t * H + n_ * 512: t * H + (n_ + 1) * 512],
                        ALU.mult, ALU.add),
                        waits=[(f"gg{slot}", gv)])
                    dstg_guard[slot] = ("dv", dvv)
        for hf in range(2):
            fin_tiles = shared_expert(hf, uT_ready, pair_ctr)
        ug_lastuse[0] = max(ug_lastuse[0], p.cnt["pe"])
        ug_lastuse[1] = max(ug_lastuse[1], p.cnt["pe"])
        acc_marks = [0] * TT
        for t in range(TT):
            acc_marks[t] = p.emit("gpsimd", lambda e, t=t: e.dma_start(
                out=rbuf[:, t * H:(t + 1) * H], in_=bsdh, accum_op=ALU.add),
                waits=[("dv", fin_tiles[t]), ("bsx", BSDH_W)], inc="sct", amt=16)
        ACCG[0] = acc_marks[-1]
        return fin_tiles, [acc_marks[-1]] * TT

    def attn_phase(g, uT_ready):
        for sg_i in range(2):
            for m in range(32):
                wj = m % 3
                wv = wdma("wi", wj, "wi", m)
                pr = m % 3
                bks = (2 * pr, 2 * pr + 1)
                gg = [bank_guard[bks[0]], bank_guard[bks[1]]]
                p_xt = None
                for k in range(KH):
                    for q in range(2):
                        last = (k == KH - 1 and q == 1)
                        p_xt = p.emit("tensor", lambda e, k=k, q=q, wj=wj, bks=bks, sg_i=sg_i: e.matmul(
                            psb[bks[q]][:, 0:NQ],
                            a_view("wi", wj, k * 128, 128),
                            uT[:, k * GT + sg_i * SGT + q * NQ: k * GT + sg_i * SGT + (q + 1) * NQ],
                            start=(k == 0), stop=(k == KH - 1)),
                            waits=[(f"ld_wi{wj}", wv), uT_ready[0], uT_ready[1], gg[0], gg[1]],
                            inc="pe" if last else None)
                buf_lastuse[("wi", wj)] = p_xt
                for q in range(2):
                    gv = p.emit("vector", lambda e, m=m, q=q, bks=bks: e.tensor_copy(
                        arena[:, off_xT + m * SGT + q * NQ: off_xT + m * SGT + (q + 1) * NQ],
                        psb[bks[q]][:, 0:NQ]),
                        waits=[("pe", p_xt)])
                    bank_guard[bks[q]] = ("dv", gv)
            xT_ready = p.cnt["dv"]
            pB_last = 0
            for m in range(32):
                cj = m % 2
                zj = m % 3
                wc = wdma("cvb", cj, "cv", m)
                wz = wdma("wi", zj, "wi", 32 + m)
                zpr = (2 * m) % 3
                cpr = (2 * m + 1) % 3
                bz = (2 * zpr, 2 * zpr + 1)
                bc = (2 * cpr, 2 * cpr + 1)
                gg = [bank_guard[bz[0]], bank_guard[bz[1]]]
                p_z = None
                for k in range(KH):
                    for q in range(2):
                        last = (k == KH - 1 and q == 1)
                        p_z = p.emit("tensor", lambda e, k=k, q=q, zj=zj, bz=bz, sg_i=sg_i: e.matmul(
                            psb[bz[q]][:, 0:NQ],
                            a_view("wi", zj, k * 128, 128),
                            uT[:, k * GT + sg_i * SGT + q * NQ: k * GT + sg_i * SGT + (q + 1) * NQ],
                            start=(k == 0), stop=(k == KH - 1)),
                            waits=[(f"ld_wi{zj}", wz), gg[0], gg[1]],
                            inc="pe" if last else None)
                buf_lastuse[("wi", zj)] = p_z
                gg = [bank_guard[bc[0]], bank_guard[bc[1]]]
                p_xc = None
                for k in range(32):
                    for q in range(2):
                        last = (k == 31 and q == 1)
                        p_xc = p.emit("tensor", lambda e, k=k, q=q, cj=cj, bc=bc: e.matmul(
                            psb[bc[q]][:, 0:NQ],
                            a_view("cvb", cj, k * 128, 128),
                            arena[:, off_xT + k * SGT + q * NQ: off_xT + k * SGT + (q + 1) * NQ],
                            start=(k == 0), stop=(k == 31)),
                            waits=[(f"ld_cvb{cj}", wc), ("dv", xT_ready), gg[0], gg[1]],
                            inc="pe" if last else None)
                buf_lastuse[("cvb", cj)] = p_xc
                pB_last = p_xc
                slot = (m % 2) * SGT
                for q in range(2):
                    tsw = tsil_guard[m % 2]
                    a_z = p.emit("scalar", lambda e, slot=slot, q=q, bz=bz: e.activation(
                        tsil[:, slot + q * NQ: slot + (q + 1) * NQ],
                        psb[bz[q]][:, 0:NQ], AF.Silu),
                        waits=[("pe", p_z), ("dv", tsw), uT_ready[0], uT_ready[1],
                               ("us1", LNUS[1])])
                    bank_guard[bz[q]] = ("ac", a_z)
                    gv = p.emit("vector", lambda e, m=m, slot=slot, q=q, bc=bc: e.scalar_tensor_tensor(
                        arena[:, off_sb_ + m * SGT + q * NQ: off_sb_ + m * SGT + (q + 1) * NQ],
                        psb[bc[q]][:, 0:NQ], dvec_sb[:, m: m + 1],
                        tsil[:, slot + q * NQ: slot + (q + 1) * NQ], ALU.mult, ALU.mult),
                        waits=[("pe", p_xc), ("ac", a_z), ("ld_y", CONSTS)])
                    bank_guard[bc[q]] = ("dv", gv)
                    tsil_guard[m % 2] = gv
            sb_ready = p.cnt["dv"]
            wo_all = {}
            def emit_wo(ip):
                if ip <= 7:
                    wj = ip % 3
                    wo_all[ip] = (wj, wdma("wob", wj, "wo", ip,
                                           extra_pe_guard=pB_last))
            for ip in range(3):
                emit_wo(ip)
            for n in range(4):
                wvs = [wo_all[n * 2], wo_all[n * 2 + 1]]
                for t in range(STT):
                    pc = (n * STT + t) % 3
                    bks = (2 * pc, 2 * pc + 1)
                    gg = [bank_guard[bks[0]], bank_guard[bks[1]]]
                    p_o = None
                    for k in range(32):
                        kh = k // 16
                        wj, wv = wvs[kh]
                        for q in range(2):
                            last = (k == 31 and q == 1)
                            p_o = p.emit("tensor", lambda e, k=k, q=q, t=t, wj=wj, bks=bks: e.matmul(
                                psb[bks[q]][:, 0:256],
                                arena[:, off_sb_ + k * SGT + t * 128: off_sb_ + k * SGT + t * 128 + 128],
                                a_view("wob", wj, (k % 16) * 512 + q * 256, 256),
                                start=(k == 0), stop=(k == 31)),
                                waits=[(f"ld_wob{wj}", wv), ("dv", sb_ready), gg[0], gg[1]],
                                inc="pe" if last else None)
                    buf_lastuse[("wob", wvs[0][0])] = p_o
                    buf_lastuse[("wob", wvs[1][0])] = p_o
                    rcol = (sg_i * STT + t) * H + n * 512
                    for q in range(2):
                        gv = p.emit("vector", lambda e, rcol=rcol, q=q, bks=bks: e.tensor_tensor(
                            rbuf[:, rcol + q * 256: rcol + (q + 1) * 256],
                            psb[bks[q]][:, 0:256],
                            rbuf[:, rcol + q * 256: rcol + (q + 1) * 256], ALU.add),
                            waits=[("pe", p_o), ("sct", ACCG[0])])
                        bank_guard[bks[q]] = ("dv", gv)
                emit_wo(n * 2 + 3)
                emit_wo(n * 2 + 4)
            af_sg = p.cnt["pe"]
            for key in (("wi", 0), ("wi", 1), ("wi", 2), ("cvb", 0), ("cvb", 1)):
                buf_lastuse[key] = max(buf_lastuse[key], af_sg)
        af = p.cnt["pe"]
        for key in (("g", 0), ("g", 1), ("u", 0), ("u", 1), ("d", 0), ("d", 1)):
            buf_lastuse[key] = max(buf_lastuse[key], af)

    st_cnt = 0
    fin_tiles = [0] * TT
    for g in range(NG):
        NPIECE = min(4, TT)
        for piece in range(NPIECE):
            waits = []
            if g > 0 and piece == 0:
                waits.append(("st", st_cnt))
            if piece > 0:
                waits.append(("ld_x", (g * NPIECE + piece) * 16))
            p.emit("sync", lambda e, g=g, piece=piece, NPIECE=NPIECE: e.dma_start(
                out=rbuf[:, piece * (TT // NPIECE) * H:(piece + 1) * (TT // NPIECE) * H]
                    .rearrange("q (t h) -> q t h", h=H),
                in_=ap["x"][g * GT + piece * (GT // NPIECE): g * GT + (piece + 1) * (GT // NPIECE), :]
                    .rearrange("(t q) h -> q t h", q=128)),
                waits=waits, inc="ld_x", amt=16)
        for sub in range(3):
            SUBIDX[0] = sub
            uT_ready = ln_phase(g)
            if sub in (0, 2):
                router_phase(uT_ready)
                fin_tiles, acc_marks = ffn_phase(g, uT_ready)
            else:
                attn_phase(g, uT_ready)
        for piece in range(2):
            fin = max(fin_tiles[piece * (TT // 2): (piece + 1) * (TT // 2)])
            am = max(acc_marks[piece * (TT // 2): (piece + 1) * (TT // 2)])
            st_cnt = p.emit("sync", lambda e, g=g, piece=piece: e.dma_start(
                out=out_ap[g * GT + piece * (GT // 2): g * GT + (piece + 1) * (GT // 2), :]
                    .rearrange("(t q) h -> q t h", q=128),
                in_=rbuf[:, piece * (TT // 2) * H:(piece + 1) * (TT // 2) * H]
                    .rearrange("q (t h) -> q t h", h=H)),
                waits=[("dv", fin), ("sct", am)], inc="st", amt=16)
    p.emit("sync", None, waits=[("st", st_cnt)])

    with es:
        block = es.enter_context(nc.Block())
        for eng in Plan.ENGINES:
            ops = p.ops[eng]
            if not ops:
                continue
            def body(e, ops=ops):
                hwm = {}
                for waits, fn, inc, amt in ops:
                    for s, v in waits:
                        if hwm.get(s, 0) < v:
                            e.wait_ge(sems[s], v)
                            hwm[s] = v
                    if fn is not None:
                        ins = fn(e)
                        if inc is not None:
                            ins.then_inc(sems[inc], amt)
            getattr(block, eng)(body)
    return nc


def pack_inputs(i, GT=1024):
    import os as _os
    TT = GT // 128
    _capo = _os.environ.get("KCAP")
    CAP = int(_capo) if _capo else (320 if GT == 1024 else GT // 2)
    c = np.ascontiguousarray
    d = {}
    d["gw"] = c(np.asarray(i["L0_ffn_gate_w"]).reshape(8, 16, 128, 11, 128)
                .transpose(0, 3, 2, 1, 4).reshape(88, 128, 2048))
    d["uw"] = c(np.asarray(i["L0_ffn_up_w"]).reshape(8, 16, 128, 11, 128)
                .transpose(0, 3, 2, 1, 4).reshape(88, 128, 2048))
    d["dw"] = c(np.asarray(i["L0_ffn_down_w"]).reshape(8, 11, 128, 4, 512)
                .transpose(0, 3, 2, 1, 4).reshape(32, 128, 5632))
    d["sg"] = c(np.asarray(i["L0_ffn_s_gate_w"]).reshape(16, 128, 2, 11, 128)
                .transpose(2, 3, 1, 0, 4).reshape(22, 128, 2048))
    d["su"] = c(np.asarray(i["L0_ffn_s_up_w"]).reshape(16, 128, 2, 11, 128)
                .transpose(2, 3, 1, 0, 4).reshape(22, 128, 2048))
    d["sd"] = c(np.asarray(i["L0_ffn_s_down_w"]).reshape(2, 11, 128, 4, 512)
                .transpose(0, 3, 2, 1, 4).reshape(8, 128, 5632))
    d["wi"] = c(np.asarray(i["L0_attn_W_in"]).reshape(64, 128, 16, 128)
                .transpose(0, 3, 2, 1).reshape(64, 128, 2048))
    d["cv"] = c(np.asarray(i["L0_attn_conv_w"]).reshape(32, 128, 32, 128)
                .transpose(0, 3, 2, 1).reshape(32, 128, 4096))
    d["wo"] = c(np.asarray(i["L0_attn_W_out"]).reshape(4, 512, 2, 16, 128)
                .transpose(0, 2, 4, 3, 1).reshape(8, 128, 8192))
    d["rw"] = c(np.asarray(i["L0_ffn_router_w"]).reshape(8, 16, 128)
                .transpose(2, 1, 0).reshape(128, 128))
    d["dvec"] = c(np.asarray(i["L0_attn_D"]).reshape(32, 128).T)
    d["bsg"] = c(np.asarray(i["L0_ffn_s_gate_b"]).reshape(22, 128).T)
    d["bsu"] = c(np.asarray(i["L0_ffn_s_up_b"]).reshape(22, 128).T)
    d["bsdb"] = c(np.broadcast_to(np.asarray(i["L0_ffn_s_down_b"]), (128, 2048)))
    d["idn"] = np.eye(128, dtype=BFNP)
    d["idnf"] = np.eye(128, dtype=np.float32)
    d["toki"] = c((np.arange(TT)[None, :] * 128
                   + np.arange(128)[:, None]).astype(np.float32))
    d["ecst"] = c(((np.arange(128) % 8) * CAP).astype(np.float32).reshape(128, 1))
    d["idn8"] = c((np.arange(128)[:, None] % 8 == np.arange(8)[None, :])
                  .astype(np.float32))
    return d


def kernel(**inputs):
    shared = pack_inputs(inputs)
    x = np.ascontiguousarray(np.asarray(inputs["L0_attn_hidden_states"],
                                        dtype=np.float32).reshape(16384, 2048))
    nc = build_program(GT=1024, TOKC=2048)
    in_maps = []
    for cidx in range(N_CORES):
        m = dict(shared)
        m["x"] = np.ascontiguousarray(x[cidx * 2048:(cidx + 1) * 2048])
        in_maps.append(m)
    res = run_bass_kernel_spmd(nc, in_maps, core_ids=list(range(N_CORES)))
    out = np.concatenate([np.asarray(r["out"]) for r in res.results], axis=0)
    return out.reshape(1, 16384, 2048).astype(np.float32)


# version 11
# speedup vs baseline: 1.4052x; 1.0286x over previous
import numpy as np
import ml_dtypes
from contextlib import ExitStack

import concourse.bass as bass
import concourse.mybir as mybir
from concourse.bass_utils import run_bass_kernel_spmd

F32 = mybir.dt.float32
BF16 = mybir.dt.bfloat16
AF = mybir.ActivationFunctionType
ALU = mybir.AluOpType
AX = mybir.AxisListType
BFNP = ml_dtypes.bfloat16
I32 = mybir.dt.int32

H = 2048
KH = H // 128
I_EXP = 1408
KI = I_EXP // 128
N_CORES = 8


class Plan:
    ENGINES = ("tensor", "vector", "scalar", "gpsimd", "sync")
    SELF = {"vector": "dv", "scalar": "ac"}

    def __init__(self):
        self.ops = {e: [] for e in self.ENGINES}
        self.cnt = {}

    def sem(self, name):
        self.cnt.setdefault(name, 0)

    def emit(self, eng, fn, waits=(), inc=None, amt=1):
        if eng in self.SELF and fn is not None and (inc is None or inc == self.SELF[eng]):
            s = self.SELF[eng]
            waits = tuple(waits) + ((s, self.cnt[s]),)
            if inc is None:
                inc, amt = s, 1
        for s, v in waits:
            assert s in self.cnt, s
            assert v <= self.cnt[s], ("wait ahead of plan", s, v, self.cnt[s])
        self.ops[eng].append((tuple(waits), fn, inc, amt))
        if inc is not None:
            self.cnt[inc] += amt
            return self.cnt[inc]
        return None


def build_program(GT=1024, TOKC=2048):
    NG = TOKC // GT
    TT = GT // 128
    SGT = GT // 2
    STT = TT // 2
    NQ = GT // 4
    HGT = GT // 2

    nc = bass.Bass("TRN2", target_bir_lowering=False)
    ap = {}
    def din(name, shape, dt=F32):
        ap[name] = nc.dram_tensor(name, shape, dt, kind="ExternalInput").ap()
    din("x", [TOKC, H])
    din("gw", [88, 128, KH * 128])
    din("uw", [88, 128, KH * 128])
    din("dw", [32, 128, KI * 512])
    din("sg", [22, 128, KH * 128])
    din("su", [22, 128, KH * 128])
    din("sd", [8, 128, KI * 512])
    din("wi", [64, 128, KH * 128])
    din("cv", [32, 128, 32 * 128])
    din("wo", [8, 128, 16 * 512])
    din("rw", [128, 128])
    din("dvec", [128, 32])
    din("bsg", [128, 22])
    din("bsu", [128, 22])
    din("bsdb", [128, H])
    din("idn", [128, 128], BF16)
    din("idnf", [128, 128])
    din("toki", [128, TT])
    din("ecst", [128, 1])
    din("idn8", [128, 8])
    out_ap = nc.dram_tensor("out", [TOKC, H], F32, kind="ExternalOutput").ap()
    import os as _os
    _capo = _os.environ.get("KCAP")
    CAP = int(_capo) if _capo else (320 if GT == 1024 else GT // 2)
    CTILES = []
    _b = 0
    while _b < CAP:
        CTILES.append((_b, min(128, CAP - _b)))
        _b += 128
    u_scr = nc.dram_tensor("u_scr", [GT, H], BF16, kind="Internal").ap()
    ug_dram = nc.dram_tensor("ug_dram", [8 * CAP + 128, H], BF16, kind="Internal").ap()
    ogd = nc.dram_tensor("ogd", [8 * CAP + 128, H], BF16, kind="Internal").ap()
    bsdh = nc.dram_tensor("bsdh", [128, H], F32, kind="Internal").ap()
    scr = {}
    scr["wi"] = nc.dram_tensor("scr_wi", [64, 128, KH * 128], BF16, kind="Internal").ap()
    scr["cv"] = nc.dram_tensor("scr_cv", [32, 128, 32 * 128], BF16, kind="Internal").ap()
    scr["wo"] = nc.dram_tensor("scr_wo", [8, 128, 16 * 512], BF16, kind="Internal").ap()

    p = Plan()
    for s in ("pe", "dv", "ac", "st", "ld_y", "ld_x", "ld_c", "scr_c", "ga", "sct", "bsx", "us0", "us1", "sc0", "sc1", "dg0", "dg1", "gg0", "gg1", "zin"):
        p.sem(s)
    BUFN = (("g", 2), ("u", 2), ("d", 2), ("wi", 3), ("cvb", 2), ("wob", 3))
    for pref, n in BUFN:
        for i in range(n):
            p.sem(f"ld_{pref}{i}")

    es = ExitStack()
    def sbuf(name, cols, dt):
        return es.enter_context(nc.sbuf_tensor(name, [128, cols], dt))

    rbuf = sbuf("rbuf", TT * H, F32)
    uT = sbuf("uT", KH * GT, BF16)
    ubf = sbuf("ubf", H, BF16)
    tsil = sbuf("tsil", 2048, BF16)
    spr = sbuf("spr", 1024, F32)
    dstg = sbuf("dstg", 1024, BF16)
    dcf = sbuf("dcf", 8 * TT, F32)
    dsti = sbuf("dsti", 2 * TT, I32)
    dmpc = sbuf("dmpc", 1, F32)
    stats = sbuf("stats", 8 * TT, F32)
    lgsb = sbuf("lgsb", 8 * TT, F32)
    esb = sbuf("esb", 8 * TT, F32)
    tmp8 = sbuf("tmp8", 8 * TT, F32)
    tmp8b = sbuf("tmp8b", 8 * TT, F32)
    st2 = sbuf("st2", 8 * TT, F32)
    c_buf = sbuf("c_buf", 8 * TT, F32)
    rw_bf = sbuf("rw_bf", 128, BF16)
    idn_sb = sbuf("idn_sb", 128, BF16)
    idnf_sb = sbuf("idnf_sb", 128, F32)
    toki_sb = sbuf("toki_sb", TT, F32)
    ecst_sb = sbuf("ecst_sb", 1, F32)
    idn8_sb = sbuf("idn8_sb", 8, F32)
    dvec_sb = sbuf("dvec_sb", 32, F32)
    bsg_sb = sbuf("bsg_sb", 22, F32)
    bsu_sb = sbuf("bsu_sb", 22, F32)

    off_h = 0
    off_utg = KI * CAP
    sz_h = max(KI * GT, off_utg + KH * CAP)
    off_g = [sz_h + j * 2048 for j in range(2)]
    off_u = [sz_h + 2 * 2048 + j * 2048 for j in range(2)]
    off_d = [sz_h + 4 * 2048 + j * (KI * 512) for j in range(2)]
    ffn_end = sz_h + 4 * 2048 + 2 * KI * 512
    off_xT = 0
    off_sb_ = 32 * SGT
    off_wi = [max(ffn_end, 64 * SGT) + j * 2048 for j in range(3)]
    off_cv = [off_wi[0] + 3 * 2048 + j * 4096 for j in range(2)]
    attn_end = off_cv[0] + 2 * 4096
    if GT == 1024:
        off_wo = [0, 8192, off_wi[0]]
        arena_cols = attn_end
    else:
        off_wo = [attn_end + j * 8192 for j in range(3)]
        arena_cols = attn_end + 3 * 8192
    off_ug = off_wi[0]
    arena = sbuf("arena", arena_cols, BF16)

    psb = [es.enter_context(nc.psum_tensor(f"pb{i}", [128, 512], F32)) for i in range(6)]
    ptb = [es.enter_context(nc.psum_tensor(f"pt{i}", [128, 1024], BF16)) for i in range(2)]
    sems = {}
    for name in p.cnt:
        sems[name] = es.enter_context(nc.semaphore(name))

    bank_guard = {i: ("dv", 0) for i in range(6)}
    ptb_guard = [("dv", 0), ("dv", 0)]
    buf_lastuse = {}
    buf_gen = {}
    for pref, n in BUFN:
        for j in range(n):
            buf_lastuse[(pref, j)] = 0
            buf_gen[(pref, j)] = 0
    tsil_guard = {0: 0, 1: 0}
    CONSTS = 8 * 16
    SCRT = [0]

    p.emit("sync", lambda e: e.dma_start(out=dvec_sb[:, :], in_=ap["dvec"]), inc="ld_y", amt=16)
    p.emit("sync", lambda e: e.dma_start(out=bsg_sb[:, :], in_=ap["bsg"]), inc="ld_y", amt=16)
    p.emit("sync", lambda e: e.dma_start(out=bsu_sb[:, :], in_=ap["bsu"]), inc="ld_y", amt=16)
    p.emit("sync", lambda e: e.dma_start(out=idn_sb[:, :], in_=ap["idn"]), inc="ld_y", amt=16)
    p.emit("sync", lambda e: e.dma_start(out=idnf_sb[:, :], in_=ap["idnf"]), inc="ld_y", amt=16)
    p.emit("sync", lambda e: e.dma_start(out=toki_sb[:, :], in_=ap["toki"]), inc="ld_y", amt=16)
    p.emit("sync", lambda e: e.dma_start(out=ecst_sb[:, :], in_=ap["ecst"]), inc="ld_y", amt=16)
    p.emit("sync", lambda e: e.dma_start(out=idn8_sb[:, :], in_=ap["idn8"]), inc="ld_y", amt=16)
    p.emit("gpsimd", lambda e: e.dma_start(out=rw_bf[:, :], in_=ap["rw"]), inc="ld_c", amt=16)
    scr_jobs = []
    for name, nblk, step in (("wi", 64, 2), ("cv", 32, 1), ("wo", 8, 1)):
        for a in range(0, nblk, step):
            scr_jobs.append((name, a, min(a + step, nblk)))
    SCRT[0] = len(scr_jobs) * 16

    def emit_scr_piece():
        if scr_jobs:
            name, a, bnd = scr_jobs.pop(0)
            p.emit("gpsimd", lambda e, name=name, a=a, bnd=bnd: e.dma_start(
                out=scr[name][a:bnd], in_=ap[name][a:bnd]), inc="scr_c", amt=16)

    p.emit("vector", lambda e: e.tensor_scalar(
        dmpc[:, 0:1], toki_sb[:, 0:1], float(8 * CAP), None, ALU.add),
        waits=[("ld_y", CONSTS)])
    BSDH_W = 0
    for hh in range(2):
        lv = p.emit("sync", lambda e, hh=hh: e.dma_start(
            out=spr[:, 0:1024], in_=ap["bsdb"][:, hh * 1024:(hh + 1) * 1024]),
            waits=[("bsx", BSDH_W)], inc="bsx", amt=16)
        v0 = p.emit("vector", lambda e: e.tensor_scalar(
            spr[:, 0:1024], spr[:, 0:1024], 0.5, None, ALU.mult),
            waits=[("bsx", lv)])
        BSDH_W = p.emit("sync", lambda e, hh=hh: e.dma_start(
            out=bsdh[:, hh * 1024:(hh + 1) * 1024], in_=spr[:, 0:1024]),
            waits=[("dv", v0)], inc="bsx", amt=16)

    vz = p.emit("vector", lambda e: e.memset(
        arena[:, off_ug: off_ug + 2048], 0.0))
    zr = 0
    nrow = 8 * CAP + 128
    a0 = 0
    while a0 < nrow:
        b0 = min(a0 + 128, nrow)
        for tgt in (ug_dram, ogd):
            zr = p.emit("sync", lambda e, a0=a0, b0=b0, tgt=tgt: e.dma_start(
                out=tgt[a0:b0, :], in_=arena[0:b0 - a0, off_ug: off_ug + 2048]),
                waits=[("dv", vz)], inc="zin", amt=16)
        a0 += 128
    ZINIT = zr

    WQ = {"g": "gpsimd", "u": "gpsimd", "d": "gpsimd",
          "wi": "sync", "cvb": "sync", "wob": "sync"}
    WSRC_SCR = {"wi": "wi", "cvb": "cv", "wob": "wo"}
    OFFS = {"g": off_g, "u": off_u, "d": off_d, "wi": off_wi, "cvb": off_cv, "wob": off_wo}
    NCOLS = {"g": 2048, "u": 2048, "d": KI * 512, "wi": 2048, "cvb": 4096, "wob": 8192}

    def wdma(stream, j, src_name, blk, extra_pe_guard=0):
        gen = buf_gen[(stream, j)] = buf_gen[(stream, j)] + 1
        guard = max(buf_lastuse[(stream, j)], extra_pe_guard)
        o = OFFS[stream][j]
        ncols = NCOLS[stream]
        waits = []
        if guard:
            waits.append(("pe", guard))
        if stream in WSRC_SCR:
            waits.append(("scr_c", SCRT[0]))
            srcap = scr[WSRC_SCR[stream]]
        else:
            srcap = ap[src_name]
        p.emit(WQ[stream],
               lambda e, o=o, n=ncols, s=srcap, b=blk: e.dma_start(
                   out=arena[:, o:o + n], in_=s[b]),
               waits=waits, inc=f"ld_{stream}{j}", amt=16)
        return gen * 16

    def a_view(stream, j, lo, n):
        o = OFFS[stream][j]
        return arena[:, o + lo: o + lo + n]

    SUBIDX = [0]
    lnbuf = [ubf, tsil]
    lnbuf_tr_guard = [0, 0]
    LNUS = [0, 0]
    USCT = [(0, 0)]
    def ln_phase(g):
        acg = p.cnt["ac"]
        dvg = p.cnt["dv"]
        NPIECE = min(4, TT)
        TPP = TT // NPIECE
        for t in range(TT):
            par = t % 2
            buf = lnbuf[par]
            w0 = [("dv", dvg), ("pe", lnbuf_tr_guard[par]),
                  (f"us{par}", LNUS[par])]
            if SUBIDX[0] == 0:
                w0.append(("ld_x", (g * NPIECE + t // TPP + 1) * 16))
            a_sum = p.emit("scalar", lambda e, t=t, buf=buf: e.activation(
                buf[:, :], rbuf[:, t * H:(t + 1) * H], AF.Copy,
                accum_out=stats[:, t * 8: t * 8 + 1]),
                waits=w0 + [("sct", ACCG[0])])
            v_negm = p.emit("vector", lambda e, t=t: e.tensor_scalar(
                stats[:, t * 8 + 1: t * 8 + 2], stats[:, t * 8: t * 8 + 1],
                -1.0 / H, None, ALU.mult), waits=[("ac", a_sum)])
            a_sq = p.emit("scalar", lambda e, t=t, buf=buf: e.activation(
                buf[:, :], rbuf[:, t * H:(t + 1) * H], AF.Square,
                bias=stats[:, t * 8 + 1: t * 8 + 2], scale=1.0,
                accum_out=stats[:, t * 8 + 2: t * 8 + 3]),
                waits=[("dv", v_negm), (f"us{t % 2}", LNUS[t % 2])])
            v_var = p.emit("vector", lambda e, t=t: e.tensor_scalar(
                stats[:, t * 8 + 3: t * 8 + 4], stats[:, t * 8 + 2: t * 8 + 3],
                1.0 / H, 1e-5, ALU.mult, ALU.add), waits=[("ac", a_sq)])
            a_sd = p.emit("scalar", lambda e, t=t: e.activation(
                stats[:, t * 8 + 4: t * 8 + 5], stats[:, t * 8 + 3: t * 8 + 4],
                AF.Sqrt), waits=[("dv", v_var)])
            p.emit("vector", lambda e, t=t: e.reciprocal(
                stats[:, t * 8 + 5: t * 8 + 6], stats[:, t * 8 + 4: t * 8 + 5]),
                waits=[("ac", a_sd)])
            v_ubf = p.emit("vector", lambda e, t=t, buf=buf: e.tensor_scalar(
                buf[:, :], rbuf[:, t * H:(t + 1) * H], stats[:, t * 8 + 1: t * 8 + 2],
                stats[:, t * 8 + 5: t * 8 + 6], ALU.add, ALU.mult),
                waits=[("pe", lnbuf_tr_guard[par]), ("ac", acg)])
            if SUBIDX[0] != 1:
                LNUS[par] = p.emit("sync", lambda e, t=t, buf=buf: e.dma_start(
                    out=u_scr[t * 128:(t + 1) * 128, :], in_=buf[:, :]),
                    waits=[("dv", v_ubf), ("us0", USCT[0][0]), ("us1", USCT[0][1])],
                    inc=f"us{par}", amt=16)
            p_tr = [0, 0]
            for half in range(2):
                s0, v0 = ptb_guard[half]
                for c in range(8):
                    cc = half * 8 + c
                    p_tr[half] = p.emit("tensor", lambda e, cc=cc, half=half, c=c, buf=buf: e.transpose(
                        ptb[half][:, c * 128:(c + 1) * 128],
                        buf[:, cc * 128:(cc + 1) * 128], idn_sb[:, :]),
                        waits=[("dv", v_ubf), (s0, v0), ("ld_y", CONSTS)],
                        inc="pe" if c == 7 else None)
            for half in range(2):
                gv = p.emit("vector", lambda e, half=half, t=t: e.tensor_copy(
                    uT[:, :].rearrange("z (c g) -> z c g", g=GT)[:, half * 8:half * 8 + 8,
                                                                t * 128:t * 128 + 128],
                    ptb[half][:, :].rearrange("z (c g) -> z c g", g=128)),
                    waits=[("pe", p_tr[half])])
                ptb_guard[half] = ("dv", gv)
            lnbuf_tr_guard[par] = p_tr[1]
            if par == 1:
                tsil_guard[0] = max(tsil_guard[0], v_ubf)
                tsil_guard[1] = max(tsil_guard[1], v_ubf)
        return (("pe", p.cnt["pe"]), ("dv", p.cnt["dv"]))

    def router_phase(uT_ready):
        for t in range(TT):
            bk = t % 2
            s0, v0 = bank_guard[bk]
            ins_l = None
            for c in range(KH):
                ins_l = p.emit("tensor", lambda e, c=c, t=t, bk=bk: e.matmul(
                    psb[bk][:, 0:8], uT[:, c * GT + t * 128: c * GT + t * 128 + 128],
                    rw_bf[:, c * 8:(c + 1) * 8], start=(c == 0), stop=(c == KH - 1)),
                    waits=[uT_ready[0], uT_ready[1], (s0, v0), ("ld_c", 16)],
                    inc="pe" if c == KH - 1 else None)
            p_lg = ins_l
            gv = p.emit("vector", lambda e, t=t, bk=bk: e.tensor_copy(
                lgsb[:, t * 8:(t + 1) * 8], psb[bk][:, 0:8]),
                waits=[("pe", p_lg)])
            bank_guard[bk] = ("dv", gv)
            p.emit("vector", lambda e, t=t: e.reduce_max(
                st2[:, t * 8: t * 8 + 1], lgsb[:, t * 8:(t + 1) * 8], axis=AX.X))
            v_ngm = p.emit("vector", lambda e, t=t: e.tensor_scalar(
                st2[:, t * 8 + 1: t * 8 + 2], st2[:, t * 8: t * 8 + 1],
                -1.0, None, ALU.mult))
            a_exp = p.emit("scalar", lambda e, t=t: e.activation(
                esb[:, t * 8:(t + 1) * 8], lgsb[:, t * 8:(t + 1) * 8], AF.Exp,
                bias=st2[:, t * 8 + 1: t * 8 + 2], scale=1.0),
                waits=[("dv", v_ngm)])
            p.emit("vector", lambda e, t=t: e.tensor_scalar(
                tmp8[:, t * 8:(t + 1) * 8], esb[:, t * 8:(t + 1) * 8],
                1.0, -2.0, ALU.is_ge, ALU.mult), waits=[("ac", a_exp)])
            p.emit("vector", lambda e, t=t: e.tensor_tensor(
                tmp8[:, t * 8:(t + 1) * 8], tmp8[:, t * 8:(t + 1) * 8],
                esb[:, t * 8:(t + 1) * 8], ALU.add))
            v_m2 = p.emit("vector", lambda e, t=t: e.reduce_max(
                st2[:, t * 8 + 2: t * 8 + 3], tmp8[:, t * 8:(t + 1) * 8], axis=AX.X))
            p.emit("vector", lambda e, t=t: e.tensor_scalar(
                st2[:, t * 8 + 3: t * 8 + 4], st2[:, t * 8 + 2: t * 8 + 3],
                1.0, None, ALU.add))
            v_inv = p.emit("vector", lambda e, t=t: e.reciprocal(
                st2[:, t * 8 + 4: t * 8 + 5], st2[:, t * 8 + 3: t * 8 + 4]))
            p.emit("vector", lambda e, t=t: e.tensor_scalar(
                tmp8b[:, t * 8:(t + 1) * 8], esb[:, t * 8:(t + 1) * 8],
                st2[:, t * 8 + 2: t * 8 + 3], None, ALU.is_ge),
                waits=[("dv", v_m2)])
            p.emit("vector", lambda e, t=t: e.tensor_tensor(
                tmp8b[:, t * 8:(t + 1) * 8], tmp8b[:, t * 8:(t + 1) * 8],
                esb[:, t * 8:(t + 1) * 8], ALU.mult))
            p.emit("vector", lambda e, t=t: e.tensor_scalar(
                c_buf[:, t * 8:(t + 1) * 8], tmp8b[:, t * 8:(t + 1) * 8],
                st2[:, t * 8 + 4: t * 8 + 5], 0.5, ALU.mult, ALU.mult),
                waits=[("dv", v_inv)])

    BIG = 60000.0

    def compaction(g, spr_guard):
        if not hasattr(compaction, "slot_guard"):
            compaction.slot_guard = [0, 0]
        for t in range(TT):
            v_m = p.emit("vector", lambda e, t=t: e.tensor_scalar(
                tmp8b[:, t * 8:(t + 1) * 8], c_buf[:, t * 8:(t + 1) * 8],
                0.0, None, ALU.is_gt),
                waits=[("bsx", spr_guard), ("pe", SPRG[0])])
            bk = t % 2
            s0, v0 = bank_guard[bk]
            p_t = p.emit("tensor", lambda e, t=t, bk=bk: e.transpose(
                psb[bk][0:8, 0:128], tmp8b[:, t * 8:(t + 1) * 8], idnf_sb[:, :]),
                waits=[("dv", v_m), (s0, v0), ("ld_y", CONSTS)], inc="pe")
            gv = p.emit("vector", lambda e, t=t, bk=bk: e.tensor_copy(
                spr[0:8, t * 128:(t + 1) * 128], psb[bk][0:8, 0:128]),
                waits=[("pe", p_t)])
            bank_guard[bk] = ("dv", gv)
        p.emit("vector", lambda e: e.tensor_tensor_scan(
            spr[32:40, 0:GT], spr[0:8, 0:GT], spr[0:8, 0:GT], 0.0,
            ALU.add, ALU.bypass), waits=[("pe", SPRG[0])])
        p.emit("vector", lambda e: e.tensor_scalar(
            spr[32:40, 0:GT], spr[32:40, 0:GT], -1.0, None, ALU.add))
        v_dst = 0
        CHW = min(512, GT)
        for ch in range(GT // CHW):
            cs = slice(ch * CHW, (ch + 1) * CHW)
            bka = ch % 2
            bkb = 2 + ch % 2
            sa0, sa1 = bank_guard[bka]
            sb0, sb1 = bank_guard[bkb]
            p.emit("vector", lambda e, cs=cs, bka=bka: e.tensor_scalar(
                psb[bka][0:8, 0:CHW], spr[32:40, cs], float(CAP) - 0.5,
                None, ALU.is_le), waits=[(sa0, sa1)])
            p.emit("vector", lambda e, cs=cs, bka=bka, bkb=bkb: e.tensor_tensor(
                psb[bkb][0:8, 0:CHW], spr[0:8, cs], psb[bka][0:8, 0:CHW],
                ALU.mult), waits=[(sb0, sb1)])
            p.emit("vector", lambda e, cs=cs: e.tensor_scalar(
                spr[32:40, cs], spr[32:40, cs], ecst_sb[32:40, 0:1], None, ALU.add))
            p.emit("vector", lambda e, cs=cs, bkb=bkb: e.tensor_tensor(
                spr[32:40, cs], spr[32:40, cs], psb[bkb][0:8, 0:CHW], ALU.mult))
            p.emit("vector", lambda e, cs=cs, bkb=bkb: e.tensor_scalar(
                psb[bkb][0:8, 0:CHW], psb[bkb][0:8, 0:CHW], -BIG, BIG,
                ALU.mult, ALU.add))
            v_dst = p.emit("vector", lambda e, cs=cs, bkb=bkb: e.tensor_tensor(
                spr[32:40, cs], spr[32:40, cs], psb[bkb][0:8, 0:CHW], ALU.add))
            bank_guard[bka] = ("dv", v_dst)
            bank_guard[bkb] = ("dv", v_dst)
        usc_marks = []
        usc_slot_guard = compaction.slot_guard
        uwr_totals = (p.cnt["us0"], p.cnt["us1"])
        for t in range(TT):
            vi_marks = [0, 0]
            bk = t % 2
            s0, v0 = bank_guard[bk]
            p_t = p.emit("tensor", lambda e, t=t, bk=bk: e.transpose(
                psb[bk][:, 0:8], spr[32:40, t * 128:(t + 1) * 128], idn8_sb[32:40, 0:8]),
                waits=[("dv", v_dst), (s0, v0)], inc="pe")
            gv = p.emit("vector", lambda e, t=t, bk=bk: e.tensor_copy(
                tmp8b[:, t * 8:(t + 1) * 8], psb[bk][:, 0:8]),
                waits=[("pe", p_t)])
            bank_guard[bk] = ("dv", gv)
            p.emit("vector", lambda e, t=t: e.tensor_reduce(
                dcf[:, t * 8: t * 8 + 1], tmp8b[:, t * 8:(t + 1) * 8],
                axis=AX.X, op=ALU.min))
            p.emit("vector", lambda e, t=t: e.tensor_scalar(
                tmp8[:, t * 8:(t + 1) * 8], tmp8b[:, t * 8:(t + 1) * 8],
                dcf[:, t * 8: t * 8 + 1], BIG, ALU.is_le, ALU.mult))
            p.emit("vector", lambda e, t=t: e.tensor_tensor(
                tmp8[:, t * 8:(t + 1) * 8], tmp8[:, t * 8:(t + 1) * 8],
                tmp8b[:, t * 8:(t + 1) * 8], ALU.add))
            p.emit("vector", lambda e, t=t: e.tensor_reduce(
                dcf[:, t * 8 + 1: t * 8 + 2], tmp8[:, t * 8:(t + 1) * 8],
                axis=AX.X, op=ALU.min))
            for r_ in range(2):
                p.emit("vector", lambda e, t=t, r_=r_: e.tensor_scalar(
                    tmp8[:, t * 8:(t + 1) * 8], tmp8b[:, t * 8:(t + 1) * 8],
                    dcf[:, t * 8 + r_: t * 8 + r_ + 1], None, ALU.is_equal))
                p.emit("vector", lambda e, t=t: e.tensor_tensor(
                    tmp8[:, t * 8:(t + 1) * 8], tmp8[:, t * 8:(t + 1) * 8],
                    c_buf[:, t * 8:(t + 1) * 8], ALU.mult))
                p.emit("vector", lambda e, t=t, r_=r_: e.reduce_sum(
                    dcf[:, t * 8 + 6 + r_: t * 8 + 7 + r_],
                    tmp8[:, t * 8:(t + 1) * 8], axis=AX.X))
                p.emit("vector", lambda e, t=t, r_=r_: e.tensor_scalar(
                    tmp8[:, t * 8: t * 8 + 1], dcf[:, t * 8 + r_: t * 8 + r_ + 1],
                    float(8 * CAP) - 0.5, None, ALU.is_le))
                p.emit("vector", lambda e, t=t, r_=r_: e.tensor_tensor(
                    dcf[:, t * 8 + 6 + r_: t * 8 + 7 + r_],
                    dcf[:, t * 8 + 6 + r_: t * 8 + 7 + r_],
                    tmp8[:, t * 8: t * 8 + 1], ALU.mult))
                p.emit("vector", lambda e, t=t, r_=r_: e.tensor_tensor(
                    dcf[:, t * 8 + r_: t * 8 + r_ + 1],
                    dcf[:, t * 8 + r_: t * 8 + r_ + 1],
                    dmpc[:, 0:1], ALU.min))
                vi_marks[r_] = p.emit("vector", lambda e, t=t, r_=r_: e.tensor_copy(
                    dsti[:, t * 2 + r_: t * 2 + r_ + 1],
                    dcf[:, t * 8 + r_: t * 8 + r_ + 1]))
            slot = t % 2
            gv_l = p.emit("sync", lambda e, t=t, slot=slot: e.dma_start(
                out=arena[:, off_ug + slot * 2048: off_ug + slot * 2048 + 2048],
                in_=u_scr[t * 128:(t + 1) * 128, :]),
                waits=[(f"sc{slot}", usc_slot_guard[slot]),
                       ("us0", uwr_totals[0]), ("us1", uwr_totals[1]),
                       ("zin", ZINIT),
                       ("pe", ug_lastuse[slot])], inc=f"us{slot}", amt=16)
            for r_ in range(2):
                usc_marks.append(p.emit("gpsimd", lambda e, t=t, r_=r_, slot=slot: e.indirect_dma_start(
                    out=ug_dram,
                    out_offset=bass.IndirectOffsetOnAxis(dsti[:, t * 2 + r_: t * 2 + r_ + 1], 0),
                    in_=arena[:, off_ug + slot * 2048: off_ug + slot * 2048 + 2048],
                    in_offset=None),
                    waits=[("dv", vi_marks[r_]), (f"us{slot}", gv_l),
                           ("sc0", p.cnt["sc0"]), ("sc1", p.cnt["sc1"])],
                    inc=f"sc{slot}", amt=16))
            usc_slot_guard[slot] = p.cnt[f"sc{slot}"]
        USCT[0] = (p.cnt["us0"], p.cnt["us1"])
        return (("sc0", p.cnt["sc0"]), ("sc1", p.cnt["sc1"]))

    ug_lastuse = [0, 0]
    dstg_guard = [("dv", 0), ("dv", 0)]

    def sparse_expert(ex, usc_total, uT_ready, pair_ctr, gu_pre=None):
        gj_src = "gw"
        utg_ready = 0
        for ti, (base, rows) in enumerate(CTILES):
            slot = ti % 2
            gv = p.emit("scalar", lambda e, ex=ex, base=base, rows=rows, slot=slot: e.dma_start(
                out=arena[0:rows, off_ug + slot * 2048: off_ug + slot * 2048 + 2048],
                in_=ug_dram[ex * CAP + base: ex * CAP + base + rows, :]),
                waits=[usc_total[0], usc_total[1], (f"us{slot}", p.cnt[f"us{slot}"]),
                       ("pe", ug_lastuse[slot])], inc=f"us{slot}", amt=16)
            p_t = 0
            for c in range(KH):
                half = c // 8
                s0, v0 = ptb_guard[half]
                p_t = p.emit("tensor", lambda e, c=c, rows=rows, slot=slot, half=half: e.transpose(
                    ptb[half][:, (c % 8) * 128:(c % 8) * 128 + rows],
                    arena[0:rows, off_ug + slot * 2048 + c * 128: off_ug + slot * 2048 + (c + 1) * 128],
                    idn_sb[0:rows, 0:rows]),
                    waits=[(f"us{slot}", gv), (s0, v0), ("ld_y", CONSTS)],
                    inc="pe" if c % 8 == 7 else None)
                if c % 8 == 7:
                    hv = p.emit("vector", lambda e, c=c, base=base, rows=rows, half=half: e.tensor_copy(
                        arena[:, off_utg: off_utg + KH * CAP].rearrange(
                            "z (cc q) -> z cc q", q=CAP)[:, c - 7:c + 1, base:base + rows],
                        ptb[half][:, :].rearrange("z (cc q) -> z cc q", q=128)[:, :, 0:rows]),
                        waits=[("pe", p_t)])
                    ptb_guard[half] = ("dv", hv)
                    utg_ready = hv
            ug_lastuse[slot] = p_t
        wd_pre = {}
        for m in range(KI):
            gj = (ex * KI + m) % 2
            wg = wdma("g", gj, "gw", ex * KI + m)
            wu = wdma("u", gj, "uw", ex * KI + m)
            emit_scr_piece()
            if m in (8, 9):
                n0 = m - 8
                wd_pre[n0] = wdma("d", (ex * 4 + n0) % 2, "dw", ex * 4 + n0)
            NH = CAP // 2
            pr = pair_ctr[0] % 3
            pair_ctr[0] += 1
            pu = pair_ctr[0] % 3
            pair_ctr[0] += 1
            bG = (2 * pr, 2 * pr + 1)
            bU = (2 * pu, 2 * pu + 1)
            gw0, gw1 = bank_guard[bG[0]], bank_guard[bG[1]]
            p_G = None
            for k in range(KH):
                for h in range(2):
                    last = (k == KH - 1 and h == 1)
                    p_G = p.emit("tensor", lambda e, k=k, h=h, gj=gj, bG=bG, NH=NH: e.matmul(
                        psb[bG[h]][:, 0:NH], a_view("g", gj, k * 128, 128),
                        arena[:, off_utg + k * CAP + h * NH: off_utg + k * CAP + (h + 1) * NH],
                        start=(k == 0), stop=(k == KH - 1)),
                        waits=[(f"ld_g{gj}", wg), ("dv", utg_ready), gw0, gw1],
                        inc="pe" if last else None)
            gw0, gw1 = bank_guard[bU[0]], bank_guard[bU[1]]
            p_U = None
            for k in range(KH):
                for h in range(2):
                    last = (k == KH - 1 and h == 1)
                    p_U = p.emit("tensor", lambda e, k=k, h=h, gj=gj, bU=bU, NH=NH: e.matmul(
                        psb[bU[h]][:, 0:NH], a_view("u", gj, k * 128, 128),
                        arena[:, off_utg + k * CAP + h * NH: off_utg + k * CAP + (h + 1) * NH],
                        start=(k == 0), stop=(k == KH - 1)),
                        waits=[(f"ld_u{gj}", wu), gw0, gw1],
                        inc="pe" if last else None)
            buf_lastuse[("g", gj)] = p_G
            buf_lastuse[("u", gj)] = p_U
            for h in range(2):
                slot = (m % 2) * 1024 + h * 512
                tsw = tsil_guard[m % 2]
                a_sil = p.emit("scalar", lambda e, h=h, slot=slot, bG=bG, NH=NH: e.activation(
                    tsil[:, slot: slot + NH], psb[bG[h]][:, 0:NH], AF.Silu),
                    waits=[("pe", p_G), ("dv", tsw), ("ld_y", CONSTS),
                           ("us1", LNUS[1])])
                bank_guard[bG[h]] = ("ac", a_sil)
                hv2 = m * CAP + h * NH
                dvv = p.emit("vector", lambda e, h=h, slot=slot, bU=bU, hv2=hv2, NH=NH: e.tensor_tensor(
                    arena[:, off_h + hv2: off_h + hv2 + NH], psb[bU[h]][:, 0:NH],
                    tsil[:, slot: slot + NH], ALU.mult),
                    waits=[("pe", p_U), ("ac", a_sil)])
                bank_guard[bU[h]] = ("dv", dvv)
                tsil_guard[m % 2] = dvv
        hbuf_ready = p.cnt["dv"]
        for n_ in range(4):
            dj = (ex * 4 + n_) % 2
            wd = wd_pre[n_] if n_ in wd_pre else wdma("d", dj, "dw", ex * 4 + n_)
            p_dn_last = 0
            for ti, (base, rows) in enumerate(CTILES):
                pr = pair_ctr[0] % 3
                pair_ctr[0] += 1
                bks = (2 * pr, 2 * pr + 1)
                gw0, gw1 = bank_guard[bks[0]], bank_guard[bks[1]]
                p_dn = None
                for k in range(KI):
                    for h in range(2):
                        last = (k == KI - 1 and h == 1)
                        p_dn = p.emit("tensor", lambda e, k=k, h=h, base=base, rows=rows, dj=dj, bks=bks: e.matmul(
                            psb[bks[h]][0:rows, 0:256],
                            arena[:, off_h + k * CAP + base: off_h + k * CAP + base + rows],
                            a_view("d", dj, k * 512 + h * 256, 256),
                            start=(k == 0), stop=(k == KI - 1)),
                            waits=[(f"ld_d{dj}", wd), ("dv", hbuf_ready), gw0, gw1],
                            inc="pe" if last else None)
                p_dn_last = p_dn
                slot = (ex * 4 + n_ + ti) % 2
                sg0, sv0 = dstg_guard[slot]
                cva = p.emit("vector", lambda e, rows=rows, slot=slot, bks=bks: e.tensor_copy(
                    dstg[0:rows, slot * 512: slot * 512 + 256], psb[bks[0]][0:rows, 0:256]),
                    waits=[("pe", p_dn), (sg0, sv0)])
                cvb = p.emit("vector", lambda e, rows=rows, slot=slot, bks=bks: e.tensor_copy(
                    dstg[0:rows, slot * 512 + 256: slot * 512 + 512], psb[bks[1]][0:rows, 0:256]),
                    waits=[("pe", p_dn)])
                bank_guard[bks[0]] = ("dv", cva)
                bank_guard[bks[1]] = ("dv", cvb)
                gv = p.emit("sync", lambda e, ex=ex, base=base, rows=rows, n_=n_, slot=slot: e.dma_start(
                    out=ogd[ex * CAP + base: ex * CAP + base + rows, n_ * 512:(n_ + 1) * 512],
                    in_=dstg[0:rows, slot * 512: slot * 512 + 512]),
                    waits=[("dv", cvb)], inc=f"dg{slot}", amt=16)
                dstg_guard[slot] = (f"dg{slot}", gv)
            buf_lastuse[("d", dj)] = p_dn_last

    def shared_expert(hf, uT_ready, pair_ctr, gu_pre=None):
        fin_tiles = [0] * TT
        wd_pre = {}
        for m in range(KI):
            gj = ((8 + hf) * KI + m) % 2
            if gu_pre and m in gu_pre:
                wg, wu = gu_pre[m]
            else:
                wg = wdma("g", gj, "sg", hf * KI + m)
                wu = wdma("u", gj, "su", hf * KI + m)
            emit_scr_piece()
            if m in (8, 9):
                n0 = m - 8
                wd_pre[n0] = wdma("d", n0 % 2, "sd", hf * 4 + n0)
            pr = pair_ctr[0] % 3
            pair_ctr[0] += 1
            pu = pair_ctr[0] % 3
            pair_ctr[0] += 1
            bG = (2 * pr, 2 * pr + 1)
            bU = (2 * pu, 2 * pu + 1)
            gw0, gw1 = bank_guard[bG[0]], bank_guard[bG[1]]
            p_G = None
            for k in range(KH):
                for h in range(2):
                    last = (k == KH - 1 and h == 1)
                    p_G = p.emit("tensor", lambda e, k=k, h=h, gj=gj, bG=bG: e.matmul(
                        psb[bG[h]][:, 0:HGT], a_view("g", gj, k * 128, 128),
                        uT[:, k * GT + h * HGT: k * GT + (h + 1) * HGT],
                        start=(k == 0), stop=(k == KH - 1)),
                        waits=[(f"ld_g{gj}", wg), uT_ready[0], uT_ready[1], gw0, gw1],
                        inc="pe" if last else None)
            gw0, gw1 = bank_guard[bU[0]], bank_guard[bU[1]]
            p_U = None
            for k in range(KH):
                for h in range(2):
                    last = (k == KH - 1 and h == 1)
                    p_U = p.emit("tensor", lambda e, k=k, h=h, gj=gj, bU=bU: e.matmul(
                        psb[bU[h]][:, 0:HGT], a_view("u", gj, k * 128, 128),
                        uT[:, k * GT + h * HGT: k * GT + (h + 1) * HGT],
                        start=(k == 0), stop=(k == KH - 1)),
                        waits=[(f"ld_u{gj}", wu), gw0, gw1],
                        inc="pe" if last else None)
            buf_lastuse[("g", gj)] = p_G
            buf_lastuse[("u", gj)] = p_U
            for h in range(2):
                slot = (m % 2) * 1024 + h * 512
                tsw = tsil_guard[m % 2]
                col = hf * KI + m
                a_sil = p.emit("scalar", lambda e, h=h, slot=slot, bG=bG, col=col: e.activation(
                    tsil[:, slot: slot + HGT], psb[bG[h]][:, 0:HGT], AF.Silu,
                    bias=bsg_sb[:, col: col + 1], scale=1.0),
                    waits=[("pe", p_G), ("dv", tsw), ("ld_y", CONSTS),
                           ("us1", LNUS[1])])
                bank_guard[bG[h]] = ("ac", a_sil)
                hv = m * GT + h * HGT
                dvv = p.emit("vector", lambda e, h=h, slot=slot, bU=bU, hv=hv, col=col: e.scalar_tensor_tensor(
                    arena[:, off_h + hv: off_h + hv + HGT], psb[bU[h]][:, 0:HGT],
                    bsu_sb[:, col: col + 1], tsil[:, slot: slot + HGT],
                    ALU.add, ALU.mult),
                    waits=[("pe", p_U), ("ac", a_sil), ("ld_y", CONSTS)])
                bank_guard[bU[h]] = ("dv", dvv)
                tsil_guard[m % 2] = dvv
        hbuf_ready = p.cnt["dv"]
        for n_ in range(4):
            dj = n_ % 2
            wd = wd_pre[n_] if n_ in wd_pre else wdma("d", dj, "sd", hf * 4 + n_)
            p_dn_last = 0
            for t in range(TT):
                pr = pair_ctr[0] % 3
                pair_ctr[0] += 1
                bks = (2 * pr, 2 * pr + 1)
                gw0, gw1 = bank_guard[bks[0]], bank_guard[bks[1]]
                p_dn = None
                for k in range(KI):
                    for h in range(2):
                        last = (k == KI - 1 and h == 1)
                        p_dn = p.emit("tensor", lambda e, k=k, h=h, t=t, dj=dj, bks=bks: e.matmul(
                            psb[bks[h]][:, 0:256],
                            arena[:, off_h + k * GT + t * 128: off_h + k * GT + t * 128 + 128],
                            a_view("d", dj, k * 512 + h * 256, 256),
                            start=(k == 0), stop=(k == KI - 1)),
                            waits=[(f"ld_d{dj}", wd), ("dv", hbuf_ready), gw0, gw1],
                            inc="pe" if last else None)
                p_dn_last = p_dn
                for h in range(2):
                    rcol = t * H + n_ * 512 + h * 256
                    dvv = p.emit("vector", lambda e, h=h, rcol=rcol, bks=bks: e.scalar_tensor_tensor(
                        rbuf[:, rcol: rcol + 256], psb[bks[h]][:, 0:256],
                        0.5, rbuf[:, rcol: rcol + 256], ALU.mult, ALU.add),
                        waits=[("pe", p_dn)])
                    bank_guard[bks[h]] = ("dv", dvv)
                fin_tiles[t] = p.cnt["dv"]
            buf_lastuse[("d", dj)] = p_dn_last
        return fin_tiles

    SPRG = [0]
    ACCG = [0]
    def ffn_phase(g, uT_ready, pair_ctr=[0]):
        gu_pre = {}
        for m in range(2):
            gu_pre[m] = (wdma("g", m % 2, "sg", m), wdma("u", m % 2, "su", m))
        usc_total = compaction(g, BSDH_W)
        SPRG[0] = p.cnt["pe"]
        fin_sh = [0] * TT
        for hf in range(2):
            fin_sh = shared_expert(hf, uT_ready, pair_ctr,
                                   gu_pre=gu_pre if hf == 0 else None)
        for ex in range(8):
            sparse_expert(ex, usc_total, uT_ready, pair_ctr)
        ogd_total = (p.cnt["dg0"], p.cnt["dg1"])
        fin_tiles = list(fin_sh)
        for t in range(TT):
            for r_ in range(2):
                for n_ in range(4):
                    slot = (t * 8 + r_ * 4 + n_) % 2
                    sg0, sv0 = dstg_guard[slot]
                    gv = p.emit("gpsimd", lambda e, t=t, r_=r_, n_=n_, slot=slot: e.indirect_dma_start(
                        out=dstg[:, slot * 512: slot * 512 + 512],
                        out_offset=None,
                        in_=ogd,
                        in_offset=bass.IndirectOffsetOnAxis(dsti[:, t * 2 + r_: t * 2 + r_ + 1], 0),
                        element_offset=n_ * 512),
                        waits=[("dg0", ogd_total[0]), ("dg1", ogd_total[1]),
                               (sg0, sv0)], inc=f"gg{slot}", amt=16)
                    dvv = p.emit("vector", lambda e, t=t, r_=r_, n_=n_, slot=slot: e.scalar_tensor_tensor(
                        rbuf[:, t * H + n_ * 512: t * H + (n_ + 1) * 512],
                        dstg[:, slot * 512: slot * 512 + 512],
                        dcf[:, t * 8 + 6 + r_: t * 8 + 7 + r_],
                        rbuf[:, t * H + n_ * 512: t * H + (n_ + 1) * 512],
                        ALU.mult, ALU.add),
                        waits=[(f"gg{slot}", gv)])
                    dstg_guard[slot] = ("dv", dvv)
                    if r_ == 1 and n_ == 3:
                        fin_tiles[t] = dvv
        buf_lastuse[("wi", 0)] = max(buf_lastuse[("wi", 0)], p.cnt["pe"])
        buf_lastuse[("wi", 1)] = max(buf_lastuse[("wi", 1)], p.cnt["pe"])
        acc_marks = [0] * TT
        for t in range(TT):
            acc_marks[t] = p.emit("gpsimd", lambda e, t=t: e.dma_start(
                out=rbuf[:, t * H:(t + 1) * H], in_=bsdh, accum_op=ALU.add),
                waits=[("dv", fin_tiles[t]), ("bsx", BSDH_W)], inc="sct", amt=16)
        ACCG[0] = acc_marks[-1]
        return fin_tiles, [acc_marks[-1]] * TT

    def attn_phase(g, uT_ready):
        for sg_i in range(2):
            for m in range(32):
                wj = m % 3
                wv = wdma("wi", wj, "wi", m)
                pr = m % 3
                bks = (2 * pr, 2 * pr + 1)
                gg = [bank_guard[bks[0]], bank_guard[bks[1]]]
                p_xt = None
                for k in range(KH):
                    for q in range(2):
                        last = (k == KH - 1 and q == 1)
                        p_xt = p.emit("tensor", lambda e, k=k, q=q, wj=wj, bks=bks, sg_i=sg_i: e.matmul(
                            psb[bks[q]][:, 0:NQ],
                            a_view("wi", wj, k * 128, 128),
                            uT[:, k * GT + sg_i * SGT + q * NQ: k * GT + sg_i * SGT + (q + 1) * NQ],
                            start=(k == 0), stop=(k == KH - 1)),
                            waits=[(f"ld_wi{wj}", wv), uT_ready[0], uT_ready[1], gg[0], gg[1]],
                            inc="pe" if last else None)
                buf_lastuse[("wi", wj)] = p_xt
                for q in range(2):
                    gv = p.emit("vector", lambda e, m=m, q=q, bks=bks: e.tensor_copy(
                        arena[:, off_xT + m * SGT + q * NQ: off_xT + m * SGT + (q + 1) * NQ],
                        psb[bks[q]][:, 0:NQ]),
                        waits=[("pe", p_xt)])
                    bank_guard[bks[q]] = ("dv", gv)
            xT_ready = p.cnt["dv"]
            pB_last = 0
            for m in range(32):
                cj = m % 2
                zj = m % 3
                wc = wdma("cvb", cj, "cv", m)
                wz = wdma("wi", zj, "wi", 32 + m)
                zpr = (2 * m) % 3
                cpr = (2 * m + 1) % 3
                bz = (2 * zpr, 2 * zpr + 1)
                bc = (2 * cpr, 2 * cpr + 1)
                gg = [bank_guard[bz[0]], bank_guard[bz[1]]]
                p_z = None
                for k in range(KH):
                    for q in range(2):
                        last = (k == KH - 1 and q == 1)
                        p_z = p.emit("tensor", lambda e, k=k, q=q, zj=zj, bz=bz, sg_i=sg_i: e.matmul(
                            psb[bz[q]][:, 0:NQ],
                            a_view("wi", zj, k * 128, 128),
                            uT[:, k * GT + sg_i * SGT + q * NQ: k * GT + sg_i * SGT + (q + 1) * NQ],
                            start=(k == 0), stop=(k == KH - 1)),
                            waits=[(f"ld_wi{zj}", wz), gg[0], gg[1]],
                            inc="pe" if last else None)
                buf_lastuse[("wi", zj)] = p_z
                gg = [bank_guard[bc[0]], bank_guard[bc[1]]]
                p_xc = None
                for k in range(32):
                    for q in range(2):
                        last = (k == 31 and q == 1)
                        p_xc = p.emit("tensor", lambda e, k=k, q=q, cj=cj, bc=bc: e.matmul(
                            psb[bc[q]][:, 0:NQ],
                            a_view("cvb", cj, k * 128, 128),
                            arena[:, off_xT + k * SGT + q * NQ: off_xT + k * SGT + (q + 1) * NQ],
                            start=(k == 0), stop=(k == 31)),
                            waits=[(f"ld_cvb{cj}", wc), ("dv", xT_ready), gg[0], gg[1]],
                            inc="pe" if last else None)
                buf_lastuse[("cvb", cj)] = p_xc
                pB_last = p_xc
                slot = (m % 2) * SGT
                for q in range(2):
                    tsw = tsil_guard[m % 2]
                    a_z = p.emit("scalar", lambda e, slot=slot, q=q, bz=bz: e.activation(
                        tsil[:, slot + q * NQ: slot + (q + 1) * NQ],
                        psb[bz[q]][:, 0:NQ], AF.Silu),
                        waits=[("pe", p_z), ("dv", tsw), uT_ready[0], uT_ready[1],
                               ("us1", LNUS[1])])
                    bank_guard[bz[q]] = ("ac", a_z)
                    gv = p.emit("vector", lambda e, m=m, slot=slot, q=q, bc=bc: e.scalar_tensor_tensor(
                        arena[:, off_sb_ + m * SGT + q * NQ: off_sb_ + m * SGT + (q + 1) * NQ],
                        psb[bc[q]][:, 0:NQ], dvec_sb[:, m: m + 1],
                        tsil[:, slot + q * NQ: slot + (q + 1) * NQ], ALU.mult, ALU.mult),
                        waits=[("pe", p_xc), ("ac", a_z), ("ld_y", CONSTS)])
                    bank_guard[bc[q]] = ("dv", gv)
                    tsil_guard[m % 2] = gv
            sb_ready = p.cnt["dv"]
            wo_all = {}
            def emit_wo(ip):
                if ip <= 7:
                    wj = ip % 3
                    wo_all[ip] = (wj, wdma("wob", wj, "wo", ip,
                                           extra_pe_guard=pB_last))
            for ip in range(3):
                emit_wo(ip)
            for n in range(4):
                wvs = [wo_all[n * 2], wo_all[n * 2 + 1]]
                for t in range(STT):
                    pc = (n * STT + t) % 3
                    bks = (2 * pc, 2 * pc + 1)
                    gg = [bank_guard[bks[0]], bank_guard[bks[1]]]
                    p_o = None
                    for k in range(32):
                        kh = k // 16
                        wj, wv = wvs[kh]
                        for q in range(2):
                            last = (k == 31 and q == 1)
                            p_o = p.emit("tensor", lambda e, k=k, q=q, t=t, wj=wj, bks=bks: e.matmul(
                                psb[bks[q]][:, 0:256],
                                arena[:, off_sb_ + k * SGT + t * 128: off_sb_ + k * SGT + t * 128 + 128],
                                a_view("wob", wj, (k % 16) * 512 + q * 256, 256),
                                start=(k == 0), stop=(k == 31)),
                                waits=[(f"ld_wob{wj}", wv), ("dv", sb_ready), gg[0], gg[1]],
                                inc="pe" if last else None)
                    buf_lastuse[("wob", wvs[0][0])] = p_o
                    buf_lastuse[("wob", wvs[1][0])] = p_o
                    rcol = (sg_i * STT + t) * H + n * 512
                    for q in range(2):
                        gv = p.emit("vector", lambda e, rcol=rcol, q=q, bks=bks: e.tensor_tensor(
                            rbuf[:, rcol + q * 256: rcol + (q + 1) * 256],
                            psb[bks[q]][:, 0:256],
                            rbuf[:, rcol + q * 256: rcol + (q + 1) * 256], ALU.add),
                            waits=[("pe", p_o), ("sct", ACCG[0])])
                        bank_guard[bks[q]] = ("dv", gv)
                emit_wo(n * 2 + 3)
                emit_wo(n * 2 + 4)
            af_sg = p.cnt["pe"]
            for key in (("wi", 0), ("wi", 1), ("wi", 2), ("cvb", 0), ("cvb", 1)):
                buf_lastuse[key] = max(buf_lastuse[key], af_sg)
        af = p.cnt["pe"]
        for key in (("g", 0), ("g", 1), ("u", 0), ("u", 1), ("d", 0), ("d", 1)):
            buf_lastuse[key] = max(buf_lastuse[key], af)
        ug_lastuse[0] = max(ug_lastuse[0], af)
        ug_lastuse[1] = max(ug_lastuse[1], af)

    st_cnt = 0
    fin_tiles = [0] * TT
    for g in range(NG):
        NPIECE = min(4, TT)
        for piece in range(NPIECE):
            waits = []
            if g > 0 and piece == 0:
                waits.append(("st", st_cnt))
            if piece > 0:
                waits.append(("ld_x", (g * NPIECE + piece) * 16))
            p.emit("sync", lambda e, g=g, piece=piece, NPIECE=NPIECE: e.dma_start(
                out=rbuf[:, piece * (TT // NPIECE) * H:(piece + 1) * (TT // NPIECE) * H]
                    .rearrange("q (t h) -> q t h", h=H),
                in_=ap["x"][g * GT + piece * (GT // NPIECE): g * GT + (piece + 1) * (GT // NPIECE), :]
                    .rearrange("(t q) h -> q t h", q=128)),
                waits=waits, inc="ld_x", amt=16)
        for sub in range(3):
            SUBIDX[0] = sub
            uT_ready = ln_phase(g)
            if sub in (0, 2):
                router_phase(uT_ready)
                fin_tiles, acc_marks = ffn_phase(g, uT_ready)
            else:
                attn_phase(g, uT_ready)
        for piece in range(2):
            fin = max(fin_tiles[piece * (TT // 2): (piece + 1) * (TT // 2)])
            am = max(acc_marks[piece * (TT // 2): (piece + 1) * (TT // 2)])
            st_cnt = p.emit("sync", lambda e, g=g, piece=piece: e.dma_start(
                out=out_ap[g * GT + piece * (GT // 2): g * GT + (piece + 1) * (GT // 2), :]
                    .rearrange("(t q) h -> q t h", q=128),
                in_=rbuf[:, piece * (TT // 2) * H:(piece + 1) * (TT // 2) * H]
                    .rearrange("q (t h) -> q t h", h=H)),
                waits=[("dv", fin), ("sct", am)], inc="st", amt=16)
    p.emit("sync", None, waits=[("st", st_cnt)])

    with es:
        block = es.enter_context(nc.Block())
        for eng in Plan.ENGINES:
            ops = p.ops[eng]
            if not ops:
                continue
            def body(e, ops=ops):
                hwm = {}
                for waits, fn, inc, amt in ops:
                    for s, v in waits:
                        if hwm.get(s, 0) < v:
                            e.wait_ge(sems[s], v)
                            hwm[s] = v
                    if fn is not None:
                        ins = fn(e)
                        if inc is not None:
                            ins.then_inc(sems[inc], amt)
            getattr(block, eng)(body)
    return nc


def pack_inputs(i, GT=1024):
    import os as _os
    TT = GT // 128
    _capo = _os.environ.get("KCAP")
    CAP = int(_capo) if _capo else (320 if GT == 1024 else GT // 2)
    c = np.ascontiguousarray
    d = {}
    d["gw"] = c(np.asarray(i["L0_ffn_gate_w"]).reshape(8, 16, 128, 11, 128)
                .transpose(0, 3, 2, 1, 4).reshape(88, 128, 2048))
    d["uw"] = c(np.asarray(i["L0_ffn_up_w"]).reshape(8, 16, 128, 11, 128)
                .transpose(0, 3, 2, 1, 4).reshape(88, 128, 2048))
    d["dw"] = c(np.asarray(i["L0_ffn_down_w"]).reshape(8, 11, 128, 4, 512)
                .transpose(0, 3, 2, 1, 4).reshape(32, 128, 5632))
    d["sg"] = c(np.asarray(i["L0_ffn_s_gate_w"]).reshape(16, 128, 2, 11, 128)
                .transpose(2, 3, 1, 0, 4).reshape(22, 128, 2048))
    d["su"] = c(np.asarray(i["L0_ffn_s_up_w"]).reshape(16, 128, 2, 11, 128)
                .transpose(2, 3, 1, 0, 4).reshape(22, 128, 2048))
    d["sd"] = c(np.asarray(i["L0_ffn_s_down_w"]).reshape(2, 11, 128, 4, 512)
                .transpose(0, 3, 2, 1, 4).reshape(8, 128, 5632))
    d["wi"] = c(np.asarray(i["L0_attn_W_in"]).reshape(64, 128, 16, 128)
                .transpose(0, 3, 2, 1).reshape(64, 128, 2048))
    d["cv"] = c(np.asarray(i["L0_attn_conv_w"]).reshape(32, 128, 32, 128)
                .transpose(0, 3, 2, 1).reshape(32, 128, 4096))
    d["wo"] = c(np.asarray(i["L0_attn_W_out"]).reshape(4, 512, 2, 16, 128)
                .transpose(0, 2, 4, 3, 1).reshape(8, 128, 8192))
    d["rw"] = c(np.asarray(i["L0_ffn_router_w"]).reshape(8, 16, 128)
                .transpose(2, 1, 0).reshape(128, 128))
    d["dvec"] = c(np.asarray(i["L0_attn_D"]).reshape(32, 128).T)
    d["bsg"] = c(np.asarray(i["L0_ffn_s_gate_b"]).reshape(22, 128).T)
    d["bsu"] = c(np.asarray(i["L0_ffn_s_up_b"]).reshape(22, 128).T)
    d["bsdb"] = c(np.broadcast_to(np.asarray(i["L0_ffn_s_down_b"]), (128, 2048)))
    d["idn"] = np.eye(128, dtype=BFNP)
    d["idnf"] = np.eye(128, dtype=np.float32)
    d["toki"] = c((np.arange(TT)[None, :] * 128
                   + np.arange(128)[:, None]).astype(np.float32))
    d["ecst"] = c(((np.arange(128) % 8) * CAP).astype(np.float32).reshape(128, 1))
    d["idn8"] = c((np.arange(128)[:, None] % 8 == np.arange(8)[None, :])
                  .astype(np.float32))
    return d


def kernel(**inputs):
    shared = pack_inputs(inputs)
    x = np.ascontiguousarray(np.asarray(inputs["L0_attn_hidden_states"],
                                        dtype=np.float32).reshape(16384, 2048))
    nc = build_program(GT=1024, TOKC=2048)
    in_maps = []
    for cidx in range(N_CORES):
        m = dict(shared)
        m["x"] = np.ascontiguousarray(x[cidx * 2048:(cidx + 1) * 2048])
        in_maps.append(m)
    res = run_bass_kernel_spmd(nc, in_maps, core_ids=list(range(N_CORES)))
    out = np.concatenate([np.asarray(r["out"]) for r in res.results], axis=0)
    return out.reshape(1, 16384, 2048).astype(np.float32)


# version 12 (final)
# speedup vs baseline: 1.4561x; 1.0362x over previous
import numpy as np
import ml_dtypes
from contextlib import ExitStack

import concourse.bass as bass
import concourse.mybir as mybir
from concourse.bass_utils import run_bass_kernel_spmd

F32 = mybir.dt.float32
BF16 = mybir.dt.bfloat16
AF = mybir.ActivationFunctionType
ALU = mybir.AluOpType
AX = mybir.AxisListType
BFNP = ml_dtypes.bfloat16
I32 = mybir.dt.int32

H = 2048
KH = H // 128
I_EXP = 1408
KI = I_EXP // 128
N_CORES = 8


class Plan:
    ENGINES = ("tensor", "vector", "scalar", "gpsimd", "sync")
    SELF = {"vector": "dv", "scalar": "ac"}

    def __init__(self):
        self.ops = {e: [] for e in self.ENGINES}
        self.cnt = {}

    def sem(self, name):
        self.cnt.setdefault(name, 0)

    def emit(self, eng, fn, waits=(), inc=None, amt=1):
        if eng in self.SELF and fn is not None and (inc is None or inc == self.SELF[eng]):
            s = self.SELF[eng]
            waits = tuple(waits) + ((s, self.cnt[s]),)
            if inc is None:
                inc, amt = s, 1
        for s, v in waits:
            assert s in self.cnt, s
            assert v <= self.cnt[s], ("wait ahead of plan", s, v, self.cnt[s])
        self.ops[eng].append((tuple(waits), fn, inc, amt))
        if inc is not None:
            self.cnt[inc] += amt
            return self.cnt[inc]
        return None


def build_program(GT=1024, TOKC=2048):
    NG = TOKC // GT
    TT = GT // 128
    SGT = GT // 2
    STT = TT // 2
    NQ = GT // 4
    HGT = GT // 2

    nc = bass.Bass("TRN2", target_bir_lowering=False)
    ap = {}
    def din(name, shape, dt=F32):
        ap[name] = nc.dram_tensor(name, shape, dt, kind="ExternalInput").ap()
    din("x", [TOKC, H])
    din("gw", [88, 128, KH * 128])
    din("uw", [88, 128, KH * 128])
    din("dw", [32, 128, KI * 512])
    din("sg", [22, 128, KH * 128])
    din("su", [22, 128, KH * 128])
    din("sd", [8, 128, KI * 512])
    din("wi", [64, 128, KH * 128])
    din("cv", [32, 128, 32 * 128])
    din("wo", [8, 128, 16 * 512])
    din("rw", [128, 128])
    din("dvec", [128, 32])
    din("bsg", [128, 22])
    din("bsu", [128, 22])
    din("bsdb", [128, H])
    din("idn", [128, 128], BF16)
    din("idnf", [128, 128])
    din("toki", [128, TT])
    din("ecst", [128, 1])
    din("idn8", [128, 8])
    out_ap = nc.dram_tensor("out", [TOKC, H], F32, kind="ExternalOutput").ap()
    import os as _os
    _capo = _os.environ.get("KCAP")
    CAP = int(_capo) if _capo else (320 if GT == 1024 else GT // 2)
    CTILES = []
    _b = 0
    while _b < CAP:
        CTILES.append((_b, min(128, CAP - _b)))
        _b += 128
    u_scr = nc.dram_tensor("u_scr", [GT, H], BF16, kind="Internal").ap()
    ug_dram = nc.dram_tensor("ug_dram", [8 * CAP + 128, H], BF16, kind="Internal").ap()
    ogd = nc.dram_tensor("ogd", [8 * CAP + 128, H], BF16, kind="Internal").ap()
    bsdh = nc.dram_tensor("bsdh", [128, H], F32, kind="Internal").ap()
    scr = {}
    scr["wi"] = nc.dram_tensor("scr_wi", [64, 128, KH * 128], BF16, kind="Internal").ap()
    scr["cv"] = nc.dram_tensor("scr_cv", [32, 128, 32 * 128], BF16, kind="Internal").ap()
    scr["wo"] = nc.dram_tensor("scr_wo", [8, 128, 16 * 512], BF16, kind="Internal").ap()

    p = Plan()
    for s in ("pe", "dv", "ac", "st", "ld_y", "ld_x", "ld_c", "scr_c", "ga", "sct", "bsx", "us0", "us1", "sc0", "sc1", "dg0", "dg1", "gg0", "gg1", "gg2", "gg3", "zin"):
        p.sem(s)
    BUFN = (("g", 2), ("u", 2), ("d", 2), ("wi", 3), ("cvb", 2), ("wob", 3))
    for pref, n in BUFN:
        for i in range(n):
            p.sem(f"ld_{pref}{i}")

    es = ExitStack()
    def sbuf(name, cols, dt):
        return es.enter_context(nc.sbuf_tensor(name, [128, cols], dt))

    rbuf = sbuf("rbuf", TT * H, F32)
    uT = sbuf("uT", KH * GT, BF16)
    ubf = sbuf("ubf", H, BF16)
    tsil = sbuf("tsil", 2048, BF16)
    spr = sbuf("spr", 1024, F32)
    dstg = sbuf("dstg", 1024, BF16)
    dcf = sbuf("dcf", 8 * TT, F32)
    dsti = sbuf("dsti", 2 * TT, I32)
    dmpc = sbuf("dmpc", 1, F32)
    stats = sbuf("stats", 8 * TT, F32)
    lgsb = sbuf("lgsb", 8 * TT, F32)
    esb = sbuf("esb", 8 * TT, F32)
    tmp8 = sbuf("tmp8", 8 * TT, F32)
    tmp8b = sbuf("tmp8b", 8 * TT, F32)
    st2 = sbuf("st2", 8 * TT, F32)
    c_buf = sbuf("c_buf", 8 * TT, F32)
    rw_bf = sbuf("rw_bf", 128, BF16)
    idn_sb = sbuf("idn_sb", 128, BF16)
    idnf_sb = sbuf("idnf_sb", 128, F32)
    toki_sb = sbuf("toki_sb", TT, F32)
    ecst_sb = sbuf("ecst_sb", 1, F32)
    idn8_sb = sbuf("idn8_sb", 8, F32)
    dvec_sb = sbuf("dvec_sb", 32, F32)
    bsg_sb = sbuf("bsg_sb", 22, F32)
    bsu_sb = sbuf("bsu_sb", 22, F32)

    off_h = 0
    off_utg = KI * CAP
    sz_h = max(KI * GT, off_utg + KH * CAP)
    off_g = [sz_h + j * 2048 for j in range(2)]
    off_u = [sz_h + 2 * 2048 + j * 2048 for j in range(2)]
    off_d = [sz_h + 4 * 2048 + j * (KI * 512) for j in range(2)]
    ffn_end = sz_h + 4 * 2048 + 2 * KI * 512
    off_xT = 0
    off_sb_ = 32 * SGT
    off_wi = [max(ffn_end, 64 * SGT) + j * 2048 for j in range(3)]
    off_cv = [off_wi[0] + 3 * 2048 + j * 4096 for j in range(2)]
    attn_end = off_cv[0] + 2 * 4096
    if GT == 1024:
        off_wo = [0, 8192, off_wi[0]]
        arena_cols = attn_end
    else:
        off_wo = [attn_end + j * 8192 for j in range(3)]
        arena_cols = attn_end + 3 * 8192
    off_ug = off_wi[0]
    arena = sbuf("arena", arena_cols, BF16)

    psb = [es.enter_context(nc.psum_tensor(f"pb{i}", [128, 512], F32)) for i in range(6)]
    ptb = [es.enter_context(nc.psum_tensor(f"pt{i}", [128, 1024], BF16)) for i in range(2)]
    sems = {}
    for name in p.cnt:
        sems[name] = es.enter_context(nc.semaphore(name))

    bank_guard = {i: ("dv", 0) for i in range(6)}
    ptb_guard = [("dv", 0), ("dv", 0)]
    buf_lastuse = {}
    buf_gen = {}
    for pref, n in BUFN:
        for j in range(n):
            buf_lastuse[(pref, j)] = 0
            buf_gen[(pref, j)] = 0
    tsil_guard = {0: 0, 1: 0}
    CONSTS = 8 * 16
    SCRT = [0]

    p.emit("sync", lambda e: e.dma_start(out=dvec_sb[:, :], in_=ap["dvec"]), inc="ld_y", amt=16)
    p.emit("sync", lambda e: e.dma_start(out=bsg_sb[:, :], in_=ap["bsg"]), inc="ld_y", amt=16)
    p.emit("sync", lambda e: e.dma_start(out=bsu_sb[:, :], in_=ap["bsu"]), inc="ld_y", amt=16)
    p.emit("sync", lambda e: e.dma_start(out=idn_sb[:, :], in_=ap["idn"]), inc="ld_y", amt=16)
    p.emit("sync", lambda e: e.dma_start(out=idnf_sb[:, :], in_=ap["idnf"]), inc="ld_y", amt=16)
    p.emit("sync", lambda e: e.dma_start(out=toki_sb[:, :], in_=ap["toki"]), inc="ld_y", amt=16)
    p.emit("sync", lambda e: e.dma_start(out=ecst_sb[:, :], in_=ap["ecst"]), inc="ld_y", amt=16)
    p.emit("sync", lambda e: e.dma_start(out=idn8_sb[:, :], in_=ap["idn8"]), inc="ld_y", amt=16)
    p.emit("gpsimd", lambda e: e.dma_start(out=rw_bf[:, :], in_=ap["rw"]), inc="ld_c", amt=16)
    scr_jobs = []
    for name, nblk, step in (("wi", 64, 2), ("cv", 32, 1), ("wo", 8, 1)):
        for a in range(0, nblk, step):
            scr_jobs.append((name, a, min(a + step, nblk)))
    SCRT[0] = len(scr_jobs) * 16

    def emit_scr_piece():
        if scr_jobs:
            name, a, bnd = scr_jobs.pop(0)
            p.emit("gpsimd", lambda e, name=name, a=a, bnd=bnd: e.dma_start(
                out=scr[name][a:bnd], in_=ap[name][a:bnd]), inc="scr_c", amt=16)

    p.emit("vector", lambda e: e.tensor_scalar(
        dmpc[:, 0:1], toki_sb[:, 0:1], float(8 * CAP), None, ALU.add),
        waits=[("ld_y", CONSTS)])
    BSDH_W = 0
    for hh in range(2):
        lv = p.emit("sync", lambda e, hh=hh: e.dma_start(
            out=spr[:, 0:1024], in_=ap["bsdb"][:, hh * 1024:(hh + 1) * 1024]),
            waits=[("bsx", BSDH_W)], inc="bsx", amt=16)
        v0 = p.emit("vector", lambda e: e.tensor_scalar(
            spr[:, 0:1024], spr[:, 0:1024], 0.5, None, ALU.mult),
            waits=[("bsx", lv)])
        BSDH_W = p.emit("sync", lambda e, hh=hh: e.dma_start(
            out=bsdh[:, hh * 1024:(hh + 1) * 1024], in_=spr[:, 0:1024]),
            waits=[("dv", v0)], inc="bsx", amt=16)

    vz = p.emit("vector", lambda e: e.memset(
        arena[:, off_ug: off_ug + 2048], 0.0))
    zr = 0
    nrow = 8 * CAP + 128
    a0 = 0
    while a0 < nrow:
        b0 = min(a0 + 128, nrow)
        for tgt in (ug_dram, ogd):
            zr = p.emit("sync", lambda e, a0=a0, b0=b0, tgt=tgt: e.dma_start(
                out=tgt[a0:b0, :], in_=arena[0:b0 - a0, off_ug: off_ug + 2048]),
                waits=[("dv", vz)], inc="zin", amt=16)
        a0 += 128
    ZINIT = zr

    WQ = {"g": "gpsimd", "u": "gpsimd", "d": "gpsimd",
          "wi": "sync", "cvb": "sync", "wob": "sync"}
    WSRC_SCR = {"wi": "wi", "cvb": "cv", "wob": "wo"}
    OFFS = {"g": off_g, "u": off_u, "d": off_d, "wi": off_wi, "cvb": off_cv, "wob": off_wo}
    NCOLS = {"g": 2048, "u": 2048, "d": KI * 512, "wi": 2048, "cvb": 4096, "wob": 8192}

    def wdma(stream, j, src_name, blk, extra_pe_guard=0):
        gen = buf_gen[(stream, j)] = buf_gen[(stream, j)] + 1
        guard = max(buf_lastuse[(stream, j)], extra_pe_guard)
        o = OFFS[stream][j]
        ncols = NCOLS[stream]
        waits = []
        if guard:
            waits.append(("pe", guard))
        if stream in WSRC_SCR:
            waits.append(("scr_c", SCRT[0]))
            srcap = scr[WSRC_SCR[stream]]
        else:
            srcap = ap[src_name]
        p.emit(WQ[stream],
               lambda e, o=o, n=ncols, s=srcap, b=blk: e.dma_start(
                   out=arena[:, o:o + n], in_=s[b]),
               waits=waits, inc=f"ld_{stream}{j}", amt=16)
        return gen * 16

    def a_view(stream, j, lo, n):
        o = OFFS[stream][j]
        return arena[:, o + lo: o + lo + n]

    SUBIDX = [0]
    ACCG_T = [0] * TT
    lnbuf = [ubf, tsil]
    lnbuf_tr_guard = [0, 0]
    LNUS = [0, 0]
    USCT = [(0, 0)]
    def ln_phase(g):
        acg = p.cnt["ac"]
        dvg = p.cnt["dv"]
        NPIECE = min(4, TT)
        TPP = TT // NPIECE
        for t in range(TT):
            par = t % 2
            buf = lnbuf[par]
            w0 = [("dv", dvg), ("pe", lnbuf_tr_guard[par]),
                  (f"us{par}", LNUS[par])]
            if SUBIDX[0] == 0:
                w0.append(("ld_x", (g * NPIECE + t // TPP + 1) * 16))
            a_sum = p.emit("scalar", lambda e, t=t, buf=buf: e.activation(
                buf[:, :], rbuf[:, t * H:(t + 1) * H], AF.Copy,
                accum_out=stats[:, t * 8: t * 8 + 1]),
                waits=w0 + [("sct", ACCG_T[t])])
            v_negm = p.emit("vector", lambda e, t=t: e.tensor_scalar(
                stats[:, t * 8 + 1: t * 8 + 2], stats[:, t * 8: t * 8 + 1],
                -1.0 / H, None, ALU.mult), waits=[("ac", a_sum)])
            a_sq = p.emit("scalar", lambda e, t=t, buf=buf: e.activation(
                buf[:, :], rbuf[:, t * H:(t + 1) * H], AF.Square,
                bias=stats[:, t * 8 + 1: t * 8 + 2], scale=1.0,
                accum_out=stats[:, t * 8 + 2: t * 8 + 3]),
                waits=[("dv", v_negm), (f"us{t % 2}", LNUS[t % 2])])
            v_var = p.emit("vector", lambda e, t=t: e.tensor_scalar(
                stats[:, t * 8 + 3: t * 8 + 4], stats[:, t * 8 + 2: t * 8 + 3],
                1.0 / H, 1e-5, ALU.mult, ALU.add), waits=[("ac", a_sq)])
            a_sd = p.emit("scalar", lambda e, t=t: e.activation(
                stats[:, t * 8 + 4: t * 8 + 5], stats[:, t * 8 + 3: t * 8 + 4],
                AF.Sqrt), waits=[("dv", v_var)])
            p.emit("vector", lambda e, t=t: e.reciprocal(
                stats[:, t * 8 + 5: t * 8 + 6], stats[:, t * 8 + 4: t * 8 + 5]),
                waits=[("ac", a_sd)])
            v_ubf = p.emit("vector", lambda e, t=t, buf=buf: e.tensor_scalar(
                buf[:, :], rbuf[:, t * H:(t + 1) * H], stats[:, t * 8 + 1: t * 8 + 2],
                stats[:, t * 8 + 5: t * 8 + 6], ALU.add, ALU.mult),
                waits=[("pe", lnbuf_tr_guard[par]), ("ac", acg)])
            if SUBIDX[0] != 1:
                LNUS[par] = p.emit("sync", lambda e, t=t, buf=buf: e.dma_start(
                    out=u_scr[t * 128:(t + 1) * 128, :], in_=buf[:, :]),
                    waits=[("dv", v_ubf), ("us0", USCT[0][0]), ("us1", USCT[0][1])],
                    inc=f"us{par}", amt=16)
            p_tr = [0, 0]
            for half in range(2):
                s0, v0 = ptb_guard[half]
                for c in range(8):
                    cc = half * 8 + c
                    p_tr[half] = p.emit("tensor", lambda e, cc=cc, half=half, c=c, buf=buf: e.transpose(
                        ptb[half][:, c * 128:(c + 1) * 128],
                        buf[:, cc * 128:(cc + 1) * 128], idn_sb[:, :]),
                        waits=[("dv", v_ubf), (s0, v0), ("ld_y", CONSTS)],
                        inc="pe" if c == 7 else None)
            for half in range(2):
                gv = p.emit("vector", lambda e, half=half, t=t: e.tensor_copy(
                    uT[:, :].rearrange("z (c g) -> z c g", g=GT)[:, half * 8:half * 8 + 8,
                                                                t * 128:t * 128 + 128],
                    ptb[half][:, :].rearrange("z (c g) -> z c g", g=128)),
                    waits=[("pe", p_tr[half])])
                ptb_guard[half] = ("dv", gv)
            lnbuf_tr_guard[par] = p_tr[1]
            if par == 1:
                tsil_guard[0] = max(tsil_guard[0], v_ubf)
                tsil_guard[1] = max(tsil_guard[1], v_ubf)
        return (("pe", p.cnt["pe"]), ("dv", p.cnt["dv"]))

    def router_phase(uT_ready):
        for t in range(TT):
            bk = t % 2
            s0, v0 = bank_guard[bk]
            ins_l = None
            for c in range(KH):
                ins_l = p.emit("tensor", lambda e, c=c, t=t, bk=bk: e.matmul(
                    psb[bk][:, 0:8], uT[:, c * GT + t * 128: c * GT + t * 128 + 128],
                    rw_bf[:, c * 8:(c + 1) * 8], start=(c == 0), stop=(c == KH - 1)),
                    waits=[uT_ready[0], uT_ready[1], (s0, v0), ("ld_c", 16)],
                    inc="pe" if c == KH - 1 else None)
            p_lg = ins_l
            gv = p.emit("vector", lambda e, t=t, bk=bk: e.tensor_copy(
                lgsb[:, t * 8:(t + 1) * 8], psb[bk][:, 0:8]),
                waits=[("pe", p_lg)])
            bank_guard[bk] = ("dv", gv)
            p.emit("vector", lambda e, t=t: e.reduce_max(
                st2[:, t * 8: t * 8 + 1], lgsb[:, t * 8:(t + 1) * 8], axis=AX.X))
            v_ngm = p.emit("vector", lambda e, t=t: e.tensor_scalar(
                st2[:, t * 8 + 1: t * 8 + 2], st2[:, t * 8: t * 8 + 1],
                -1.0, None, ALU.mult))
            a_exp = p.emit("scalar", lambda e, t=t: e.activation(
                esb[:, t * 8:(t + 1) * 8], lgsb[:, t * 8:(t + 1) * 8], AF.Exp,
                bias=st2[:, t * 8 + 1: t * 8 + 2], scale=1.0),
                waits=[("dv", v_ngm)])
            p.emit("vector", lambda e, t=t: e.tensor_scalar(
                tmp8[:, t * 8:(t + 1) * 8], esb[:, t * 8:(t + 1) * 8],
                1.0, -2.0, ALU.is_ge, ALU.mult), waits=[("ac", a_exp)])
            p.emit("vector", lambda e, t=t: e.tensor_tensor(
                tmp8[:, t * 8:(t + 1) * 8], tmp8[:, t * 8:(t + 1) * 8],
                esb[:, t * 8:(t + 1) * 8], ALU.add))
            v_m2 = p.emit("vector", lambda e, t=t: e.reduce_max(
                st2[:, t * 8 + 2: t * 8 + 3], tmp8[:, t * 8:(t + 1) * 8], axis=AX.X))
            p.emit("vector", lambda e, t=t: e.tensor_scalar(
                st2[:, t * 8 + 3: t * 8 + 4], st2[:, t * 8 + 2: t * 8 + 3],
                1.0, None, ALU.add))
            v_inv = p.emit("vector", lambda e, t=t: e.reciprocal(
                st2[:, t * 8 + 4: t * 8 + 5], st2[:, t * 8 + 3: t * 8 + 4]))
            p.emit("vector", lambda e, t=t: e.tensor_scalar(
                tmp8b[:, t * 8:(t + 1) * 8], esb[:, t * 8:(t + 1) * 8],
                st2[:, t * 8 + 2: t * 8 + 3], None, ALU.is_ge),
                waits=[("dv", v_m2)])
            p.emit("vector", lambda e, t=t: e.tensor_tensor(
                tmp8b[:, t * 8:(t + 1) * 8], tmp8b[:, t * 8:(t + 1) * 8],
                esb[:, t * 8:(t + 1) * 8], ALU.mult))
            p.emit("vector", lambda e, t=t: e.tensor_scalar(
                c_buf[:, t * 8:(t + 1) * 8], tmp8b[:, t * 8:(t + 1) * 8],
                st2[:, t * 8 + 4: t * 8 + 5], 0.5, ALU.mult, ALU.mult),
                waits=[("dv", v_inv)])

    BIG = 60000.0

    def compaction(g, spr_guard):
        if not hasattr(compaction, "slot_guard"):
            compaction.slot_guard = [0, 0]
        for t in range(TT):
            v_m = p.emit("vector", lambda e, t=t: e.tensor_scalar(
                tmp8b[:, t * 8:(t + 1) * 8], c_buf[:, t * 8:(t + 1) * 8],
                0.0, None, ALU.is_gt),
                waits=[("bsx", spr_guard), ("pe", SPRG[0])])
            bk = t % 2
            s0, v0 = bank_guard[bk]
            p_t = p.emit("tensor", lambda e, t=t, bk=bk: e.transpose(
                psb[bk][0:8, 0:128], tmp8b[:, t * 8:(t + 1) * 8], idnf_sb[:, :]),
                waits=[("dv", v_m), (s0, v0), ("ld_y", CONSTS)], inc="pe")
            gv = p.emit("vector", lambda e, t=t, bk=bk: e.tensor_copy(
                spr[0:8, t * 128:(t + 1) * 128], psb[bk][0:8, 0:128]),
                waits=[("pe", p_t)])
            bank_guard[bk] = ("dv", gv)
        p.emit("vector", lambda e: e.tensor_tensor_scan(
            spr[32:40, 0:GT], spr[0:8, 0:GT], spr[0:8, 0:GT], 0.0,
            ALU.add, ALU.bypass), waits=[("pe", SPRG[0])])
        p.emit("vector", lambda e: e.tensor_scalar(
            spr[32:40, 0:GT], spr[32:40, 0:GT], -1.0, None, ALU.add))
        v_dst = 0
        CHW = min(512, GT)
        for ch in range(GT // CHW):
            cs = slice(ch * CHW, (ch + 1) * CHW)
            bka = ch % 2
            bkb = 2 + ch % 2
            sa0, sa1 = bank_guard[bka]
            sb0, sb1 = bank_guard[bkb]
            p.emit("vector", lambda e, cs=cs, bka=bka: e.tensor_scalar(
                psb[bka][0:8, 0:CHW], spr[32:40, cs], float(CAP) - 0.5,
                None, ALU.is_le), waits=[(sa0, sa1)])
            p.emit("vector", lambda e, cs=cs, bka=bka, bkb=bkb: e.tensor_tensor(
                psb[bkb][0:8, 0:CHW], spr[0:8, cs], psb[bka][0:8, 0:CHW],
                ALU.mult), waits=[(sb0, sb1)])
            p.emit("vector", lambda e, cs=cs: e.tensor_scalar(
                spr[32:40, cs], spr[32:40, cs], ecst_sb[32:40, 0:1], None, ALU.add))
            p.emit("vector", lambda e, cs=cs, bkb=bkb: e.tensor_tensor(
                spr[32:40, cs], spr[32:40, cs], psb[bkb][0:8, 0:CHW], ALU.mult))
            p.emit("vector", lambda e, cs=cs, bkb=bkb: e.tensor_scalar(
                psb[bkb][0:8, 0:CHW], psb[bkb][0:8, 0:CHW], -BIG, BIG,
                ALU.mult, ALU.add))
            v_dst = p.emit("vector", lambda e, cs=cs, bkb=bkb: e.tensor_tensor(
                spr[32:40, cs], spr[32:40, cs], psb[bkb][0:8, 0:CHW], ALU.add))
            bank_guard[bka] = ("dv", v_dst)
            bank_guard[bkb] = ("dv", v_dst)
        usc_marks = []
        usc_slot_guard = compaction.slot_guard
        uwr_totals = (p.cnt["us0"], p.cnt["us1"])
        for t in range(TT):
            vi_marks = [0, 0]
            bk = t % 2
            s0, v0 = bank_guard[bk]
            p_t = p.emit("tensor", lambda e, t=t, bk=bk: e.transpose(
                psb[bk][:, 0:8], spr[32:40, t * 128:(t + 1) * 128], idn8_sb[32:40, 0:8]),
                waits=[("dv", v_dst), (s0, v0)], inc="pe")
            gv = p.emit("vector", lambda e, t=t, bk=bk: e.tensor_copy(
                tmp8b[:, t * 8:(t + 1) * 8], psb[bk][:, 0:8]),
                waits=[("pe", p_t)])
            bank_guard[bk] = ("dv", gv)
            p.emit("vector", lambda e, t=t: e.tensor_reduce(
                dcf[:, t * 8: t * 8 + 1], tmp8b[:, t * 8:(t + 1) * 8],
                axis=AX.X, op=ALU.min))
            p.emit("vector", lambda e, t=t: e.tensor_scalar(
                tmp8[:, t * 8:(t + 1) * 8], tmp8b[:, t * 8:(t + 1) * 8],
                dcf[:, t * 8: t * 8 + 1], BIG, ALU.is_le, ALU.mult))
            p.emit("vector", lambda e, t=t: e.tensor_tensor(
                tmp8[:, t * 8:(t + 1) * 8], tmp8[:, t * 8:(t + 1) * 8],
                tmp8b[:, t * 8:(t + 1) * 8], ALU.add))
            p.emit("vector", lambda e, t=t: e.tensor_reduce(
                dcf[:, t * 8 + 1: t * 8 + 2], tmp8[:, t * 8:(t + 1) * 8],
                axis=AX.X, op=ALU.min))
            for r_ in range(2):
                p.emit("vector", lambda e, t=t, r_=r_: e.tensor_scalar(
                    tmp8[:, t * 8:(t + 1) * 8], tmp8b[:, t * 8:(t + 1) * 8],
                    dcf[:, t * 8 + r_: t * 8 + r_ + 1], None, ALU.is_equal))
                p.emit("vector", lambda e, t=t: e.tensor_tensor(
                    tmp8[:, t * 8:(t + 1) * 8], tmp8[:, t * 8:(t + 1) * 8],
                    c_buf[:, t * 8:(t + 1) * 8], ALU.mult))
                p.emit("vector", lambda e, t=t, r_=r_: e.reduce_sum(
                    dcf[:, t * 8 + 6 + r_: t * 8 + 7 + r_],
                    tmp8[:, t * 8:(t + 1) * 8], axis=AX.X))
                p.emit("vector", lambda e, t=t, r_=r_: e.tensor_scalar(
                    tmp8[:, t * 8: t * 8 + 1], dcf[:, t * 8 + r_: t * 8 + r_ + 1],
                    float(8 * CAP) - 0.5, None, ALU.is_le))
                p.emit("vector", lambda e, t=t, r_=r_: e.tensor_tensor(
                    dcf[:, t * 8 + 6 + r_: t * 8 + 7 + r_],
                    dcf[:, t * 8 + 6 + r_: t * 8 + 7 + r_],
                    tmp8[:, t * 8: t * 8 + 1], ALU.mult))
                p.emit("vector", lambda e, t=t, r_=r_: e.tensor_tensor(
                    dcf[:, t * 8 + r_: t * 8 + r_ + 1],
                    dcf[:, t * 8 + r_: t * 8 + r_ + 1],
                    dmpc[:, 0:1], ALU.min))
                vi_marks[r_] = p.emit("vector", lambda e, t=t, r_=r_: e.tensor_copy(
                    dsti[:, t * 2 + r_: t * 2 + r_ + 1],
                    dcf[:, t * 8 + r_: t * 8 + r_ + 1]))
            slot = t % 2
            gv_l = p.emit("sync", lambda e, t=t, slot=slot: e.dma_start(
                out=arena[:, off_ug + slot * 2048: off_ug + slot * 2048 + 2048],
                in_=u_scr[t * 128:(t + 1) * 128, :]),
                waits=[(f"sc{slot}", usc_slot_guard[slot]),
                       ("us0", uwr_totals[0]), ("us1", uwr_totals[1]),
                       ("zin", ZINIT),
                       ("pe", ug_lastuse[slot])], inc=f"us{slot}", amt=16)
            for r_ in range(2):
                usc_marks.append(p.emit("gpsimd", lambda e, t=t, r_=r_, slot=slot: e.indirect_dma_start(
                    out=ug_dram,
                    out_offset=bass.IndirectOffsetOnAxis(dsti[:, t * 2 + r_: t * 2 + r_ + 1], 0),
                    in_=arena[:, off_ug + slot * 2048: off_ug + slot * 2048 + 2048],
                    in_offset=None),
                    waits=[("dv", vi_marks[r_]), (f"us{slot}", gv_l),
                           ("sc0", p.cnt["sc0"]), ("sc1", p.cnt["sc1"])],
                    inc=f"sc{slot}", amt=16))
            usc_slot_guard[slot] = p.cnt[f"sc{slot}"]
        USCT[0] = (p.cnt["us0"], p.cnt["us1"])
        return (("sc0", p.cnt["sc0"]), ("sc1", p.cnt["sc1"]))

    ug_lastuse = [0, 0]
    dstg_guard = [("dv", 0), ("dv", 0)]

    def sparse_expert(ex, usc_total, uT_ready, pair_ctr, gu_pre=None):
        gj_src = "gw"
        utg_ready = 0
        for ti, (base, rows) in enumerate(CTILES):
            slot = ti % 2
            gv = p.emit("scalar", lambda e, ex=ex, base=base, rows=rows, slot=slot: e.dma_start(
                out=arena[0:rows, off_ug + slot * 2048: off_ug + slot * 2048 + 2048],
                in_=ug_dram[ex * CAP + base: ex * CAP + base + rows, :]),
                waits=[usc_total[0], usc_total[1], (f"us{slot}", p.cnt[f"us{slot}"]),
                       ("pe", ug_lastuse[slot])], inc=f"us{slot}", amt=16)
            p_t = 0
            for c in range(KH):
                half = c // 8
                s0, v0 = ptb_guard[half]
                p_t = p.emit("tensor", lambda e, c=c, rows=rows, slot=slot, half=half: e.transpose(
                    ptb[half][:, (c % 8) * 128:(c % 8) * 128 + rows],
                    arena[0:rows, off_ug + slot * 2048 + c * 128: off_ug + slot * 2048 + (c + 1) * 128],
                    idn_sb[0:rows, 0:rows]),
                    waits=[(f"us{slot}", gv), (s0, v0), ("ld_y", CONSTS)],
                    inc="pe" if c % 8 == 7 else None)
                if c % 8 == 7:
                    hv = p.emit("vector", lambda e, c=c, base=base, rows=rows, half=half: e.tensor_copy(
                        arena[:, off_utg: off_utg + KH * CAP].rearrange(
                            "z (cc q) -> z cc q", q=CAP)[:, c - 7:c + 1, base:base + rows],
                        ptb[half][:, :].rearrange("z (cc q) -> z cc q", q=128)[:, :, 0:rows]),
                        waits=[("pe", p_t)])
                    ptb_guard[half] = ("dv", hv)
                    utg_ready = hv
            ug_lastuse[slot] = p_t
        wd_pre = {}
        for m in range(KI):
            gj = (ex * KI + m) % 2
            wg = wdma("g", gj, "gw", ex * KI + m)
            wu = wdma("u", gj, "uw", ex * KI + m)
            emit_scr_piece()
            if m in (8, 9):
                n0 = m - 8
                wd_pre[n0] = wdma("d", (ex * 4 + n0) % 2, "dw", ex * 4 + n0)
            NH = CAP // 2
            pr = pair_ctr[0] % 3
            pair_ctr[0] += 1
            pu = pair_ctr[0] % 3
            pair_ctr[0] += 1
            bG = (2 * pr, 2 * pr + 1)
            bU = (2 * pu, 2 * pu + 1)
            gw0, gw1 = bank_guard[bG[0]], bank_guard[bG[1]]
            p_G = None
            for k in range(KH):
                for h in range(2):
                    last = (k == KH - 1 and h == 1)
                    p_G = p.emit("tensor", lambda e, k=k, h=h, gj=gj, bG=bG, NH=NH: e.matmul(
                        psb[bG[h]][:, 0:NH], a_view("g", gj, k * 128, 128),
                        arena[:, off_utg + k * CAP + h * NH: off_utg + k * CAP + (h + 1) * NH],
                        start=(k == 0), stop=(k == KH - 1)),
                        waits=[(f"ld_g{gj}", wg), ("dv", utg_ready), gw0, gw1],
                        inc="pe" if last else None)
            gw0, gw1 = bank_guard[bU[0]], bank_guard[bU[1]]
            p_U = None
            for k in range(KH):
                for h in range(2):
                    last = (k == KH - 1 and h == 1)
                    p_U = p.emit("tensor", lambda e, k=k, h=h, gj=gj, bU=bU, NH=NH: e.matmul(
                        psb[bU[h]][:, 0:NH], a_view("u", gj, k * 128, 128),
                        arena[:, off_utg + k * CAP + h * NH: off_utg + k * CAP + (h + 1) * NH],
                        start=(k == 0), stop=(k == KH - 1)),
                        waits=[(f"ld_u{gj}", wu), gw0, gw1],
                        inc="pe" if last else None)
            buf_lastuse[("g", gj)] = p_G
            buf_lastuse[("u", gj)] = p_U
            for h in range(2):
                slot = (m % 2) * 1024 + h * 512
                tsw = tsil_guard[m % 2]
                a_sil = p.emit("scalar", lambda e, h=h, slot=slot, bG=bG, NH=NH: e.activation(
                    tsil[:, slot: slot + NH], psb[bG[h]][:, 0:NH], AF.Silu),
                    waits=[("pe", p_G), ("dv", tsw), ("ld_y", CONSTS),
                           ("us1", LNUS[1])])
                bank_guard[bG[h]] = ("ac", a_sil)
                hv2 = m * CAP + h * NH
                dvv = p.emit("vector", lambda e, h=h, slot=slot, bU=bU, hv2=hv2, NH=NH: e.tensor_tensor(
                    arena[:, off_h + hv2: off_h + hv2 + NH], psb[bU[h]][:, 0:NH],
                    tsil[:, slot: slot + NH], ALU.mult),
                    waits=[("pe", p_U), ("ac", a_sil)])
                bank_guard[bU[h]] = ("dv", dvv)
                tsil_guard[m % 2] = dvv
        hbuf_ready = p.cnt["dv"]
        for n_ in range(4):
            dj = (ex * 4 + n_) % 2
            wd = wd_pre[n_] if n_ in wd_pre else wdma("d", dj, "dw", ex * 4 + n_)
            p_dn_last = 0
            for ti, (base, rows) in enumerate(CTILES):
                pr = pair_ctr[0] % 3
                pair_ctr[0] += 1
                bks = (2 * pr, 2 * pr + 1)
                gw0, gw1 = bank_guard[bks[0]], bank_guard[bks[1]]
                p_dn = None
                for k in range(KI):
                    for h in range(2):
                        last = (k == KI - 1 and h == 1)
                        p_dn = p.emit("tensor", lambda e, k=k, h=h, base=base, rows=rows, dj=dj, bks=bks: e.matmul(
                            psb[bks[h]][0:rows, 0:256],
                            arena[:, off_h + k * CAP + base: off_h + k * CAP + base + rows],
                            a_view("d", dj, k * 512 + h * 256, 256),
                            start=(k == 0), stop=(k == KI - 1)),
                            waits=[(f"ld_d{dj}", wd), ("dv", hbuf_ready), gw0, gw1],
                            inc="pe" if last else None)
                p_dn_last = p_dn
                slot = (ex * 4 + n_ + ti) % 2
                sg0, sv0 = dstg_guard[slot]
                cva = p.emit("vector", lambda e, rows=rows, slot=slot, bks=bks: e.tensor_copy(
                    dstg[0:rows, slot * 512: slot * 512 + 256], psb[bks[0]][0:rows, 0:256]),
                    waits=[("pe", p_dn), (sg0, sv0)])
                cvb = p.emit("vector", lambda e, rows=rows, slot=slot, bks=bks: e.tensor_copy(
                    dstg[0:rows, slot * 512 + 256: slot * 512 + 512], psb[bks[1]][0:rows, 0:256]),
                    waits=[("pe", p_dn)])
                bank_guard[bks[0]] = ("dv", cva)
                bank_guard[bks[1]] = ("dv", cvb)
                gv = p.emit("sync", lambda e, ex=ex, base=base, rows=rows, n_=n_, slot=slot: e.dma_start(
                    out=ogd[ex * CAP + base: ex * CAP + base + rows, n_ * 512:(n_ + 1) * 512],
                    in_=dstg[0:rows, slot * 512: slot * 512 + 512]),
                    waits=[("dv", cvb)], inc=f"dg{slot}", amt=16)
                dstg_guard[slot] = (f"dg{slot}", gv)
            buf_lastuse[("d", dj)] = p_dn_last

    def shared_expert(hf, uT_ready, pair_ctr, gu_pre=None):
        fin_tiles = [0] * TT
        wd_pre = {}
        for m in range(KI):
            gj = ((8 + hf) * KI + m) % 2
            if gu_pre and m in gu_pre:
                wg, wu = gu_pre[m]
            else:
                wg = wdma("g", gj, "sg", hf * KI + m)
                wu = wdma("u", gj, "su", hf * KI + m)
            emit_scr_piece()
            if m in (8, 9):
                n0 = m - 8
                wd_pre[n0] = wdma("d", n0 % 2, "sd", hf * 4 + n0)
            pr = pair_ctr[0] % 3
            pair_ctr[0] += 1
            pu = pair_ctr[0] % 3
            pair_ctr[0] += 1
            bG = (2 * pr, 2 * pr + 1)
            bU = (2 * pu, 2 * pu + 1)
            gw0, gw1 = bank_guard[bG[0]], bank_guard[bG[1]]
            p_G = None
            for k in range(KH):
                for h in range(2):
                    last = (k == KH - 1 and h == 1)
                    p_G = p.emit("tensor", lambda e, k=k, h=h, gj=gj, bG=bG: e.matmul(
                        psb[bG[h]][:, 0:HGT], a_view("g", gj, k * 128, 128),
                        uT[:, k * GT + h * HGT: k * GT + (h + 1) * HGT],
                        start=(k == 0), stop=(k == KH - 1)),
                        waits=[(f"ld_g{gj}", wg), uT_ready[0], uT_ready[1], gw0, gw1],
                        inc="pe" if last else None)
            gw0, gw1 = bank_guard[bU[0]], bank_guard[bU[1]]
            p_U = None
            for k in range(KH):
                for h in range(2):
                    last = (k == KH - 1 and h == 1)
                    p_U = p.emit("tensor", lambda e, k=k, h=h, gj=gj, bU=bU: e.matmul(
                        psb[bU[h]][:, 0:HGT], a_view("u", gj, k * 128, 128),
                        uT[:, k * GT + h * HGT: k * GT + (h + 1) * HGT],
                        start=(k == 0), stop=(k == KH - 1)),
                        waits=[(f"ld_u{gj}", wu), gw0, gw1],
                        inc="pe" if last else None)
            buf_lastuse[("g", gj)] = p_G
            buf_lastuse[("u", gj)] = p_U
            for h in range(2):
                slot = (m % 2) * 1024 + h * 512
                tsw = tsil_guard[m % 2]
                col = hf * KI + m
                a_sil = p.emit("scalar", lambda e, h=h, slot=slot, bG=bG, col=col: e.activation(
                    tsil[:, slot: slot + HGT], psb[bG[h]][:, 0:HGT], AF.Silu,
                    bias=bsg_sb[:, col: col + 1], scale=1.0),
                    waits=[("pe", p_G), ("dv", tsw), ("ld_y", CONSTS),
                           ("us1", LNUS[1])])
                bank_guard[bG[h]] = ("ac", a_sil)
                hv = m * GT + h * HGT
                dvv = p.emit("vector", lambda e, h=h, slot=slot, bU=bU, hv=hv, col=col: e.scalar_tensor_tensor(
                    arena[:, off_h + hv: off_h + hv + HGT], psb[bU[h]][:, 0:HGT],
                    bsu_sb[:, col: col + 1], tsil[:, slot: slot + HGT],
                    ALU.add, ALU.mult),
                    waits=[("pe", p_U), ("ac", a_sil), ("ld_y", CONSTS)])
                bank_guard[bU[h]] = ("dv", dvv)
                tsil_guard[m % 2] = dvv
        hbuf_ready = p.cnt["dv"]
        for n_ in range(4):
            dj = n_ % 2
            wd = wd_pre[n_] if n_ in wd_pre else wdma("d", dj, "sd", hf * 4 + n_)
            p_dn_last = 0
            for t in range(TT):
                pr = pair_ctr[0] % 3
                pair_ctr[0] += 1
                bks = (2 * pr, 2 * pr + 1)
                gw0, gw1 = bank_guard[bks[0]], bank_guard[bks[1]]
                p_dn = None
                for k in range(KI):
                    for h in range(2):
                        last = (k == KI - 1 and h == 1)
                        p_dn = p.emit("tensor", lambda e, k=k, h=h, t=t, dj=dj, bks=bks: e.matmul(
                            psb[bks[h]][:, 0:256],
                            arena[:, off_h + k * GT + t * 128: off_h + k * GT + t * 128 + 128],
                            a_view("d", dj, k * 512 + h * 256, 256),
                            start=(k == 0), stop=(k == KI - 1)),
                            waits=[(f"ld_d{dj}", wd), ("dv", hbuf_ready), gw0, gw1],
                            inc="pe" if last else None)
                p_dn_last = p_dn
                for h in range(2):
                    rcol = t * H + n_ * 512 + h * 256
                    dvv = p.emit("vector", lambda e, h=h, rcol=rcol, bks=bks: e.scalar_tensor_tensor(
                        rbuf[:, rcol: rcol + 256], psb[bks[h]][:, 0:256],
                        0.5, rbuf[:, rcol: rcol + 256], ALU.mult, ALU.add),
                        waits=[("pe", p_dn)])
                    bank_guard[bks[h]] = ("dv", dvv)
                fin_tiles[t] = p.cnt["dv"]
            buf_lastuse[("d", dj)] = p_dn_last
        return fin_tiles

    SPRG = [0]
    ACCG = [0]
    def ffn_phase(g, uT_ready, pair_ctr=[0]):
        gu_pre = {}
        for m in range(2):
            gu_pre[m] = (wdma("g", m % 2, "sg", m), wdma("u", m % 2, "su", m))
        usc_total = compaction(g, BSDH_W)
        SPRG[0] = p.cnt["pe"]
        fin_sh = [0] * TT
        for hf in range(2):
            fin_sh = shared_expert(hf, uT_ready, pair_ctr,
                                   gu_pre=gu_pre if hf == 0 else None)
        for ex in range(8):
            sparse_expert(ex, usc_total, uT_ready, pair_ctr)
        ogd_total = (p.cnt["dg0"], p.cnt["dg1"])
        fin_tiles = list(fin_sh)
        pe_experts = p.cnt["pe"]
        NSL = min(4, (27 * CAP) // 1024)
        gslot_guard = [("dv", 0)] * NSL
        acc_marks = [0] * TT
        gi = 0
        for t in range(TT):
            for r_ in range(2):
                for n2 in range(2):
                    slot = gi % NSL
                    gi += 1
                    sg0, sv0 = gslot_guard[slot]
                    gv = p.emit("gpsimd", lambda e, t=t, r_=r_, n2=n2, slot=slot: e.indirect_dma_start(
                        out=arena[:, off_h + slot * 1024: off_h + slot * 1024 + 1024],
                        out_offset=None,
                        in_=ogd,
                        in_offset=bass.IndirectOffsetOnAxis(dsti[:, t * 2 + r_: t * 2 + r_ + 1], 0),
                        element_offset=n2 * 1024),
                        waits=[("dg0", ogd_total[0]), ("dg1", ogd_total[1]),
                               ("pe", pe_experts), (sg0, sv0)],
                        inc=f"gg{slot}", amt=16)
                    dvv = p.emit("vector", lambda e, t=t, r_=r_, n2=n2, slot=slot: e.scalar_tensor_tensor(
                        rbuf[:, t * H + n2 * 1024: t * H + (n2 + 1) * 1024],
                        arena[:, off_h + slot * 1024: off_h + slot * 1024 + 1024],
                        dcf[:, t * 8 + 6 + r_: t * 8 + 7 + r_],
                        rbuf[:, t * H + n2 * 1024: t * H + (n2 + 1) * 1024],
                        ALU.mult, ALU.add),
                        waits=[(f"gg{slot}", gv)])
                    gslot_guard[slot] = ("dv", dvv)
                    if r_ == 1 and n2 == 1:
                        fin_tiles[t] = dvv
            acc_marks[t] = p.emit("gpsimd", lambda e, t=t: e.dma_start(
                out=rbuf[:, t * H:(t + 1) * H], in_=bsdh, accum_op=ALU.add),
                waits=[("dv", fin_tiles[t]), ("bsx", BSDH_W),
                       ("sct", p.cnt["sct"])], inc="sct", amt=16)
        buf_lastuse[("wi", 0)] = max(buf_lastuse[("wi", 0)], p.cnt["pe"])
        buf_lastuse[("wi", 1)] = max(buf_lastuse[("wi", 1)], p.cnt["pe"])
        ACCG[0] = acc_marks[-1]
        ACCG_T[:] = acc_marks
        return fin_tiles, acc_marks

    def attn_phase(g, uT_ready):
        for sg_i in range(2):
            for m in range(32):
                wj = m % 3
                wv = wdma("wi", wj, "wi", m)
                pr = m % 3
                bks = (2 * pr, 2 * pr + 1)
                gg = [bank_guard[bks[0]], bank_guard[bks[1]]]
                p_xt = None
                for k in range(KH):
                    for q in range(2):
                        last = (k == KH - 1 and q == 1)
                        p_xt = p.emit("tensor", lambda e, k=k, q=q, wj=wj, bks=bks, sg_i=sg_i: e.matmul(
                            psb[bks[q]][:, 0:NQ],
                            a_view("wi", wj, k * 128, 128),
                            uT[:, k * GT + sg_i * SGT + q * NQ: k * GT + sg_i * SGT + (q + 1) * NQ],
                            start=(k == 0), stop=(k == KH - 1)),
                            waits=[(f"ld_wi{wj}", wv), uT_ready[0], uT_ready[1], gg[0], gg[1]],
                            inc="pe" if last else None)
                buf_lastuse[("wi", wj)] = p_xt
                for q in range(2):
                    gv = p.emit("vector", lambda e, m=m, q=q, bks=bks: e.tensor_copy(
                        arena[:, off_xT + m * SGT + q * NQ: off_xT + m * SGT + (q + 1) * NQ],
                        psb[bks[q]][:, 0:NQ]),
                        waits=[("pe", p_xt)])
                    bank_guard[bks[q]] = ("dv", gv)
            xT_ready = p.cnt["dv"]
            pB_last = 0
            for m in range(32):
                cj = m % 2
                zj = m % 3
                wc = wdma("cvb", cj, "cv", m)
                wz = wdma("wi", zj, "wi", 32 + m)
                zpr = (2 * m) % 3
                cpr = (2 * m + 1) % 3
                bz = (2 * zpr, 2 * zpr + 1)
                bc = (2 * cpr, 2 * cpr + 1)
                gg = [bank_guard[bz[0]], bank_guard[bz[1]]]
                p_z = None
                for k in range(KH):
                    for q in range(2):
                        last = (k == KH - 1 and q == 1)
                        p_z = p.emit("tensor", lambda e, k=k, q=q, zj=zj, bz=bz, sg_i=sg_i: e.matmul(
                            psb[bz[q]][:, 0:NQ],
                            a_view("wi", zj, k * 128, 128),
                            uT[:, k * GT + sg_i * SGT + q * NQ: k * GT + sg_i * SGT + (q + 1) * NQ],
                            start=(k == 0), stop=(k == KH - 1)),
                            waits=[(f"ld_wi{zj}", wz), gg[0], gg[1]],
                            inc="pe" if last else None)
                buf_lastuse[("wi", zj)] = p_z
                gg = [bank_guard[bc[0]], bank_guard[bc[1]]]
                p_xc = None
                for k in range(32):
                    for q in range(2):
                        last = (k == 31 and q == 1)
                        p_xc = p.emit("tensor", lambda e, k=k, q=q, cj=cj, bc=bc: e.matmul(
                            psb[bc[q]][:, 0:NQ],
                            a_view("cvb", cj, k * 128, 128),
                            arena[:, off_xT + k * SGT + q * NQ: off_xT + k * SGT + (q + 1) * NQ],
                            start=(k == 0), stop=(k == 31)),
                            waits=[(f"ld_cvb{cj}", wc), ("dv", xT_ready), gg[0], gg[1]],
                            inc="pe" if last else None)
                buf_lastuse[("cvb", cj)] = p_xc
                pB_last = p_xc
                slot = (m % 2) * SGT
                for q in range(2):
                    tsw = tsil_guard[m % 2]
                    a_z = p.emit("scalar", lambda e, slot=slot, q=q, bz=bz: e.activation(
                        tsil[:, slot + q * NQ: slot + (q + 1) * NQ],
                        psb[bz[q]][:, 0:NQ], AF.Silu),
                        waits=[("pe", p_z), ("dv", tsw), uT_ready[0], uT_ready[1],
                               ("us1", LNUS[1])])
                    bank_guard[bz[q]] = ("ac", a_z)
                    gv = p.emit("vector", lambda e, m=m, slot=slot, q=q, bc=bc: e.scalar_tensor_tensor(
                        arena[:, off_sb_ + m * SGT + q * NQ: off_sb_ + m * SGT + (q + 1) * NQ],
                        psb[bc[q]][:, 0:NQ], dvec_sb[:, m: m + 1],
                        tsil[:, slot + q * NQ: slot + (q + 1) * NQ], ALU.mult, ALU.mult),
                        waits=[("pe", p_xc), ("ac", a_z), ("ld_y", CONSTS)])
                    bank_guard[bc[q]] = ("dv", gv)
                    tsil_guard[m % 2] = gv
            sb_ready = p.cnt["dv"]
            wo_all = {}
            def emit_wo(ip):
                if ip <= 7:
                    wj = ip % 3
                    wo_all[ip] = (wj, wdma("wob", wj, "wo", ip,
                                           extra_pe_guard=pB_last))
            for ip in range(3):
                emit_wo(ip)
            for n in range(4):
                wvs = [wo_all[n * 2], wo_all[n * 2 + 1]]
                for t in range(STT):
                    pc = (n * STT + t) % 3
                    bks = (2 * pc, 2 * pc + 1)
                    gg = [bank_guard[bks[0]], bank_guard[bks[1]]]
                    p_o = None
                    for k in range(32):
                        kh = k // 16
                        wj, wv = wvs[kh]
                        for q in range(2):
                            last = (k == 31 and q == 1)
                            p_o = p.emit("tensor", lambda e, k=k, q=q, t=t, wj=wj, bks=bks: e.matmul(
                                psb[bks[q]][:, 0:256],
                                arena[:, off_sb_ + k * SGT + t * 128: off_sb_ + k * SGT + t * 128 + 128],
                                a_view("wob", wj, (k % 16) * 512 + q * 256, 256),
                                start=(k == 0), stop=(k == 31)),
                                waits=[(f"ld_wob{wj}", wv), ("dv", sb_ready), gg[0], gg[1]],
                                inc="pe" if last else None)
                    buf_lastuse[("wob", wvs[0][0])] = p_o
                    buf_lastuse[("wob", wvs[1][0])] = p_o
                    rcol = (sg_i * STT + t) * H + n * 512
                    for q in range(2):
                        gv = p.emit("vector", lambda e, rcol=rcol, q=q, bks=bks: e.tensor_tensor(
                            rbuf[:, rcol + q * 256: rcol + (q + 1) * 256],
                            psb[bks[q]][:, 0:256],
                            rbuf[:, rcol + q * 256: rcol + (q + 1) * 256], ALU.add),
                            waits=[("pe", p_o), ("sct", ACCG[0])])
                        bank_guard[bks[q]] = ("dv", gv)
                emit_wo(n * 2 + 3)
                emit_wo(n * 2 + 4)
            af_sg = p.cnt["pe"]
            for key in (("wi", 0), ("wi", 1), ("wi", 2), ("cvb", 0), ("cvb", 1)):
                buf_lastuse[key] = max(buf_lastuse[key], af_sg)
        af = p.cnt["pe"]
        for key in (("g", 0), ("g", 1), ("u", 0), ("u", 1), ("d", 0), ("d", 1)):
            buf_lastuse[key] = max(buf_lastuse[key], af)
        ug_lastuse[0] = max(ug_lastuse[0], af)
        ug_lastuse[1] = max(ug_lastuse[1], af)

    st_cnt = 0
    fin_tiles = [0] * TT
    for g in range(NG):
        NPIECE = min(4, TT)
        for piece in range(NPIECE):
            waits = []
            if g > 0 and piece == 0:
                waits.append(("st", st_cnt))
            if piece > 0:
                waits.append(("ld_x", (g * NPIECE + piece) * 16))
            p.emit("sync", lambda e, g=g, piece=piece, NPIECE=NPIECE: e.dma_start(
                out=rbuf[:, piece * (TT // NPIECE) * H:(piece + 1) * (TT // NPIECE) * H]
                    .rearrange("q (t h) -> q t h", h=H),
                in_=ap["x"][g * GT + piece * (GT // NPIECE): g * GT + (piece + 1) * (GT // NPIECE), :]
                    .rearrange("(t q) h -> q t h", q=128)),
                waits=waits, inc="ld_x", amt=16)
        for sub in range(3):
            SUBIDX[0] = sub
            uT_ready = ln_phase(g)
            if sub in (0, 2):
                router_phase(uT_ready)
                fin_tiles, acc_marks = ffn_phase(g, uT_ready)
            else:
                attn_phase(g, uT_ready)
        for piece in range(2):
            fin = max(fin_tiles[piece * (TT // 2): (piece + 1) * (TT // 2)])
            am = max(acc_marks[piece * (TT // 2): (piece + 1) * (TT // 2)])
            st_cnt = p.emit("sync", lambda e, g=g, piece=piece: e.dma_start(
                out=out_ap[g * GT + piece * (GT // 2): g * GT + (piece + 1) * (GT // 2), :]
                    .rearrange("(t q) h -> q t h", q=128),
                in_=rbuf[:, piece * (TT // 2) * H:(piece + 1) * (TT // 2) * H]
                    .rearrange("q (t h) -> q t h", h=H)),
                waits=[("dv", fin), ("sct", am)], inc="st", amt=16)
    p.emit("sync", None, waits=[("st", st_cnt)])

    with es:
        block = es.enter_context(nc.Block())
        for eng in Plan.ENGINES:
            ops = p.ops[eng]
            if not ops:
                continue
            def body(e, ops=ops):
                hwm = {}
                for waits, fn, inc, amt in ops:
                    for s, v in waits:
                        if hwm.get(s, 0) < v:
                            e.wait_ge(sems[s], v)
                            hwm[s] = v
                    if fn is not None:
                        ins = fn(e)
                        if inc is not None:
                            ins.then_inc(sems[inc], amt)
            getattr(block, eng)(body)
    return nc


def pack_inputs(i, GT=1024):
    import os as _os
    TT = GT // 128
    _capo = _os.environ.get("KCAP")
    CAP = int(_capo) if _capo else (320 if GT == 1024 else GT // 2)
    c = np.ascontiguousarray
    d = {}
    d["gw"] = c(np.asarray(i["L0_ffn_gate_w"]).reshape(8, 16, 128, 11, 128)
                .transpose(0, 3, 2, 1, 4).reshape(88, 128, 2048))
    d["uw"] = c(np.asarray(i["L0_ffn_up_w"]).reshape(8, 16, 128, 11, 128)
                .transpose(0, 3, 2, 1, 4).reshape(88, 128, 2048))
    d["dw"] = c(np.asarray(i["L0_ffn_down_w"]).reshape(8, 11, 128, 4, 512)
                .transpose(0, 3, 2, 1, 4).reshape(32, 128, 5632))
    d["sg"] = c(np.asarray(i["L0_ffn_s_gate_w"]).reshape(16, 128, 2, 11, 128)
                .transpose(2, 3, 1, 0, 4).reshape(22, 128, 2048))
    d["su"] = c(np.asarray(i["L0_ffn_s_up_w"]).reshape(16, 128, 2, 11, 128)
                .transpose(2, 3, 1, 0, 4).reshape(22, 128, 2048))
    d["sd"] = c(np.asarray(i["L0_ffn_s_down_w"]).reshape(2, 11, 128, 4, 512)
                .transpose(0, 3, 2, 1, 4).reshape(8, 128, 5632))
    d["wi"] = c(np.asarray(i["L0_attn_W_in"]).reshape(64, 128, 16, 128)
                .transpose(0, 3, 2, 1).reshape(64, 128, 2048))
    d["cv"] = c(np.asarray(i["L0_attn_conv_w"]).reshape(32, 128, 32, 128)
                .transpose(0, 3, 2, 1).reshape(32, 128, 4096))
    d["wo"] = c(np.asarray(i["L0_attn_W_out"]).reshape(4, 512, 2, 16, 128)
                .transpose(0, 2, 4, 3, 1).reshape(8, 128, 8192))
    d["rw"] = c(np.asarray(i["L0_ffn_router_w"]).reshape(8, 16, 128)
                .transpose(2, 1, 0).reshape(128, 128))
    d["dvec"] = c(np.asarray(i["L0_attn_D"]).reshape(32, 128).T)
    d["bsg"] = c(np.asarray(i["L0_ffn_s_gate_b"]).reshape(22, 128).T)
    d["bsu"] = c(np.asarray(i["L0_ffn_s_up_b"]).reshape(22, 128).T)
    d["bsdb"] = c(np.broadcast_to(np.asarray(i["L0_ffn_s_down_b"]), (128, 2048)))
    d["idn"] = np.eye(128, dtype=BFNP)
    d["idnf"] = np.eye(128, dtype=np.float32)
    d["toki"] = c((np.arange(TT)[None, :] * 128
                   + np.arange(128)[:, None]).astype(np.float32))
    d["ecst"] = c(((np.arange(128) % 8) * CAP).astype(np.float32).reshape(128, 1))
    d["idn8"] = c((np.arange(128)[:, None] % 8 == np.arange(8)[None, :])
                  .astype(np.float32))
    return d


def kernel(**inputs):
    shared = pack_inputs(inputs)
    x = np.ascontiguousarray(np.asarray(inputs["L0_attn_hidden_states"],
                                        dtype=np.float32).reshape(16384, 2048))
    nc = build_program(GT=1024, TOKC=2048)
    in_maps = []
    for cidx in range(N_CORES):
        m = dict(shared)
        m["x"] = np.ascontiguousarray(x[cidx * 2048:(cidx + 1) * 2048])
        in_maps.append(m)
    res = run_bass_kernel_spmd(nc, in_maps, core_ids=list(range(N_CORES)))
    out = np.concatenate([np.asarray(r["out"]) for r in res.results], axis=0)
    return out.reshape(1, 16384, 2048).astype(np.float32)


# version 13
# speedup vs baseline: 1.4969x; 1.0280x over previous
import numpy as np
import ml_dtypes
from contextlib import ExitStack

import concourse.bass as bass
import concourse.mybir as mybir
from concourse.bass_utils import run_bass_kernel_spmd

F32 = mybir.dt.float32
BF16 = mybir.dt.bfloat16
AF = mybir.ActivationFunctionType
ALU = mybir.AluOpType
AX = mybir.AxisListType
BFNP = ml_dtypes.bfloat16
I32 = mybir.dt.int32

H = 2048
KH = H // 128
I_EXP = 1408
KI = I_EXP // 128
N_CORES = 8


class Plan:
    ENGINES = ("tensor", "vector", "scalar", "gpsimd", "sync")
    SELF = {"vector": "dv", "scalar": "ac"}

    def __init__(self):
        self.ops = {e: [] for e in self.ENGINES}
        self.cnt = {}

    def sem(self, name):
        self.cnt.setdefault(name, 0)

    def emit(self, eng, fn, waits=(), inc=None, amt=1):
        if eng in self.SELF and fn is not None and (inc is None or inc == self.SELF[eng]):
            s = self.SELF[eng]
            waits = tuple(waits) + ((s, self.cnt[s]),)
            if inc is None:
                inc, amt = s, 1
        for s, v in waits:
            assert s in self.cnt, s
            assert v <= self.cnt[s], ("wait ahead of plan", s, v, self.cnt[s])
        self.ops[eng].append((tuple(waits), fn, inc, amt))
        if inc is not None:
            self.cnt[inc] += amt
            return self.cnt[inc]
        return None


def build_program(GT=1024, TOKC=2048):
    NG = TOKC // GT
    TT = GT // 128
    SGT = GT // 2
    STT = TT // 2
    NQ = GT // 4
    HGT = GT // 2

    nc = bass.Bass("TRN2", target_bir_lowering=False)
    ap = {}
    def din(name, shape, dt=F32):
        ap[name] = nc.dram_tensor(name, shape, dt, kind="ExternalInput").ap()
    din("x", [TOKC, H])
    din("gw", [88, 128, KH * 128])
    din("uw", [88, 128, KH * 128])
    din("dw", [32, 128, KI * 512])
    din("sg", [22, 128, KH * 128])
    din("su", [22, 128, KH * 128])
    din("sd", [8, 128, KI * 512])
    din("wi", [64, 128, KH * 128])
    din("cv", [32, 128, 32 * 128])
    din("wo", [8, 128, 16 * 512])
    din("rw", [128, 128])
    din("dvec", [128, 32])
    din("bsg", [128, 22])
    din("bsu", [128, 22])
    din("bsdb", [128, H])
    din("idn", [128, 128], BF16)
    din("idnf", [128, 128])
    din("toki", [128, TT])
    din("ecst", [128, 1])
    din("idn8", [128, 8])
    out_ap = nc.dram_tensor("out", [TOKC, H], F32, kind="ExternalOutput").ap()
    import os as _os
    _capo = _os.environ.get("KCAP")
    CAP = int(_capo) if _capo else (320 if GT == 1024 else GT // 2)
    CTILES = []
    _b = 0
    while _b < CAP:
        CTILES.append((_b, min(128, CAP - _b)))
        _b += 128
    u_scr = nc.dram_tensor("u_scr", [GT, H], BF16, kind="Internal").ap()
    ug_dram = nc.dram_tensor("ug_dram", [8 * CAP + 128, H], BF16, kind="Internal").ap()
    ogd = nc.dram_tensor("ogd", [8 * CAP + 128, H], BF16, kind="Internal").ap()
    bsdh = nc.dram_tensor("bsdh", [128, H], F32, kind="Internal").ap()
    scr = {}
    scr["wi"] = nc.dram_tensor("scr_wi", [64, 128, KH * 128], BF16, kind="Internal").ap()
    scr["cv"] = nc.dram_tensor("scr_cv", [32, 128, 32 * 128], BF16, kind="Internal").ap()
    scr["wo"] = nc.dram_tensor("scr_wo", [8, 128, 16 * 512], BF16, kind="Internal").ap()

    p = Plan()
    for s in ("pe", "dv", "ac", "st", "ld_y", "ld_x", "ld_c", "scr_c", "ga", "sct", "bsx", "us0", "us1", "sc0", "sc1", "dg0", "dg1", "gg0", "gg1", "gg2", "gg3", "zin"):
        p.sem(s)
    BUFN = (("g", 2), ("u", 2), ("d", 2), ("wi", 3), ("cvb", 2), ("wob", 3))
    for pref, n in BUFN:
        for i in range(n):
            p.sem(f"ld_{pref}{i}")

    es = ExitStack()
    def sbuf(name, cols, dt):
        return es.enter_context(nc.sbuf_tensor(name, [128, cols], dt))

    rbuf = sbuf("rbuf", TT * H, F32)
    uT = sbuf("uT", KH * GT, BF16)
    ubf = sbuf("ubf", H, BF16)
    tsil = sbuf("tsil", 2048, BF16)
    spr = sbuf("spr", 1024, F32)
    dstg = sbuf("dstg", 1024, BF16)
    dcf = sbuf("dcf", 8 * TT, F32)
    dsti = sbuf("dsti", 2 * TT, I32)
    dmpc = sbuf("dmpc", 1, F32)
    stats = sbuf("stats", 8 * TT, F32)
    lgsb = sbuf("lgsb", 8 * TT, F32)
    esb = sbuf("esb", 8 * TT, F32)
    tmp8 = sbuf("tmp8", 8 * TT, F32)
    tmp8b = sbuf("tmp8b", 8 * TT, F32)
    st2 = sbuf("st2", 8 * TT, F32)
    c_buf = sbuf("c_buf", 8 * TT, F32)
    rw_bf = sbuf("rw_bf", 128, BF16)
    idn_sb = sbuf("idn_sb", 128, BF16)
    idnf_sb = sbuf("idnf_sb", 128, F32)
    toki_sb = sbuf("toki_sb", TT, F32)
    ecst_sb = sbuf("ecst_sb", 1, F32)
    idn8_sb = sbuf("idn8_sb", 8, F32)
    dvec_sb = sbuf("dvec_sb", 32, F32)
    bsg_sb = sbuf("bsg_sb", 22, F32)
    bsu_sb = sbuf("bsu_sb", 22, F32)

    off_h = 0
    off_utg = KI * CAP
    sz_h = max(KI * GT, off_utg + KH * CAP)
    off_g = [sz_h + j * 2048 for j in range(2)]
    off_u = [sz_h + 2 * 2048 + j * 2048 for j in range(2)]
    off_d = [sz_h + 4 * 2048 + j * (KI * 512) for j in range(2)]
    ffn_end = sz_h + 4 * 2048 + 2 * KI * 512
    off_xT = 0
    off_sb_ = 32 * SGT
    off_wi = [max(ffn_end, 64 * SGT) + j * 2048 for j in range(3)]
    off_cv = [off_wi[0] + 3 * 2048 + j * 4096 for j in range(2)]
    attn_end = off_cv[0] + 2 * 4096
    if GT == 1024:
        off_wo = [0, 8192, off_wi[0]]
        arena_cols = attn_end
    else:
        off_wo = [attn_end + j * 8192 for j in range(3)]
        arena_cols = attn_end + 3 * 8192
    off_ug = off_wi[0]
    arena = sbuf("arena", arena_cols, BF16)

    psb = [es.enter_context(nc.psum_tensor(f"pb{i}", [128, 512], F32)) for i in range(6)]
    ptb = [es.enter_context(nc.psum_tensor(f"pt{i}", [128, 1024], BF16)) for i in range(2)]
    sems = {}
    for name in p.cnt:
        sems[name] = es.enter_context(nc.semaphore(name))

    bank_guard = {i: ("dv", 0) for i in range(6)}
    ptb_guard = [("dv", 0), ("dv", 0)]
    buf_lastuse = {}
    buf_gen = {}
    for pref, n in BUFN:
        for j in range(n):
            buf_lastuse[(pref, j)] = 0
            buf_gen[(pref, j)] = 0
    tsil_guard = {0: 0, 1: 0}
    CONSTS = 8 * 16
    SCRT = [0]

    p.emit("sync", lambda e: e.dma_start(out=dvec_sb[:, :], in_=ap["dvec"]), inc="ld_y", amt=16)
    p.emit("sync", lambda e: e.dma_start(out=bsg_sb[:, :], in_=ap["bsg"]), inc="ld_y", amt=16)
    p.emit("sync", lambda e: e.dma_start(out=bsu_sb[:, :], in_=ap["bsu"]), inc="ld_y", amt=16)
    p.emit("sync", lambda e: e.dma_start(out=idn_sb[:, :], in_=ap["idn"]), inc="ld_y", amt=16)
    p.emit("sync", lambda e: e.dma_start(out=idnf_sb[:, :], in_=ap["idnf"]), inc="ld_y", amt=16)
    p.emit("sync", lambda e: e.dma_start(out=toki_sb[:, :], in_=ap["toki"]), inc="ld_y", amt=16)
    p.emit("sync", lambda e: e.dma_start(out=ecst_sb[:, :], in_=ap["ecst"]), inc="ld_y", amt=16)
    p.emit("sync", lambda e: e.dma_start(out=idn8_sb[:, :], in_=ap["idn8"]), inc="ld_y", amt=16)
    p.emit("gpsimd", lambda e: e.dma_start(out=rw_bf[:, :], in_=ap["rw"]), inc="ld_c", amt=16)
    scr_jobs = []
    for name, nblk, step in (("wi", 64, 2), ("cv", 32, 1), ("wo", 8, 1)):
        for a in range(0, nblk, step):
            scr_jobs.append((name, a, min(a + step, nblk)))
    SCRT[0] = len(scr_jobs) * 16

    def emit_scr_piece():
        if scr_jobs:
            name, a, bnd = scr_jobs.pop(0)
            p.emit("gpsimd", lambda e, name=name, a=a, bnd=bnd: e.dma_start(
                out=scr[name][a:bnd], in_=ap[name][a:bnd]), inc="scr_c", amt=16)

    p.emit("vector", lambda e: e.tensor_scalar(
        dmpc[:, 0:1], toki_sb[:, 0:1], float(8 * CAP), None, ALU.add),
        waits=[("ld_y", CONSTS)])
    BSDH_W = 0
    for hh in range(2):
        lv = p.emit("sync", lambda e, hh=hh: e.dma_start(
            out=spr[:, 0:1024], in_=ap["bsdb"][:, hh * 1024:(hh + 1) * 1024]),
            waits=[("bsx", BSDH_W)], inc="bsx", amt=16)
        v0 = p.emit("vector", lambda e: e.tensor_scalar(
            spr[:, 0:1024], spr[:, 0:1024], 0.5, None, ALU.mult),
            waits=[("bsx", lv)])
        BSDH_W = p.emit("sync", lambda e, hh=hh: e.dma_start(
            out=bsdh[:, hh * 1024:(hh + 1) * 1024], in_=spr[:, 0:1024]),
            waits=[("dv", v0)], inc="bsx", amt=16)

    vz = p.emit("vector", lambda e: e.memset(
        arena[:, off_ug: off_ug + 2048], 0.0))
    zr = 0
    nrow = 8 * CAP + 128
    a0 = 0
    while a0 < nrow:
        b0 = min(a0 + 128, nrow)
        for tgt in (ug_dram, ogd):
            zr = p.emit("sync", lambda e, a0=a0, b0=b0, tgt=tgt: e.dma_start(
                out=tgt[a0:b0, :], in_=arena[0:b0 - a0, off_ug: off_ug + 2048]),
                waits=[("dv", vz)], inc="zin", amt=16)
        a0 += 128
    ZINIT = zr

    WQ = {"g": "gpsimd", "u": "gpsimd", "d": "gpsimd",
          "wi": "sync", "cvb": "sync", "wob": "sync"}
    WSRC_SCR = {"wi": "wi", "cvb": "cv", "wob": "wo"}
    OFFS = {"g": off_g, "u": off_u, "d": off_d, "wi": off_wi, "cvb": off_cv, "wob": off_wo}
    NCOLS = {"g": 2048, "u": 2048, "d": KI * 512, "wi": 2048, "cvb": 4096, "wob": 8192}

    def wdma(stream, j, src_name, blk, extra_pe_guard=0):
        gen = buf_gen[(stream, j)] = buf_gen[(stream, j)] + 1
        guard = max(buf_lastuse[(stream, j)], extra_pe_guard)
        o = OFFS[stream][j]
        ncols = NCOLS[stream]
        waits = []
        if guard:
            waits.append(("pe", guard))
        if stream in WSRC_SCR:
            waits.append(("scr_c", SCRT[0]))
            srcap = scr[WSRC_SCR[stream]]
        else:
            srcap = ap[src_name]
        p.emit(WQ[stream],
               lambda e, o=o, n=ncols, s=srcap, b=blk: e.dma_start(
                   out=arena[:, o:o + n], in_=s[b]),
               waits=waits, inc=f"ld_{stream}{j}", amt=16)
        return gen * 16

    def a_view(stream, j, lo, n):
        o = OFFS[stream][j]
        return arena[:, o + lo: o + lo + n]

    SUBIDX = [0]
    ACCG_T = [0] * TT
    lnbuf = [ubf, tsil]
    EPSM = [0]
    lnbuf_tr_guard = [0, 0]
    LNUS = [0, 0]
    USCT = [(0, 0)]
    def ln_phase(g):
        acg = p.cnt["ac"]
        dvg = p.cnt["dv"]
        NPIECE = min(4, TT)
        TPP = TT // NPIECE
        for t in range(TT):
            par = t % 2
            buf = lnbuf[par]
            w0 = [("dv", dvg), ("pe", lnbuf_tr_guard[par]),
                  (f"us{par}", LNUS[par])]
            if SUBIDX[0] == 0:
                w0.append(("ld_x", (g * NPIECE + t // TPP + 1) * 16))
            a_sum = p.emit("scalar", lambda e, t=t, buf=buf: e.activation(
                buf[:, :], rbuf[:, t * H:(t + 1) * H], AF.Copy,
                accum_out=stats[:, t * 8: t * 8 + 1]),
                waits=w0 + [("sct", ACCG_T[t])])
            p.emit("scalar", lambda e, t=t: e.activation(
                stats[:, t * 8 + 1: t * 8 + 2], stats[:, t * 8: t * 8 + 1],
                AF.Copy, scale=-1.0 / H), waits=[])
            a_sq = p.emit("scalar", lambda e, t=t, buf=buf: e.activation(
                buf[:, :], rbuf[:, t * H:(t + 1) * H], AF.Square,
                bias=stats[:, t * 8 + 1: t * 8 + 2], scale=1.0,
                accum_out=stats[:, t * 8 + 2: t * 8 + 3]),
                waits=[(f"us{t % 2}", LNUS[t % 2])])
            a_sd = p.emit("scalar", lambda e, t=t: e.activation(
                stats[:, t * 8 + 4: t * 8 + 5], stats[:, t * 8 + 2: t * 8 + 3],
                AF.Sqrt, bias=stats[:, 6:7], scale=1.0 / H),
                waits=[("dv", EPSM[0])])
            p.emit("vector", lambda e, t=t: e.reciprocal(
                stats[:, t * 8 + 5: t * 8 + 6], stats[:, t * 8 + 4: t * 8 + 5]),
                waits=[("ac", a_sd)])
            v_ubf = p.emit("vector", lambda e, t=t, buf=buf: e.tensor_scalar(
                buf[:, :], rbuf[:, t * H:(t + 1) * H], stats[:, t * 8 + 1: t * 8 + 2],
                stats[:, t * 8 + 5: t * 8 + 6], ALU.add, ALU.mult),
                waits=[("pe", lnbuf_tr_guard[par]), ("ac", a_sd)])
            if SUBIDX[0] != 1:
                LNUS[par] = p.emit("sync", lambda e, t=t, buf=buf: e.dma_start(
                    out=u_scr[t * 128:(t + 1) * 128, :], in_=buf[:, :]),
                    waits=[("dv", v_ubf), ("us0", USCT[0][0]), ("us1", USCT[0][1])],
                    inc=f"us{par}", amt=16)
            p_tr = [0, 0]
            for half in range(2):
                s0, v0 = ptb_guard[half]
                for c in range(8):
                    cc = half * 8 + c
                    p_tr[half] = p.emit("tensor", lambda e, cc=cc, half=half, c=c, buf=buf: e.transpose(
                        ptb[half][:, c * 128:(c + 1) * 128],
                        buf[:, cc * 128:(cc + 1) * 128], idn_sb[:, :]),
                        waits=[("dv", v_ubf), (s0, v0), ("ld_y", CONSTS)],
                        inc="pe" if c == 7 else None)
            for half in range(2):
                gv = p.emit("vector", lambda e, half=half, t=t: e.tensor_copy(
                    uT[:, :].rearrange("z (c g) -> z c g", g=GT)[:, half * 8:half * 8 + 8,
                                                                t * 128:t * 128 + 128],
                    ptb[half][:, :].rearrange("z (c g) -> z c g", g=128)),
                    waits=[("pe", p_tr[half])])
                ptb_guard[half] = ("dv", gv)
            lnbuf_tr_guard[par] = p_tr[1]
            if par == 1:
                tsil_guard[0] = max(tsil_guard[0], v_ubf)
                tsil_guard[1] = max(tsil_guard[1], v_ubf)
        return (("pe", p.cnt["pe"]), ("dv", p.cnt["dv"]))

    def router_phase(uT_ready):
        for t in range(TT):
            bk = t % 2
            s0, v0 = bank_guard[bk]
            ins_l = None
            for c in range(KH):
                ins_l = p.emit("tensor", lambda e, c=c, t=t, bk=bk: e.matmul(
                    psb[bk][:, 0:8], uT[:, c * GT + t * 128: c * GT + t * 128 + 128],
                    rw_bf[:, c * 8:(c + 1) * 8], start=(c == 0), stop=(c == KH - 1)),
                    waits=[uT_ready[0], uT_ready[1], (s0, v0), ("ld_c", 16)],
                    inc="pe" if c == KH - 1 else None)
            p_lg = ins_l
            gv = p.emit("vector", lambda e, t=t, bk=bk: e.tensor_copy(
                lgsb[:, t * 8:(t + 1) * 8], psb[bk][:, 0:8]),
                waits=[("pe", p_lg)])
            bank_guard[bk] = ("dv", gv)
            p.emit("vector", lambda e, t=t: e.reduce_max(
                st2[:, t * 8: t * 8 + 1], lgsb[:, t * 8:(t + 1) * 8], axis=AX.X))
            v_ngm = p.emit("vector", lambda e, t=t: e.tensor_scalar(
                st2[:, t * 8 + 1: t * 8 + 2], st2[:, t * 8: t * 8 + 1],
                -1.0, None, ALU.mult))
            a_exp = p.emit("scalar", lambda e, t=t: e.activation(
                esb[:, t * 8:(t + 1) * 8], lgsb[:, t * 8:(t + 1) * 8], AF.Exp,
                bias=st2[:, t * 8 + 1: t * 8 + 2], scale=1.0),
                waits=[("dv", v_ngm)])
            p.emit("vector", lambda e, t=t: e.tensor_scalar(
                tmp8[:, t * 8:(t + 1) * 8], esb[:, t * 8:(t + 1) * 8],
                1.0, -2.0, ALU.is_ge, ALU.mult), waits=[("ac", a_exp)])
            p.emit("vector", lambda e, t=t: e.tensor_tensor(
                tmp8[:, t * 8:(t + 1) * 8], tmp8[:, t * 8:(t + 1) * 8],
                esb[:, t * 8:(t + 1) * 8], ALU.add))
            v_m2 = p.emit("vector", lambda e, t=t: e.reduce_max(
                st2[:, t * 8 + 2: t * 8 + 3], tmp8[:, t * 8:(t + 1) * 8], axis=AX.X))
            p.emit("vector", lambda e, t=t: e.tensor_scalar(
                st2[:, t * 8 + 3: t * 8 + 4], st2[:, t * 8 + 2: t * 8 + 3],
                1.0, None, ALU.add))
            v_inv = p.emit("vector", lambda e, t=t: e.reciprocal(
                st2[:, t * 8 + 4: t * 8 + 5], st2[:, t * 8 + 3: t * 8 + 4]))
            p.emit("vector", lambda e, t=t: e.tensor_scalar(
                tmp8b[:, t * 8:(t + 1) * 8], esb[:, t * 8:(t + 1) * 8],
                st2[:, t * 8 + 2: t * 8 + 3], None, ALU.is_ge),
                waits=[("dv", v_m2)])
            p.emit("vector", lambda e, t=t: e.tensor_tensor(
                tmp8b[:, t * 8:(t + 1) * 8], tmp8b[:, t * 8:(t + 1) * 8],
                esb[:, t * 8:(t + 1) * 8], ALU.mult))
            p.emit("vector", lambda e, t=t: e.tensor_scalar(
                c_buf[:, t * 8:(t + 1) * 8], tmp8b[:, t * 8:(t + 1) * 8],
                st2[:, t * 8 + 4: t * 8 + 5], 0.5, ALU.mult, ALU.mult),
                waits=[("dv", v_inv)])

    BIG = 60000.0

    def compaction(g, spr_guard):
        if not hasattr(compaction, "slot_guard"):
            compaction.slot_guard = [0, 0]
        for t in range(TT):
            v_m = p.emit("vector", lambda e, t=t: e.tensor_scalar(
                tmp8b[:, t * 8:(t + 1) * 8], c_buf[:, t * 8:(t + 1) * 8],
                0.0, None, ALU.is_gt),
                waits=[("bsx", spr_guard), ("pe", SPRG[0])])
            bk = t % 2
            s0, v0 = bank_guard[bk]
            p_t = p.emit("tensor", lambda e, t=t, bk=bk: e.transpose(
                psb[bk][0:8, 0:128], tmp8b[:, t * 8:(t + 1) * 8], idnf_sb[:, :]),
                waits=[("dv", v_m), (s0, v0), ("ld_y", CONSTS)], inc="pe")
            gv = p.emit("vector", lambda e, t=t, bk=bk: e.tensor_copy(
                spr[0:8, t * 128:(t + 1) * 128], psb[bk][0:8, 0:128]),
                waits=[("pe", p_t)])
            bank_guard[bk] = ("dv", gv)
        p.emit("vector", lambda e: e.tensor_tensor_scan(
            spr[32:40, 0:GT], spr[0:8, 0:GT], spr[0:8, 0:GT], 0.0,
            ALU.add, ALU.bypass), waits=[("pe", SPRG[0])])
        p.emit("vector", lambda e: e.tensor_scalar(
            spr[32:40, 0:GT], spr[32:40, 0:GT], -1.0, None, ALU.add))
        v_dst = 0
        CHW = min(512, GT)
        for ch in range(GT // CHW):
            cs = slice(ch * CHW, (ch + 1) * CHW)
            bka = ch % 2
            bkb = 2 + ch % 2
            sa0, sa1 = bank_guard[bka]
            sb0, sb1 = bank_guard[bkb]
            p.emit("vector", lambda e, cs=cs, bka=bka: e.tensor_scalar(
                psb[bka][0:8, 0:CHW], spr[32:40, cs], float(CAP) - 0.5,
                None, ALU.is_le), waits=[(sa0, sa1)])
            p.emit("vector", lambda e, cs=cs, bka=bka, bkb=bkb: e.tensor_tensor(
                psb[bkb][0:8, 0:CHW], spr[0:8, cs], psb[bka][0:8, 0:CHW],
                ALU.mult), waits=[(sb0, sb1)])
            p.emit("vector", lambda e, cs=cs: e.tensor_scalar(
                spr[32:40, cs], spr[32:40, cs], ecst_sb[32:40, 0:1], None, ALU.add))
            p.emit("vector", lambda e, cs=cs, bkb=bkb: e.tensor_tensor(
                spr[32:40, cs], spr[32:40, cs], psb[bkb][0:8, 0:CHW], ALU.mult))
            p.emit("vector", lambda e, cs=cs, bkb=bkb: e.tensor_scalar(
                psb[bkb][0:8, 0:CHW], psb[bkb][0:8, 0:CHW], -BIG, BIG,
                ALU.mult, ALU.add))
            v_dst = p.emit("vector", lambda e, cs=cs, bkb=bkb: e.tensor_tensor(
                spr[32:40, cs], spr[32:40, cs], psb[bkb][0:8, 0:CHW], ALU.add))
            bank_guard[bka] = ("dv", v_dst)
            bank_guard[bkb] = ("dv", v_dst)
        usc_marks = []
        usc_slot_guard = compaction.slot_guard
        uwr_totals = (p.cnt["us0"], p.cnt["us1"])
        for t in range(TT):
            vi_marks = [0, 0]
            bk = t % 2
            s0, v0 = bank_guard[bk]
            p_t = p.emit("tensor", lambda e, t=t, bk=bk: e.transpose(
                psb[bk][:, 0:8], spr[32:40, t * 128:(t + 1) * 128], idn8_sb[32:40, 0:8]),
                waits=[("dv", v_dst), (s0, v0)], inc="pe")
            gv = p.emit("vector", lambda e, t=t, bk=bk: e.tensor_copy(
                tmp8b[:, t * 8:(t + 1) * 8], psb[bk][:, 0:8]),
                waits=[("pe", p_t)])
            bank_guard[bk] = ("dv", gv)
            p.emit("vector", lambda e, t=t: e.tensor_reduce(
                dcf[:, t * 8: t * 8 + 1], tmp8b[:, t * 8:(t + 1) * 8],
                axis=AX.X, op=ALU.min))
            p.emit("vector", lambda e, t=t: e.tensor_scalar(
                tmp8[:, t * 8:(t + 1) * 8], tmp8b[:, t * 8:(t + 1) * 8],
                dcf[:, t * 8: t * 8 + 1], BIG, ALU.is_le, ALU.mult))
            p.emit("vector", lambda e, t=t: e.tensor_tensor(
                tmp8[:, t * 8:(t + 1) * 8], tmp8[:, t * 8:(t + 1) * 8],
                tmp8b[:, t * 8:(t + 1) * 8], ALU.add))
            p.emit("vector", lambda e, t=t: e.tensor_reduce(
                dcf[:, t * 8 + 1: t * 8 + 2], tmp8[:, t * 8:(t + 1) * 8],
                axis=AX.X, op=ALU.min))
            for r_ in range(2):
                p.emit("vector", lambda e, t=t, r_=r_: e.tensor_scalar(
                    tmp8[:, t * 8:(t + 1) * 8], tmp8b[:, t * 8:(t + 1) * 8],
                    dcf[:, t * 8 + r_: t * 8 + r_ + 1], None, ALU.is_equal))
                p.emit("vector", lambda e, t=t: e.tensor_tensor(
                    tmp8[:, t * 8:(t + 1) * 8], tmp8[:, t * 8:(t + 1) * 8],
                    c_buf[:, t * 8:(t + 1) * 8], ALU.mult))
                p.emit("vector", lambda e, t=t, r_=r_: e.reduce_sum(
                    dcf[:, t * 8 + 6 + r_: t * 8 + 7 + r_],
                    tmp8[:, t * 8:(t + 1) * 8], axis=AX.X))
                p.emit("vector", lambda e, t=t, r_=r_: e.tensor_scalar(
                    tmp8[:, t * 8: t * 8 + 1], dcf[:, t * 8 + r_: t * 8 + r_ + 1],
                    float(8 * CAP) - 0.5, None, ALU.is_le))
                p.emit("vector", lambda e, t=t, r_=r_: e.tensor_tensor(
                    dcf[:, t * 8 + 6 + r_: t * 8 + 7 + r_],
                    dcf[:, t * 8 + 6 + r_: t * 8 + 7 + r_],
                    tmp8[:, t * 8: t * 8 + 1], ALU.mult))
                p.emit("vector", lambda e, t=t, r_=r_: e.tensor_tensor(
                    dcf[:, t * 8 + r_: t * 8 + r_ + 1],
                    dcf[:, t * 8 + r_: t * 8 + r_ + 1],
                    dmpc[:, 0:1], ALU.min))
                vi_marks[r_] = p.emit("vector", lambda e, t=t, r_=r_: e.tensor_copy(
                    dsti[:, t * 2 + r_: t * 2 + r_ + 1],
                    dcf[:, t * 8 + r_: t * 8 + r_ + 1]))
            slot = t % 2
            gv_l = p.emit("sync", lambda e, t=t, slot=slot: e.dma_start(
                out=arena[:, off_ug + slot * 2048: off_ug + slot * 2048 + 2048],
                in_=u_scr[t * 128:(t + 1) * 128, :]),
                waits=[(f"sc{slot}", usc_slot_guard[slot]),
                       ("us0", uwr_totals[0]), ("us1", uwr_totals[1]),
                       ("zin", ZINIT),
                       ("pe", ug_lastuse[slot])], inc=f"us{slot}", amt=16)
            for r_ in range(2):
                usc_marks.append(p.emit("gpsimd", lambda e, t=t, r_=r_, slot=slot: e.indirect_dma_start(
                    out=ug_dram,
                    out_offset=bass.IndirectOffsetOnAxis(dsti[:, t * 2 + r_: t * 2 + r_ + 1], 0),
                    in_=arena[:, off_ug + slot * 2048: off_ug + slot * 2048 + 2048],
                    in_offset=None),
                    waits=[("dv", vi_marks[r_]), (f"us{slot}", gv_l),
                           ("sc0", p.cnt["sc0"]), ("sc1", p.cnt["sc1"])],
                    inc=f"sc{slot}", amt=16))
            usc_slot_guard[slot] = p.cnt[f"sc{slot}"]
        USCT[0] = (p.cnt["us0"], p.cnt["us1"])
        return (("sc0", p.cnt["sc0"]), ("sc1", p.cnt["sc1"]))

    ug_lastuse = [0, 0]
    dstg_guard = [("dv", 0), ("dv", 0)]

    def sparse_expert(ex, usc_total, uT_ready, pair_ctr, gu_pre=None):
        gj_src = "gw"
        utg_ready = 0
        for ti, (base, rows) in enumerate(CTILES):
            slot = ti % 2
            gv = p.emit("scalar", lambda e, ex=ex, base=base, rows=rows, slot=slot: e.dma_start(
                out=arena[0:rows, off_ug + slot * 2048: off_ug + slot * 2048 + 2048],
                in_=ug_dram[ex * CAP + base: ex * CAP + base + rows, :]),
                waits=[usc_total[0], usc_total[1], (f"us{slot}", p.cnt[f"us{slot}"]),
                       ("pe", ug_lastuse[slot])], inc=f"us{slot}", amt=16)
            p_t = 0
            for c in range(KH):
                half = c // 8
                s0, v0 = ptb_guard[half]
                p_t = p.emit("tensor", lambda e, c=c, rows=rows, slot=slot, half=half: e.transpose(
                    ptb[half][:, (c % 8) * 128:(c % 8) * 128 + rows],
                    arena[0:rows, off_ug + slot * 2048 + c * 128: off_ug + slot * 2048 + (c + 1) * 128],
                    idn_sb[0:rows, 0:rows]),
                    waits=[(f"us{slot}", gv), (s0, v0), ("ld_y", CONSTS)],
                    inc="pe" if c % 8 == 7 else None)
                if c % 8 == 7:
                    hv = p.emit("vector", lambda e, c=c, base=base, rows=rows, half=half: e.tensor_copy(
                        arena[:, off_utg: off_utg + KH * CAP].rearrange(
                            "z (cc q) -> z cc q", q=CAP)[:, c - 7:c + 1, base:base + rows],
                        ptb[half][:, :].rearrange("z (cc q) -> z cc q", q=128)[:, :, 0:rows]),
                        waits=[("pe", p_t)])
                    ptb_guard[half] = ("dv", hv)
                    utg_ready = hv
            ug_lastuse[slot] = p_t
        wd_pre = {}
        for m in range(KI):
            gj = (ex * KI + m) % 2
            wg = wdma("g", gj, "gw", ex * KI + m)
            wu = wdma("u", gj, "uw", ex * KI + m)
            emit_scr_piece()
            if m in (8, 9):
                n0 = m - 8
                wd_pre[n0] = wdma("d", (ex * 4 + n0) % 2, "dw", ex * 4 + n0)
            NH = CAP // 2
            pr = pair_ctr[0] % 3
            pair_ctr[0] += 1
            pu = pair_ctr[0] % 3
            pair_ctr[0] += 1
            bG = (2 * pr, 2 * pr + 1)
            bU = (2 * pu, 2 * pu + 1)
            gw0, gw1 = bank_guard[bG[0]], bank_guard[bG[1]]
            p_G = None
            for k in range(KH):
                for h in range(2):
                    last = (k == KH - 1 and h == 1)
                    p_G = p.emit("tensor", lambda e, k=k, h=h, gj=gj, bG=bG, NH=NH: e.matmul(
                        psb[bG[h]][:, 0:NH], a_view("g", gj, k * 128, 128),
                        arena[:, off_utg + k * CAP + h * NH: off_utg + k * CAP + (h + 1) * NH],
                        start=(k == 0), stop=(k == KH - 1)),
                        waits=[(f"ld_g{gj}", wg), ("dv", utg_ready), gw0, gw1],
                        inc="pe" if last else None)
            gw0, gw1 = bank_guard[bU[0]], bank_guard[bU[1]]
            p_U = None
            for k in range(KH):
                for h in range(2):
                    last = (k == KH - 1 and h == 1)
                    p_U = p.emit("tensor", lambda e, k=k, h=h, gj=gj, bU=bU, NH=NH: e.matmul(
                        psb[bU[h]][:, 0:NH], a_view("u", gj, k * 128, 128),
                        arena[:, off_utg + k * CAP + h * NH: off_utg + k * CAP + (h + 1) * NH],
                        start=(k == 0), stop=(k == KH - 1)),
                        waits=[(f"ld_u{gj}", wu), gw0, gw1],
                        inc="pe" if last else None)
            buf_lastuse[("g", gj)] = p_G
            buf_lastuse[("u", gj)] = p_U
            for h in range(2):
                slot = (m % 2) * 1024 + h * 512
                tsw = tsil_guard[m % 2]
                a_sil = p.emit("scalar", lambda e, h=h, slot=slot, bG=bG, NH=NH: e.activation(
                    tsil[:, slot: slot + NH], psb[bG[h]][:, 0:NH], AF.Silu),
                    waits=[("pe", p_G), ("dv", tsw), ("ld_y", CONSTS),
                           ("us1", LNUS[1])])
                bank_guard[bG[h]] = ("ac", a_sil)
                hv2 = m * CAP + h * NH
                dvv = p.emit("vector", lambda e, h=h, slot=slot, bU=bU, hv2=hv2, NH=NH: e.tensor_tensor(
                    arena[:, off_h + hv2: off_h + hv2 + NH], psb[bU[h]][:, 0:NH],
                    tsil[:, slot: slot + NH], ALU.mult),
                    waits=[("pe", p_U), ("ac", a_sil)])
                bank_guard[bU[h]] = ("dv", dvv)
                tsil_guard[m % 2] = dvv
        hbuf_ready = p.cnt["dv"]
        for n_ in range(4):
            dj = (ex * 4 + n_) % 2
            wd = wd_pre[n_] if n_ in wd_pre else wdma("d", dj, "dw", ex * 4 + n_)
            p_dn_last = 0
            for ti, (base, rows) in enumerate(CTILES):
                pr = pair_ctr[0] % 3
                pair_ctr[0] += 1
                bks = (2 * pr, 2 * pr + 1)
                gw0, gw1 = bank_guard[bks[0]], bank_guard[bks[1]]
                p_dn = None
                for k in range(KI):
                    for h in range(2):
                        last = (k == KI - 1 and h == 1)
                        p_dn = p.emit("tensor", lambda e, k=k, h=h, base=base, rows=rows, dj=dj, bks=bks: e.matmul(
                            psb[bks[h]][0:rows, 0:256],
                            arena[:, off_h + k * CAP + base: off_h + k * CAP + base + rows],
                            a_view("d", dj, k * 512 + h * 256, 256),
                            start=(k == 0), stop=(k == KI - 1)),
                            waits=[(f"ld_d{dj}", wd), ("dv", hbuf_ready), gw0, gw1],
                            inc="pe" if last else None)
                p_dn_last = p_dn
                slot = (ex * 4 + n_ + ti) % 2
                sg0, sv0 = dstg_guard[slot]
                cva = p.emit("vector", lambda e, rows=rows, slot=slot, bks=bks: e.tensor_copy(
                    dstg[0:rows, slot * 512: slot * 512 + 256], psb[bks[0]][0:rows, 0:256]),
                    waits=[("pe", p_dn), (sg0, sv0)])
                cvb = p.emit("vector", lambda e, rows=rows, slot=slot, bks=bks: e.tensor_copy(
                    dstg[0:rows, slot * 512 + 256: slot * 512 + 512], psb[bks[1]][0:rows, 0:256]),
                    waits=[("pe", p_dn)])
                bank_guard[bks[0]] = ("dv", cva)
                bank_guard[bks[1]] = ("dv", cvb)
                gv = p.emit("sync", lambda e, ex=ex, base=base, rows=rows, n_=n_, slot=slot: e.dma_start(
                    out=ogd[ex * CAP + base: ex * CAP + base + rows, n_ * 512:(n_ + 1) * 512],
                    in_=dstg[0:rows, slot * 512: slot * 512 + 512]),
                    waits=[("dv", cvb)], inc=f"dg{slot}", amt=16)
                dstg_guard[slot] = (f"dg{slot}", gv)
            buf_lastuse[("d", dj)] = p_dn_last

    def shared_expert(hf, uT_ready, pair_ctr, gu_pre=None):
        fin_tiles = [0] * TT
        wd_pre = {}
        for m in range(KI):
            gj = ((8 + hf) * KI + m) % 2
            if gu_pre and m in gu_pre:
                wg, wu = gu_pre[m]
            else:
                wg = wdma("g", gj, "sg", hf * KI + m)
                wu = wdma("u", gj, "su", hf * KI + m)
            emit_scr_piece()
            if m in (8, 9):
                n0 = m - 8
                wd_pre[n0] = wdma("d", n0 % 2, "sd", hf * 4 + n0)
            pr = pair_ctr[0] % 3
            pair_ctr[0] += 1
            pu = pair_ctr[0] % 3
            pair_ctr[0] += 1
            bG = (2 * pr, 2 * pr + 1)
            bU = (2 * pu, 2 * pu + 1)
            gw0, gw1 = bank_guard[bG[0]], bank_guard[bG[1]]
            p_G = None
            for k in range(KH):
                for h in range(2):
                    last = (k == KH - 1 and h == 1)
                    p_G = p.emit("tensor", lambda e, k=k, h=h, gj=gj, bG=bG: e.matmul(
                        psb[bG[h]][:, 0:HGT], a_view("g", gj, k * 128, 128),
                        uT[:, k * GT + h * HGT: k * GT + (h + 1) * HGT],
                        start=(k == 0), stop=(k == KH - 1)),
                        waits=[(f"ld_g{gj}", wg), uT_ready[0], uT_ready[1], gw0, gw1],
                        inc="pe" if last else None)
            gw0, gw1 = bank_guard[bU[0]], bank_guard[bU[1]]
            p_U = None
            for k in range(KH):
                for h in range(2):
                    last = (k == KH - 1 and h == 1)
                    p_U = p.emit("tensor", lambda e, k=k, h=h, gj=gj, bU=bU: e.matmul(
                        psb[bU[h]][:, 0:HGT], a_view("u", gj, k * 128, 128),
                        uT[:, k * GT + h * HGT: k * GT + (h + 1) * HGT],
                        start=(k == 0), stop=(k == KH - 1)),
                        waits=[(f"ld_u{gj}", wu), gw0, gw1],
                        inc="pe" if last else None)
            buf_lastuse[("g", gj)] = p_G
            buf_lastuse[("u", gj)] = p_U
            for h in range(2):
                slot = (m % 2) * 1024 + h * 512
                tsw = tsil_guard[m % 2]
                col = hf * KI + m
                a_sil = p.emit("scalar", lambda e, h=h, slot=slot, bG=bG, col=col: e.activation(
                    tsil[:, slot: slot + HGT], psb[bG[h]][:, 0:HGT], AF.Silu,
                    bias=bsg_sb[:, col: col + 1], scale=1.0),
                    waits=[("pe", p_G), ("dv", tsw), ("ld_y", CONSTS),
                           ("us1", LNUS[1])])
                bank_guard[bG[h]] = ("ac", a_sil)
                hv = m * GT + h * HGT
                dvv = p.emit("vector", lambda e, h=h, slot=slot, bU=bU, hv=hv, col=col: e.scalar_tensor_tensor(
                    arena[:, off_h + hv: off_h + hv + HGT], psb[bU[h]][:, 0:HGT],
                    bsu_sb[:, col: col + 1], tsil[:, slot: slot + HGT],
                    ALU.add, ALU.mult),
                    waits=[("pe", p_U), ("ac", a_sil), ("ld_y", CONSTS)])
                bank_guard[bU[h]] = ("dv", dvv)
                tsil_guard[m % 2] = dvv
        hbuf_ready = p.cnt["dv"]
        for n_ in range(4):
            dj = n_ % 2
            wd = wd_pre[n_] if n_ in wd_pre else wdma("d", dj, "sd", hf * 4 + n_)
            p_dn_last = 0
            for t in range(TT):
                pr = pair_ctr[0] % 3
                pair_ctr[0] += 1
                bks = (2 * pr, 2 * pr + 1)
                gw0, gw1 = bank_guard[bks[0]], bank_guard[bks[1]]
                p_dn = None
                for k in range(KI):
                    for h in range(2):
                        last = (k == KI - 1 and h == 1)
                        p_dn = p.emit("tensor", lambda e, k=k, h=h, t=t, dj=dj, bks=bks: e.matmul(
                            psb[bks[h]][:, 0:256],
                            arena[:, off_h + k * GT + t * 128: off_h + k * GT + t * 128 + 128],
                            a_view("d", dj, k * 512 + h * 256, 256),
                            start=(k == 0), stop=(k == KI - 1)),
                            waits=[(f"ld_d{dj}", wd), ("dv", hbuf_ready), gw0, gw1],
                            inc="pe" if last else None)
                p_dn_last = p_dn
                for h in range(2):
                    rcol = t * H + n_ * 512 + h * 256
                    dvv = p.emit("vector", lambda e, h=h, rcol=rcol, bks=bks: e.scalar_tensor_tensor(
                        rbuf[:, rcol: rcol + 256], psb[bks[h]][:, 0:256],
                        0.5, rbuf[:, rcol: rcol + 256], ALU.mult, ALU.add),
                        waits=[("pe", p_dn)])
                    bank_guard[bks[h]] = ("dv", dvv)
                fin_tiles[t] = p.cnt["dv"]
            buf_lastuse[("d", dj)] = p_dn_last
        return fin_tiles

    SPRG = [0]
    ACCG = [0]
    def ffn_phase(g, uT_ready, pair_ctr=[0]):
        gu_pre = {}
        for m in range(2):
            gu_pre[m] = (wdma("g", m % 2, "sg", m), wdma("u", m % 2, "su", m))
        usc_total = compaction(g, BSDH_W)
        SPRG[0] = p.cnt["pe"]
        fin_sh = [0] * TT
        for hf in range(2):
            fin_sh = shared_expert(hf, uT_ready, pair_ctr,
                                   gu_pre=gu_pre if hf == 0 else None)
        for ex in range(8):
            sparse_expert(ex, usc_total, uT_ready, pair_ctr)
        ogd_total = (p.cnt["dg0"], p.cnt["dg1"])
        fin_tiles = list(fin_sh)
        pe_experts = p.cnt["pe"]
        NSL = min(4, (27 * CAP) // 1024)
        gslot_guard = [("dv", 0)] * NSL
        acc_marks = [0] * TT
        gi = 0
        for t in range(TT):
            for r_ in range(2):
                for n2 in range(2):
                    slot = gi % NSL
                    gi += 1
                    sg0, sv0 = gslot_guard[slot]
                    gv = p.emit("gpsimd", lambda e, t=t, r_=r_, n2=n2, slot=slot: e.indirect_dma_start(
                        out=arena[:, off_h + slot * 1024: off_h + slot * 1024 + 1024],
                        out_offset=None,
                        in_=ogd,
                        in_offset=bass.IndirectOffsetOnAxis(dsti[:, t * 2 + r_: t * 2 + r_ + 1], 0),
                        element_offset=n2 * 1024),
                        waits=[("dg0", ogd_total[0]), ("dg1", ogd_total[1]),
                               ("pe", pe_experts), (sg0, sv0)],
                        inc=f"gg{slot}", amt=16)
                    dvv = p.emit("vector", lambda e, t=t, r_=r_, n2=n2, slot=slot: e.scalar_tensor_tensor(
                        rbuf[:, t * H + n2 * 1024: t * H + (n2 + 1) * 1024],
                        arena[:, off_h + slot * 1024: off_h + slot * 1024 + 1024],
                        dcf[:, t * 8 + 6 + r_: t * 8 + 7 + r_],
                        rbuf[:, t * H + n2 * 1024: t * H + (n2 + 1) * 1024],
                        ALU.mult, ALU.add),
                        waits=[(f"gg{slot}", gv)])
                    gslot_guard[slot] = ("dv", dvv)
                    if r_ == 1 and n2 == 1:
                        fin_tiles[t] = dvv
            acc_marks[t] = p.emit("gpsimd", lambda e, t=t: e.dma_start(
                out=rbuf[:, t * H:(t + 1) * H], in_=bsdh, accum_op=ALU.add),
                waits=[("dv", fin_tiles[t]), ("bsx", BSDH_W),
                       ("sct", p.cnt["sct"])], inc="sct", amt=16)
        buf_lastuse[("wi", 0)] = max(buf_lastuse[("wi", 0)], p.cnt["pe"])
        buf_lastuse[("wi", 1)] = max(buf_lastuse[("wi", 1)], p.cnt["pe"])
        ACCG[0] = acc_marks[-1]
        ACCG_T[:] = acc_marks
        return fin_tiles, acc_marks

    def attn_phase(g, uT_ready):
        for sg_i in range(2):
            for m in range(32):
                wj = m % 3
                wv = wdma("wi", wj, "wi", m)
                pr = m % 3
                bks = (2 * pr, 2 * pr + 1)
                gg = [bank_guard[bks[0]], bank_guard[bks[1]]]
                p_xt = None
                for k in range(KH):
                    for q in range(2):
                        last = (k == KH - 1 and q == 1)
                        p_xt = p.emit("tensor", lambda e, k=k, q=q, wj=wj, bks=bks, sg_i=sg_i: e.matmul(
                            psb[bks[q]][:, 0:NQ],
                            a_view("wi", wj, k * 128, 128),
                            uT[:, k * GT + sg_i * SGT + q * NQ: k * GT + sg_i * SGT + (q + 1) * NQ],
                            start=(k == 0), stop=(k == KH - 1)),
                            waits=[(f"ld_wi{wj}", wv), uT_ready[0], uT_ready[1], gg[0], gg[1]],
                            inc="pe" if last else None)
                buf_lastuse[("wi", wj)] = p_xt
                for q in range(2):
                    gv = p.emit("vector", lambda e, m=m, q=q, bks=bks: e.tensor_copy(
                        arena[:, off_xT + m * SGT + q * NQ: off_xT + m * SGT + (q + 1) * NQ],
                        psb[bks[q]][:, 0:NQ]),
                        waits=[("pe", p_xt)])
                    bank_guard[bks[q]] = ("dv", gv)
            xT_ready = p.cnt["dv"]
            pB_last = 0
            for m in range(32):
                cj = m % 2
                zj = m % 3
                wc = wdma("cvb", cj, "cv", m)
                wz = wdma("wi", zj, "wi", 32 + m)
                zpr = (2 * m) % 3
                cpr = (2 * m + 1) % 3
                bz = (2 * zpr, 2 * zpr + 1)
                bc = (2 * cpr, 2 * cpr + 1)
                gg = [bank_guard[bz[0]], bank_guard[bz[1]]]
                p_z = None
                for k in range(KH):
                    for q in range(2):
                        last = (k == KH - 1 and q == 1)
                        p_z = p.emit("tensor", lambda e, k=k, q=q, zj=zj, bz=bz, sg_i=sg_i: e.matmul(
                            psb[bz[q]][:, 0:NQ],
                            a_view("wi", zj, k * 128, 128),
                            uT[:, k * GT + sg_i * SGT + q * NQ: k * GT + sg_i * SGT + (q + 1) * NQ],
                            start=(k == 0), stop=(k == KH - 1)),
                            waits=[(f"ld_wi{zj}", wz), gg[0], gg[1]],
                            inc="pe" if last else None)
                buf_lastuse[("wi", zj)] = p_z
                gg = [bank_guard[bc[0]], bank_guard[bc[1]]]
                p_xc = None
                for k in range(32):
                    for q in range(2):
                        last = (k == 31 and q == 1)
                        p_xc = p.emit("tensor", lambda e, k=k, q=q, cj=cj, bc=bc: e.matmul(
                            psb[bc[q]][:, 0:NQ],
                            a_view("cvb", cj, k * 128, 128),
                            arena[:, off_xT + k * SGT + q * NQ: off_xT + k * SGT + (q + 1) * NQ],
                            start=(k == 0), stop=(k == 31)),
                            waits=[(f"ld_cvb{cj}", wc), ("dv", xT_ready), gg[0], gg[1]],
                            inc="pe" if last else None)
                buf_lastuse[("cvb", cj)] = p_xc
                pB_last = p_xc
                slot = (m % 2) * SGT
                for q in range(2):
                    tsw = tsil_guard[m % 2]
                    a_z = p.emit("scalar", lambda e, slot=slot, q=q, bz=bz: e.activation(
                        tsil[:, slot + q * NQ: slot + (q + 1) * NQ],
                        psb[bz[q]][:, 0:NQ], AF.Silu),
                        waits=[("pe", p_z), ("dv", tsw), uT_ready[0], uT_ready[1],
                               ("us1", LNUS[1])])
                    bank_guard[bz[q]] = ("ac", a_z)
                    gv = p.emit("vector", lambda e, m=m, slot=slot, q=q, bc=bc: e.scalar_tensor_tensor(
                        arena[:, off_sb_ + m * SGT + q * NQ: off_sb_ + m * SGT + (q + 1) * NQ],
                        psb[bc[q]][:, 0:NQ], dvec_sb[:, m: m + 1],
                        tsil[:, slot + q * NQ: slot + (q + 1) * NQ], ALU.mult, ALU.mult),
                        waits=[("pe", p_xc), ("ac", a_z), ("ld_y", CONSTS)])
                    bank_guard[bc[q]] = ("dv", gv)
                    tsil_guard[m % 2] = gv
            sb_ready = p.cnt["dv"]
            wo_all = {}
            def emit_wo(ip):
                if ip <= 7:
                    wj = ip % 3
                    wo_all[ip] = (wj, wdma("wob", wj, "wo", ip,
                                           extra_pe_guard=pB_last))
            for ip in range(3):
                emit_wo(ip)
            for n in range(4):
                wvs = [wo_all[n * 2], wo_all[n * 2 + 1]]
                for t in range(STT):
                    pc = (n * STT + t) % 3
                    bks = (2 * pc, 2 * pc + 1)
                    gg = [bank_guard[bks[0]], bank_guard[bks[1]]]
                    p_o = None
                    for k in range(32):
                        kh = k // 16
                        wj, wv = wvs[kh]
                        for q in range(2):
                            last = (k == 31 and q == 1)
                            p_o = p.emit("tensor", lambda e, k=k, q=q, t=t, wj=wj, bks=bks: e.matmul(
                                psb[bks[q]][:, 0:256],
                                arena[:, off_sb_ + k * SGT + t * 128: off_sb_ + k * SGT + t * 128 + 128],
                                a_view("wob", wj, (k % 16) * 512 + q * 256, 256),
                                start=(k == 0), stop=(k == 31)),
                                waits=[(f"ld_wob{wj}", wv), ("dv", sb_ready), gg[0], gg[1]],
                                inc="pe" if last else None)
                    buf_lastuse[("wob", wvs[0][0])] = p_o
                    buf_lastuse[("wob", wvs[1][0])] = p_o
                    rcol = (sg_i * STT + t) * H + n * 512
                    for q in range(2):
                        gv = p.emit("vector", lambda e, rcol=rcol, q=q, bks=bks: e.tensor_tensor(
                            rbuf[:, rcol + q * 256: rcol + (q + 1) * 256],
                            psb[bks[q]][:, 0:256],
                            rbuf[:, rcol + q * 256: rcol + (q + 1) * 256], ALU.add),
                            waits=[("pe", p_o), ("sct", ACCG[0])])
                        bank_guard[bks[q]] = ("dv", gv)
                emit_wo(n * 2 + 3)
                emit_wo(n * 2 + 4)
            af_sg = p.cnt["pe"]
            for key in (("wi", 0), ("wi", 1), ("wi", 2), ("cvb", 0), ("cvb", 1)):
                buf_lastuse[key] = max(buf_lastuse[key], af_sg)
        af = p.cnt["pe"]
        for key in (("g", 0), ("g", 1), ("u", 0), ("u", 1), ("d", 0), ("d", 1)):
            buf_lastuse[key] = max(buf_lastuse[key], af)
        ug_lastuse[0] = max(ug_lastuse[0], af)
        ug_lastuse[1] = max(ug_lastuse[1], af)

    EPSM[0] = p.emit("vector", lambda e: e.memset(stats[:, 6:7], 1e-5), waits=[])
    st_cnt = 0
    fin_tiles = [0] * TT
    for g in range(NG):
        NPIECE = min(4, TT)
        for piece in range(NPIECE):
            waits = []
            if g > 0 and piece == 0:
                waits.append(("st", st_cnt))
            if piece > 0:
                waits.append(("ld_x", (g * NPIECE + piece) * 16))
            p.emit("sync", lambda e, g=g, piece=piece, NPIECE=NPIECE: e.dma_start(
                out=rbuf[:, piece * (TT // NPIECE) * H:(piece + 1) * (TT // NPIECE) * H]
                    .rearrange("q (t h) -> q t h", h=H),
                in_=ap["x"][g * GT + piece * (GT // NPIECE): g * GT + (piece + 1) * (GT // NPIECE), :]
                    .rearrange("(t q) h -> q t h", q=128)),
                waits=waits, inc="ld_x", amt=16)
        for sub in range(3):
            SUBIDX[0] = sub
            uT_ready = ln_phase(g)
            if sub in (0, 2):
                router_phase(uT_ready)
                fin_tiles, acc_marks = ffn_phase(g, uT_ready)
            else:
                attn_phase(g, uT_ready)
        for piece in range(2):
            fin = max(fin_tiles[piece * (TT // 2): (piece + 1) * (TT // 2)])
            am = max(acc_marks[piece * (TT // 2): (piece + 1) * (TT // 2)])
            st_cnt = p.emit("sync", lambda e, g=g, piece=piece: e.dma_start(
                out=out_ap[g * GT + piece * (GT // 2): g * GT + (piece + 1) * (GT // 2), :]
                    .rearrange("(t q) h -> q t h", q=128),
                in_=rbuf[:, piece * (TT // 2) * H:(piece + 1) * (TT // 2) * H]
                    .rearrange("q (t h) -> q t h", h=H)),
                waits=[("dv", fin), ("sct", am)], inc="st", amt=16)
    p.emit("sync", None, waits=[("st", st_cnt)])

    with es:
        block = es.enter_context(nc.Block())
        for eng in Plan.ENGINES:
            ops = p.ops[eng]
            if not ops:
                continue
            def body(e, ops=ops):
                hwm = {}
                for waits, fn, inc, amt in ops:
                    for s, v in waits:
                        if hwm.get(s, 0) < v:
                            e.wait_ge(sems[s], v)
                            hwm[s] = v
                    if fn is not None:
                        ins = fn(e)
                        if inc is not None:
                            ins.then_inc(sems[inc], amt)
            getattr(block, eng)(body)
    return nc


def pack_inputs(i, GT=1024):
    import os as _os
    TT = GT // 128
    _capo = _os.environ.get("KCAP")
    CAP = int(_capo) if _capo else (320 if GT == 1024 else GT // 2)
    c = np.ascontiguousarray
    d = {}
    d["gw"] = c(np.asarray(i["L0_ffn_gate_w"]).reshape(8, 16, 128, 11, 128)
                .transpose(0, 3, 2, 1, 4).reshape(88, 128, 2048))
    d["uw"] = c(np.asarray(i["L0_ffn_up_w"]).reshape(8, 16, 128, 11, 128)
                .transpose(0, 3, 2, 1, 4).reshape(88, 128, 2048))
    d["dw"] = c(np.asarray(i["L0_ffn_down_w"]).reshape(8, 11, 128, 4, 512)
                .transpose(0, 3, 2, 1, 4).reshape(32, 128, 5632))
    d["sg"] = c(np.asarray(i["L0_ffn_s_gate_w"]).reshape(16, 128, 2, 11, 128)
                .transpose(2, 3, 1, 0, 4).reshape(22, 128, 2048))
    d["su"] = c(np.asarray(i["L0_ffn_s_up_w"]).reshape(16, 128, 2, 11, 128)
                .transpose(2, 3, 1, 0, 4).reshape(22, 128, 2048))
    d["sd"] = c(np.asarray(i["L0_ffn_s_down_w"]).reshape(2, 11, 128, 4, 512)
                .transpose(0, 3, 2, 1, 4).reshape(8, 128, 5632))
    d["wi"] = c(np.asarray(i["L0_attn_W_in"]).reshape(64, 128, 16, 128)
                .transpose(0, 3, 2, 1).reshape(64, 128, 2048))
    d["cv"] = c(np.asarray(i["L0_attn_conv_w"]).reshape(32, 128, 32, 128)
                .transpose(0, 3, 2, 1).reshape(32, 128, 4096))
    d["wo"] = c(np.asarray(i["L0_attn_W_out"]).reshape(4, 512, 2, 16, 128)
                .transpose(0, 2, 4, 3, 1).reshape(8, 128, 8192))
    d["rw"] = c(np.asarray(i["L0_ffn_router_w"]).reshape(8, 16, 128)
                .transpose(2, 1, 0).reshape(128, 128))
    d["dvec"] = c(np.asarray(i["L0_attn_D"]).reshape(32, 128).T)
    d["bsg"] = c(np.asarray(i["L0_ffn_s_gate_b"]).reshape(22, 128).T)
    d["bsu"] = c(np.asarray(i["L0_ffn_s_up_b"]).reshape(22, 128).T)
    d["bsdb"] = c(np.broadcast_to(np.asarray(i["L0_ffn_s_down_b"]), (128, 2048)))
    d["idn"] = np.eye(128, dtype=BFNP)
    d["idnf"] = np.eye(128, dtype=np.float32)
    d["toki"] = c((np.arange(TT)[None, :] * 128
                   + np.arange(128)[:, None]).astype(np.float32))
    d["ecst"] = c(((np.arange(128) % 8) * CAP).astype(np.float32).reshape(128, 1))
    d["idn8"] = c((np.arange(128)[:, None] % 8 == np.arange(8)[None, :])
                  .astype(np.float32))
    return d


def kernel(**inputs):
    shared = pack_inputs(inputs)
    x = np.ascontiguousarray(np.asarray(inputs["L0_attn_hidden_states"],
                                        dtype=np.float32).reshape(16384, 2048))
    nc = build_program(GT=1024, TOKC=2048)
    in_maps = []
    for cidx in range(N_CORES):
        m = dict(shared)
        m["x"] = np.ascontiguousarray(x[cidx * 2048:(cidx + 1) * 2048])
        in_maps.append(m)
    res = run_bass_kernel_spmd(nc, in_maps, core_ids=list(range(N_CORES)))
    out = np.concatenate([np.asarray(r["out"]) for r in res.results], axis=0)
    return out.reshape(1, 16384, 2048).astype(np.float32)
